# version 01
# baseline (speedup 1.0000x reference)
import math
import numpy as np
import ml_dtypes
import contextlib
import concourse.bass as bass
import concourse.mybir as mybir
from concourse.bass import ds
from concourse.bass_utils import run_bass_kernel_spmd

F32 = mybir.dt.float32
BF16 = mybir.dt.bfloat16
AF = mybir.ActivationFunctionType
ALU = mybir.AluOpType
BFNP = ml_dtypes.bfloat16

D = 2048
NH = 16
HD = 128
T = 2048
B = 4
II = 8192
MYH = 8
TOWN = 1024
KC = 16
ICH = 64
EPS = 1e-5
SCALE = 1.0 / math.sqrt(HD)

N_SUB = 4

ENGS = ("pe", "act", "dve", "pool", "sync")



def _a_layout(w):
    din, dout = w.shape
    return np.ascontiguousarray(
        w.reshape(din // 128, 128, dout // 128, 128).transpose(1, 2, 0, 3))


def _b_layout(w):
    din, dout = w.shape
    return np.ascontiguousarray(w.reshape(din // 128, 128, dout).transpose(1, 0, 2))


def _fm(x):
    t, d = x.shape
    return np.ascontiguousarray(x.T.reshape(d // 128, 128, t))


def _rope_tables():
    inv = 1.0 / 10000.0 ** (np.arange(0, HD, 2, dtype=np.float32) / HD)
    t = np.arange(T, dtype=np.float32)
    fr = np.outer(t, inv)
    c = np.cos(fr).T
    s = np.sin(fr).T
    cosf = np.concatenate([c, c], 0).astype(np.float32)
    sinS = np.concatenate([-s, s], 0).astype(np.float32)
    return cosf, sinS



class Pb:

    def __init__(self, nc):
        self.nc = nc
        self.ev = {e: [] for e in ENGS}
        self.cnt = {}
        self.sems = {}
        self.st = contextlib.ExitStack()

    def sem(self, name):
        if name not in self.sems:
            self.sems[name] = self.st.enter_context(self.nc.semaphore(name))
            self.cnt[name] = 0
        return self.sems[name]

    def on(self, eng, fn):
        self.ev[eng].append(fn)

    def wait(self, eng, token):
        if token is None:
            return
        sname, val = token
        sem = self.sem(sname)
        self.on(eng, lambda e, s=sem, v=val: e.wait_ge(s, v))

    def waits(self, eng, tokens):
        best = {}
        for t in tokens:
            if t is None:
                continue
            s, v = t
            best[s] = max(best.get(s, 0), v)
        for s, v in best.items():
            self.wait(eng, (s, v))

    def step(self, eng, build, inc=None, n=1):
        tok = None
        if inc is not None:
            sem = self.sem(inc)
            self.cnt[inc] += n
            tok = (inc, self.cnt[inc])
            def fn(e, b=build, s=sem, nn=n):
                b(e).then_inc(s, nn)
        else:
            def fn(e, b=build):
                b(e)
        self.on(eng, fn)
        return tok

    def now(self, sem):
        self.sem(sem)
        return (sem, self.cnt[sem])

    def fence(self):
        toks = [self.now(s) for s in list(self.sems)]
        for e in ENGS:
            self.waits(e, toks)

    def sb(self, scope, name, shape, dtype):
        return scope.enter_context(self.nc.sbuf_tensor(name, shape, dtype))

    def ps(self, scope, name, shape, dtype=F32):
        return scope.enter_context(self.nc.psum_tensor(name, shape, dtype))


def build_program(n_sub=N_SUB, tap=None):
    nc = bass.Bass("TRN2", target_bir_lowering=False)
    p = Pb(nc)

    dram = {}
    def din(name, shape, dtype=F32):
        dram[name] = nc.dram_tensor(name, shape, dtype, kind="ExternalInput").ap()
        return dram[name]

    wqk_a = din("wqk_a", [128, 16, 16, 128])
    wv_a = din("wv_a", [128, 16, 1024])
    wo_a = din("wo_a", [128, 16, 8, 128])
    wqk_b = din("wqk_b", [128, 16, 16, 128])
    wv_b = din("wv_b", [128, 16, 1024])
    wo_b = din("wo_b", [128, 16, 8, 128])
    wgt = din("wgt", [128, 128])
    gbi = din("gbi", [128, 1])
    wg_f = din("wg_f", [128, 16, II])
    wu_f = din("wu_f", [128, 16, II])
    wd_f = din("wd_f", [128, 16, 64, 128])
    x0 = din("x0", [KC, 128, TOWN])
    cosf = din("cosf", [128, T])
    sinSf = din("sinSf", [128, T])
    identf = din("identf", [128, 128])

    y = nc.dram_tensor("y", [KC, 128, TOWN], F32, kind="ExternalOutput").ap()
    if tap:
        dram["__tav"] = nc.dram_tensor("t_av", [128, MYH, T], BF16, kind="ExternalOutput").ap()
        dram["__tpmy"] = nc.dram_tensor("t_pmy", [KC * 128, T], F32, kind="ExternalOutput").ap()
        dram["__tpart"] = nc.dram_tensor("t_part", [2 * KC * 128, TOWN], F32, kind="ExternalOutput").ap()
        dram["__tpmd"] = nc.dram_tensor("t_pmd", [KC * 128, TOWN], F32, kind="ExternalOutput").ap()
        dram["__tppd"] = nc.dram_tensor("t_ppd", [KC * 128, TOWN], F32, kind="ExternalOutput").ap()

    xs = [x0] + [nc.dram_tensor(f"x{i}", [KC, 128, TOWN], F32, kind="Internal").ap()
                 for i in (1, 2, 3)]
    lnsh = [nc.dram_tensor(f"lnsh{i}", [KC * 128, T], BF16, kind="Internal",
                           addr_space="Shared").ap() for i in (0, 1)]
    partial = [nc.dram_tensor(f"part{i}", [2 * KC * 128, TOWN], F32, kind="Internal",
                              addr_space="Shared").ap() for i in (0, 1)]
    lnmy = [nc.dram_tensor(f"lnmy{i}", [KC * 128, TOWN], BF16, kind="Internal").ap()
            for i in (0, 1)]
    part_my = [nc.dram_tensor(f"pmy{i}", [KC * 128, T], F32, kind="Internal").ap()
               for i in (0, 1)]
    pm_dram = [nc.dram_tensor(f"pmd{i}", [KC * 128, TOWN], F32, kind="Internal").ap()
               for i in (0, 1)]
    pp_dram = [nc.dram_tensor(f"ppd{i}", [KC * 128, TOWN], F32, kind="Internal").ap()
               for i in (0, 1)]
    bar_in = nc.dram_tensor("bar_in", [1, 4], F32, kind="Internal").ap()
    bar_out = nc.dram_tensor("bar_out", [1, 4], F32, kind="Internal").ap()

    chain = [(xs[0], xs[1]), (xs[1], xs[2]), (xs[2], xs[3]), (xs[3], y)]
    chain[n_sub - 1] = (chain[n_sub - 1][0], y)

    top = p.st

    cos_sb = p.sb(top, "cos_sb", [128, T], BF16)
    sin_sb = p.sb(top, "sin_sb", [128, T], BF16)
    idn = p.sb(top, "idn", [128, 128], BF16)
    ones = p.sb(top, "ones", [128, 128], BF16)
    gwt_sb = p.sb(top, "gwt_sb", [128, 128], BF16)
    gb_sb = p.sb(top, "gb_sb", [128, 1], F32)
    xst = p.sb(top, "xst", [128, 2, TOWN], F32)
    barsb = p.sb(top, "barsb", [1, 4], F32)
    eps_sb = p.sb(top, "eps_sb", [128, 1], F32)

    t_cos = p.step("pool", lambda e: e.dma_start(out=cos_sb[:, :], in_=cosf), inc="cst", n=16)
    t_sin = p.step("pool", lambda e: e.dma_start(out=sin_sb[:, :], in_=sinSf), inc="cst", n=16)
    t_idn = p.step("pool", lambda e: e.dma_start(out=idn[:, :], in_=identf), inc="cst", n=16)
    t_gwt = p.step("pool", lambda e: e.dma_start(out=gwt_sb[:, :], in_=wgt), inc="cst", n=16)
    t_gb = p.step("sync", lambda e: e.dma_start(out=gb_sb[:, :], in_=gbi), inc="cst2", n=16)
    t_ones = p.step("dve", lambda e: e.memset(ones[:, :], 1.0), inc="dve")
    t_eps = p.step("dve", lambda e: e.memset(eps_sb[:, :], EPS), inc="dve")
    t_bar0 = p.step("dve", lambda e: e.memset(barsb[:, :], 1.0), inc="dve")
    p.wait("sync", t_bar0)
    t_barin = p.step("sync", lambda e: e.dma_start(out=bar_in, in_=barsb[:, :]), inc="cst2", n=16)

    creg = {}
    def conds(eng):
        if eng not in creg:
            box = {}
            def mk(e, bx=box):
                hv = e.snap(e.partition_id() % 2)
                bx[1] = hv
                bx[0] = e.snap((hv + 1) % 2)
            p.on(eng, mk)
            creg[eng] = box
        return creg[eng]

    for e in ("sync", "act", "pool"):
        conds(e)

    def cdma(eng, mkfn, inc):
        p.step(eng, lambda e, m=mkfn: m(e, 0, conds(eng)[0]), inc=inc, n=16)
        return p.step(eng, lambda e, m=mkfn: m(e, 1, conds(eng)[1]), inc=inc, n=16)

    nbar = [0]

    def barrier(wait_tokens):
        p.waits("pool", wait_tokens + [t_barin])
        tok = p.step(
            "pool",
            lambda e: e.collective_compute(
                "AllReduce", ALU.add, [[0, 1], [2, 3], [4, 5], [6, 7]],
                [bar_in], [bar_out]),
            inc="bar")
        nbar[0] += 1
        return tok

    ln_ctr = [0]

    def emit_ln(scope, x_in, x_bf, sink):
        ln_ctr[0] += 1
        u = str(ln_ctr[0])
        st_su = p.sb(scope, "st_su" + u, [1, TOWN], F32)
        st_sq = p.sb(scope, "st_sq" + u, [1, TOWN], F32)
        st_mu = p.sb(scope, "st_mu" + u, [1, TOWN], F32)
        st_t = p.sb(scope, "st_t" + u, [1, TOWN], F32)
        st_rb = p.sb(scope, "st_rb" + u, [1, TOWN], BF16)
        st_m2 = p.sb(scope, "st_m2" + u, [1, TOWN], BF16)
        R_sb = p.sb(scope, "R_sb" + u, [128, TOWN], BF16)
        M2_sb = p.sb(scope, "M2_sb" + u, [128, TOWN], BF16)
        x2r = p.sb(scope, "x2r" + u, [128, 2, TOWN], BF16)
        with (nc.psum_tensor("ps_su" + u, [1, TOWN], F32) as ps_su,
              nc.psum_tensor("ps_sq" + u, [1, TOWN], F32) as ps_sq,
              nc.psum_tensor("ps_bc" + u, [128, TOWN], F32) as ps_bc,
              nc.psum_tensor("ps_bc2" + u, [128, TOWN], F32) as ps_bc2):
            ld = [None] * KC
            casts = [None] * KC
            sqs = [None] * KC
            pes = [None] * KC
            for kc in range(KC):
                sl = kc % 2
                p.waits("sync", [casts[kc - 2]] if kc >= 2 else [])
                ld[kc] = p.step("sync", lambda e, sl=sl, kc=kc: e.dma_start(
                    out=xst[:, sl, :], in_=x_in[kc]), inc=f"xld{sl}", n=16)
                p.wait("dve", ld[kc])
                casts[kc] = p.step("dve", lambda e, sl=sl, kc=kc: e.tensor_copy(
                    x_bf[:, kc, :], xst[:, sl, :]), inc="dve")
                p.waits("dve", [pes[kc - 2]] if kc >= 2 else [])
                sqs[kc] = p.step("dve", lambda e, sl=sl, kc=kc: e.tensor_tensor(
                    x2r[:, sl, :], x_bf[:, kc, :], x_bf[:, kc, :], ALU.mult),
                    inc="dve")
                p.waits("pe", [casts[kc], sqs[kc], t_ones])
                for ts in range(2):
                    p.step("pe", lambda e, kc=kc, ts=ts: e.matmul(
                        ps_su[0:1, ts * 512:(ts + 1) * 512], ones[:, 0:1],
                        x_bf[:, kc, ts * 512:(ts + 1) * 512],
                        start=(kc == 0), stop=(kc == KC - 1)))
                    last = (ts == 1)
                    pes[kc] = p.step("pe", lambda e, kc=kc, ts=ts, sl=sl: e.matmul(
                        ps_sq[0:1, ts * 512:(ts + 1) * 512], ones[:, 0:1],
                        x2r[:, sl, ts * 512:(ts + 1) * 512],
                        start=(kc == 0), stop=(kc == KC - 1)), inc="pe")
            t_stats = pes[KC - 1]
            p.wait("act", t_stats)
            a1 = p.step("act", lambda e: e.copy(st_su[:, :], ps_su[:, :]), inc="act")
            a2 = p.step("act", lambda e: e.copy(st_sq[:, :], ps_sq[:, :]), inc="act")
            p.waits("dve", [a1, a2])
            d1 = p.step("dve", lambda e: e.tensor_scalar_mul(st_mu[:, :], st_su[:, :], 1.0 / D), inc="dve")
            d2 = p.step("dve", lambda e: e.tensor_scalar_mul(st_sq[:, :], st_sq[:, :], 1.0 / D), inc="dve")
            d3 = p.step("dve", lambda e: e.tensor_tensor(st_su[:, :], st_mu[:, :], st_mu[:, :], ALU.mult), inc="dve")
            d4 = p.step("dve", lambda e: e.tensor_tensor(st_sq[:, :], st_sq[:, :], st_su[:, :], ALU.subtract), inc="dve")
            p.waits("act", [d4, t_eps])
            a3 = p.step("act", lambda e: e.activation(st_t[:, :], st_sq[:, :], AF.Ln, bias=eps_sb[0:1, :], scale=1.0), inc="act")
            a4 = p.step("act", lambda e: e.activation(st_su[:, :], st_t[:, :], AF.Exp, bias=0.0, scale=-0.5), inc="act")
            p.waits("dve", [a4])
            d5 = p.step("dve", lambda e: e.tensor_copy(st_rb[:, :], st_su[:, :]), inc="dve")
            d6 = p.step("dve", lambda e: e.tensor_tensor(st_m2[:, :], st_mu[:, :], st_su[:, :], ALU.mult), inc="dve")
            p.waits("pe", [d5, d6])
            for ts in range(2):
                p.step("pe", lambda e, ts=ts: e.matmul(
                    ps_bc[:, ts * 512:(ts + 1) * 512], ones[0:1, :],
                    st_rb[0:1, ts * 512:(ts + 1) * 512], start=True, stop=True))
                tb = p.step("pe", lambda e, ts=ts: e.matmul(
                    ps_bc2[:, ts * 512:(ts + 1) * 512], ones[0:1, :],
                    st_m2[0:1, ts * 512:(ts + 1) * 512], start=True, stop=True), inc="pe")
            p.wait("act", tb)
            a5 = p.step("act", lambda e: e.copy(R_sb[:, :], ps_bc[:, :]), inc="act")
            a6 = p.step("act", lambda e: e.copy(M2_sb[:, :], ps_bc2[:, :]), inc="act")
            p.waits("dve", [a5, a6])
            for kc in range(KC):
                sink_wait = sink(kc)
                lnap, extra = sink_wait
                p.waits("dve", extra)
                p.step("dve", lambda e, kc=kc, lnap=lnap: e.tensor_tensor(
                    lnap, x_bf[:, kc, :], R_sb[:, :], ALU.mult), inc="dve")
                tok = p.step("dve", lambda e, kc=kc, lnap=lnap: e.tensor_tensor(
                    lnap, lnap, M2_sb[:, :], ALU.subtract), inc="dve")
                sink_done(kc, tok)
        return

    sink_done_cell = [None]
    def sink_done(kc, tok):
        sink_done_cell[0](kc, tok)

    def emit_attn(idx, variant, x_in, x_out):
        li = 0 if variant == "a" else 1
        lsh = lnsh[li]
        part = partial[li]
        lmy = lnmy[li]
        pmy = part_my[li]
        pmd = pm_dram[li]
        ppd = pp_dram[li]
        wqk = wqk_a if variant == "a" else wqk_b
        wv = wv_a if variant == "a" else wv_b
        wo = wo_a if variant == "a" else wo_b

        with contextlib.ExitStack() as S:
            q_fm = p.sb(S, f"qfm{idx}", [128, MYH, T], BF16)
            k_fm = p.sb(S, f"kfm{idx}", [128, MYH, T], BF16)
            v_ext = p.sb(S, f"vex{idx}", [128, KC, MYH, 129], BF16)

            S1 = S.enter_context(contextlib.ExitStack())
            x_bf = p.sb(S1, f"xbf{idx}", [128, KC, TOWN], BF16)
            lnr = p.sb(S1, f"lnr{idx}", [128, 2, TOWN], BF16)
            ln_dma = [None] * KC

            def sink(kc):
                extra = [ln_dma[kc - 2]] if kc >= 2 else []
                return lnr[:, kc % 2, :], extra

            def sdone(kc, tok):
                p.wait("sync", tok)
                ln_dma[kc] = p.step("sync", lambda e, kc=kc: e.dma_start(
                    out=lmy[kc * 128:(kc + 1) * 128, :],
                    in_=lnr[:, kc % 2, :]), inc=f"lnw{kc % 2}", n=16)

            sink_done_cell[0] = sdone
            emit_ln(S1, x_in, x_bf, sink)
            p.waits("sync", [ln_dma[KC - 1], ln_dma[KC - 2]])
            t_lnd2d = cdma("sync", lambda e, h, c: e.dma_start(
                out=lsh[:, h * TOWN:(h + 1) * TOWN], in_=lmy[:, :], cond=c),
                inc="lnd")
            t_bar = barrier([t_lnd2d])
            p.fence()
            S1.close()

            S2 = S.enter_context(contextlib.ExitStack())
            ln_all = p.sb(S2, f"lna{idx}", [128, KC, T], BF16)

            t_vones = p.step("dve", lambda e: e.memset(
                v_ext[:, :, :, 128:129], 1.0), inc="dve")

            p.wait("sync", t_bar)
            lnld = [None] * KC
            for kc in range(KC):
                lnld[kc] = p.step("sync", lambda e, kc=kc: e.dma_start(
                    out=ln_all[:, kc, :], in_=lsh[kc * 128:(kc + 1) * 128, :]),
                    inc="lnl", n=16)

            with (contextlib.ExitStack() as S2a,
                  nc.psum_tensor(f"ps_qk{idx}", [128, 2, 512], F32) as ps_qk):
                wring = p.sb(S2a, f"wr{idx}", [128, 2, KC, 128], BF16)
                wld = [None] * 16
                evac = {}
                for mt in range(16):
                    sl = mt % 2
                    p.waits("pool", [evac.get((mt - 2, 3))])
                    wld[mt] = p.step("pool", lambda e, mt=mt, sl=sl: e.dma_start(
                        out=wring[:, sl, :, :], in_=wqk[:, mt, :, :]),
                        inc=f"wq{sl}", n=16)
                    for ts in range(4):
                        psl = (mt * 4 + ts) % 2
                        p.waits("pe", [wld[mt], lnld[KC - 1], evac.get((mt, ts - 2) if ts >= 2 else (mt - 1, ts + 2))])
                        for kc in range(KC):
                            tok = p.step("pe", lambda e, mt=mt, ts=ts, kc=kc, sl=sl, psl=psl: e.matmul(
                                ps_qk[:, psl, :], wring[:, sl, kc, :],
                                ln_all[:, kc, ts * 512:(ts + 1) * 512],
                                start=(kc == 0), stop=(kc == KC - 1)),
                                inc="pe" if kc == KC - 1 else None)
                        dst = q_fm if mt < 8 else k_fm
                        hh = mt % 8
                        p.wait("act", tok)
                        evac[(mt, ts)] = p.step("act", lambda e, dst=dst, hh=hh, ts=ts, psl=psl: e.copy(
                            dst[:, hh, ts * 512:(ts + 1) * 512], ps_qk[:, psl, :]),
                            inc="act")
                p.fence()

            with (contextlib.ExitStack() as S2b,
                  nc.psum_tensor(f"ps_v{idx}", [128, 2, 512], F32) as ps_v):
                wvs = p.sb(S2b, f"wvs{idx}", [128, KC, 512], BF16)
                vevac = {}
                vmm = {}
                for ng in range(2):
                    p.waits("pool", [vmm.get((ng - 1, KC - 1))])
                    t_wv = p.step("pool", lambda e, ng=ng: e.dma_start(
                        out=wvs[:, :, :], in_=wv[:, :, ng * 512:(ng + 1) * 512]),
                        inc="wvv", n=16)
                    for tt in range(KC):
                        psl = tt % 2
                        p.waits("pe", [t_wv, vevac.get((ng, tt - 2) if tt >= 2 else (ng - 1, tt + 14))])
                        for kc in range(KC):
                            tok = p.step("pe", lambda e, tt=tt, kc=kc, psl=psl: e.matmul(
                                ps_v[:, psl, :], ln_all[:, kc, tt * 128:(tt + 1) * 128],
                                wvs[:, kc, :],
                                start=(kc == 0), stop=(kc == KC - 1)),
                                inc="pe" if kc == KC - 1 else None)
                        vmm[(ng, tt)] = tok
                        p.waits("dve", [tok, t_vones])
                        vevac[(ng, tt)] = p.step("dve", lambda e, tt=tt, ng=ng, psl=psl: e.tensor_copy(
                            v_ext[:, tt, 4 * ng:4 * ng + 4, 0:128],
                            ps_v[:, psl, :].rearrange("p (h m) -> p h m", m=128)),
                            inc="dve")
                p.fence()

            S2.close()
            with contextlib.ExitStack() as S3:
                if variant == "a":
                    qsw = p.sb(S3, f"qsw{idx}", [128, MYH, T], BF16)
                    for tgt in (q_fm, k_fm):
                        s1 = p.step("sync", lambda e, tgt=tgt: e.dma_start(
                            out=qsw[0:64, :, :], in_=tgt[64:128, :, :]), inc="rsw", n=16)
                        s2 = p.step("sync", lambda e, tgt=tgt: e.dma_start(
                            out=qsw[64:128, :, :], in_=tgt[0:64, :, :]), inc="rsw", n=16)
                        p.waits("dve", [s1, s2, t_cos, t_sin])
                        cosb = cos_sb[:, :].unsqueeze(1).broadcast_to([128, MYH, T])
                        sinb = sin_sb[:, :].unsqueeze(1).broadcast_to([128, MYH, T])
                        p.step("dve", lambda e, tgt=tgt, c=cosb: e.tensor_tensor(
                            tgt[:, :, :], tgt[:, :, :], c, ALU.mult), inc="dve")
                        p.step("dve", lambda e, s=sinb: e.tensor_tensor(
                            qsw[:, :, :], qsw[:, :, :], s, ALU.mult), inc="dve")
                        tok = p.step("dve", lambda e, tgt=tgt: e.tensor_tensor(
                            tgt[:, :, :], tgt[:, :, :], qsw[:, :, :], ALU.add), inc="dve")
                        p.wait("sync", tok)
                    p.fence()

            av_all = p.sb(S, f"av{idx}", [128, MYH, T], BF16)
            p_sb = p.sb(S, f"pp{idx}", [128, 2, 1024], BF16)
            sinv = p.sb(S, f"sinv{idx}", [128, 512], F32)
            brow = p.sb(S, f"brow{idx}", [1, 1024], BF16)
            sbg = p.sb(S, f"sbg{idx}", [128, 2, 512], BF16)

            with (nc.psum_tensor(f"ps_s{idx}", [128, 2, 1024], F32) as ps_s,
                  nc.psum_tensor(f"ps_av{idx}", [128, 2, 512], F32) as ps_av,
                  nc.psum_tensor(f"ps_sum{idx}", [128, 512], F32) as ps_sum,
                  nc.psum_tensor(f"ps_sb{idx}", [128, 512], F32) as ps_sb):
                exp_tok = {}
                af_tok = {}
                av_tok = {}
                avn_tok = {}
                last_sb = [None]
                last_sg = []
                last_sinv = []
                it = 0
                for slab in range(4):
                    nk = 4 * (slab + 1)
                    for hg in range(4):
                        h0 = 2 * hg
                        for kt in range(nk):
                            sl = it % 2
                            p.waits("pe", [exp_tok.get(it - 2), avn_tok.get((slab, hg - 1) if hg else (slab - 1, 3))])
                            for hh in range(2):
                                tok = p.step("pe", lambda e, sl=sl, hh=hh, h0=h0, kt=kt, slab=slab: e.matmul(
                                    ps_s[:, sl, hh * 512:(hh + 1) * 512],
                                    k_fm[:, h0 + hh, kt * 128:(kt + 1) * 128],
                                    q_fm[:, h0 + hh, slab * 512:(slab + 1) * 512],
                                    start=True, stop=True), inc="pe")
                            p.waits("act", [tok, av_tok.get(it - 2)])
                            exp_tok[it] = p.step("act", lambda e, sl=sl: e.activation(
                                p_sb[:, sl, :], ps_s[:, sl, :], AF.Exp,
                                bias=0.0, scale=SCALE), inc="act")
                            dep = exp_tok[it]
                            if kt >= nk - 4:
                                jrel = 512 * slab - 128 * kt
                                p.wait("pool", exp_tok[it])
                                af_tok[it] = p.step("pool", lambda e, sl=sl, jrel=jrel: e.affine_select(
                                    p_sb[:, sl, :], p_sb[:, sl, :],
                                    [[0, 2], [1, 512]], ALU.is_ge, 0.0,
                                    base=jrel, channel_multiplier=-1), inc="pool")
                                dep = af_tok[it]
                            p.waits("pe", [dep])
                            for hh in range(2):
                                p.step("pe", lambda e, sl=sl, hh=hh, h0=h0, kt=kt: e.matmul(
                                    ps_av[:, hh, :], v_ext[:, kt, h0 + hh, 0:128],
                                    p_sb[:, sl, hh * 512:(hh + 1) * 512],
                                    start=(kt == 0), stop=(kt == nk - 1)))
                                tp = (0, 32 * hh) if hh else None
                                av_tok[it] = p.step("pe", lambda e, sl=sl, hh=hh, h0=h0, kt=kt, tp=tp: e.matmul(
                                    ps_sum[32 * hh:32 * hh + 1, :],
                                    v_ext[:, kt, h0 + hh, 128:129],
                                    p_sb[:, sl, hh * 512:(hh + 1) * 512],
                                    start=(kt == 0), stop=(kt == nk - 1),
                                    tile_position=tp), inc="pe")
                            it += 1
                        tlast = av_tok[it - 1]
                        p.waits("act", [tlast, last_sb[0]] + last_sinv)
                        ac0 = p.step("act", lambda e: e.copy(
                            sinv[0:1, :], ps_sum[0:1, :]), inc="act")
                        ac1 = p.step("act", lambda e: e.copy(
                            sinv[32:33, :], ps_sum[32:33, :]), inc="act")
                        p.waits("dve", [ac0, ac1])
                        dr0 = p.step("dve", lambda e: e.reciprocal(
                            sinv[0:1, :], sinv[0:1, :]), inc="dve")
                        dr1 = p.step("dve", lambda e: e.reciprocal(
                            sinv[32:33, :], sinv[32:33, :]), inc="dve")
                        p.waits("pool", [dr0, dr1] + last_sg)
                        sg0 = p.step("pool", lambda e: e.dma_start(
                            out=brow[0:1, 0:512], in_=sinv[0:1, :]), inc="sgg", n=16)
                        sg = p.step("pool", lambda e: e.dma_start(
                            out=brow[0:1, 512:1024], in_=sinv[32:33, :]), inc="sgg", n=16)
                        last_sg = []
                        last_sinv = [sg0, sg]
                        for hh in range(2):
                            head = h0 + hh
                            p.waits("pe", [sg0, sg, t_ones, last_sb[0]])
                            bc = p.step("pe", lambda e, hh=hh: e.matmul(
                                ps_sb[:, :], ones[0:1, :], brow[0:1, hh * 512:(hh + 1) * 512],
                                start=True, stop=True), inc="pe")
                            last_sg.append(bc)
                            p.wait("act", bc)
                            sb1 = p.step("act", lambda e: e.copy(
                                sbg[:, 0, :], ps_sb[:, :]), inc="act")
                            last_sb[0] = sb1
                            if variant == "b":
                                p.waits("pe", [sb1, t_gwt])
                                gm = p.step("pe", lambda e, head=head, slab=slab: e.matmul(
                                    ps_sb[:, :], gwt_sb[:, :],
                                    q_fm[:, head, slab * 512:(slab + 1) * 512],
                                    start=True, stop=True), inc="pe")
                                p.waits("act", [gm, t_gb])
                                sb2 = p.step("act", lambda e: e.activation(
                                    sbg[:, 1, :], ps_sb[:, :], AF.Sigmoid,
                                    bias=gb_sb[:, :], scale=1.0), inc="act")
                                last_sb[0] = sb2
                            p.waits("dve", [sb1, tlast])
                            dn = p.step("dve", lambda e, head=head, slab=slab, hh=hh: e.tensor_tensor(
                                av_all[:, head, slab * 512:(slab + 1) * 512],
                                ps_av[:, hh, :], sbg[:, 0, :], ALU.mult), inc="dve")
                            if variant == "b":
                                p.waits("dve", [sb2])
                                dn = p.step("dve", lambda e, head=head, slab=slab: e.tensor_tensor(
                                    av_all[:, head, slab * 512:(slab + 1) * 512],
                                    av_all[:, head, slab * 512:(slab + 1) * 512],
                                    sbg[:, 1, :], ALU.mult), inc="dve")
                            avn_tok[(slab, hg)] = dn
                p.fence()

            ostg = p.sb(S, f"ostg{idx}", [128, 2, 512], F32)
            wro = p.sb(S, f"wro{idx}", [128, 2, 8, 128], BF16)
            with nc.psum_tensor(f"ps_o{idx}", [128, 2, 512], F32) as ps_o:
                wld = [None] * 16
                odma = {}
                for mt in range(16):
                    sl = mt % 2
                    p.waits("pool", [odma.get((mt - 2, 3))])
                    wld[mt] = p.step("pool", lambda e, mt=mt, sl=sl: e.dma_start(
                        out=wro[:, sl, 0:8, :], in_=wo[:, mt, :, :]),
                        inc=f"wo{sl}", n=16)
                    for slab in range(4):
                        psl = (mt * 4 + slab) % 2
                        p.waits("pe", [wld[mt], odma.get((mt, slab - 2) if slab >= 2 else (mt - 1, slab + 2))])
                        for hc in range(8):
                            tok = p.step("pe", lambda e, mt=mt, slab=slab, hc=hc, sl=sl, psl=psl: e.matmul(
                                ps_o[:, psl, :], wro[:, sl, hc, :],
                                av_all[:, hc, slab * 512:(slab + 1) * 512],
                                start=(hc == 0), stop=(hc == 7)),
                                inc="pe" if hc == 7 else None)
                        p.waits("act", [tok, odma.get((mt, slab - 2) if slab >= 2 else (mt - 1, slab + 2))])
                        ev = p.step("act", lambda e, psl=psl: e.copy(
                            ostg[:, psl, :], ps_o[:, psl, :]), inc="act")
                        p.wait("sync", ev)
                        odma[(mt, slab)] = p.step("sync", lambda e, mt=mt, slab=slab, psl=psl: e.dma_start(
                            out=pmy[mt * 128:(mt + 1) * 128, slab * 512:(slab + 1) * 512],
                            in_=ostg[:, psl, :]), inc=f"odm{psl}", n=16)
            p.waits("act", [p.now("odm0"), p.now("odm1")])
            t_pw = cdma("act", lambda e, h, c: e.dma_start(
                out=part[h * (KC * 128):(h + 1) * (KC * 128), :],
                in_=pmy[:, (1 - h) * TOWN:(1 - h) * TOWN + TOWN], cond=c),
                inc="tpw")
            p.waits("pool", [p.now("odm0"), p.now("odm1")])
            t_pmx = cdma("pool", lambda e, h, c: e.dma_start(
                out=pmd[:, :], in_=pmy[:, h * TOWN:h * TOWN + TOWN], cond=c),
                inc="pxx")
            t_bar2 = barrier([t_pw, t_pmx])
            p.wait("pool", t_bar2)
            t_ppx = cdma("pool", lambda e, h, c: e.dma_start(
                out=ppd[:, :], in_=part[(1 - h) * (KC * 128):(2 - h) * (KC * 128), :],
                cond=c), inc="pxx")
            p.fence()

            prr = p.sb(S, f"prr{idx}", [128, 4, TOWN], F32)
            p.wait("act", t_bar2)
            p.wait("sync", t_bar2)
            rtok = {}
            for kc in range(KC):
                sl = kc % 2
                p.waits("sync", [rtok.get(kc - 2)])
                l1 = p.step("sync", lambda e, kc=kc, sl=sl: e.dma_start(
                    out=xst[:, sl, :], in_=x_in[kc]), inc=f"xld{sl}", n=16)
                p.waits("act", [rtok.get(kc - 2)])
                l2 = p.step("act", lambda e, kc=kc, sl=sl: e.dma_start(
                    out=prr[:, 2 * sl, :],
                    in_=pmd[kc * 128:(kc + 1) * 128, :]), inc=f"prr{sl}", n=16)
                l3 = p.step("act", lambda e, kc=kc, sl=sl: e.dma_start(
                    out=prr[:, 2 * sl + 1, :],
                    in_=ppd[kc * 128:(kc + 1) * 128, :]), inc=f"prr{sl}", n=16)
                p.waits("dve", [l1, l2, l3])
                p.step("dve", lambda e, sl=sl: e.tensor_tensor(
                    prr[:, 2 * sl, :], prr[:, 2 * sl, :], prr[:, 2 * sl + 1, :],
                    ALU.add), inc="dve")
                dtok = p.step("dve", lambda e, sl=sl: e.tensor_tensor(
                    xst[:, sl, :], xst[:, sl, :], prr[:, 2 * sl, :], ALU.add),
                    inc="dve")
                p.wait("sync", dtok)
                rtok[kc] = p.step("sync", lambda e, kc=kc, sl=sl: e.dma_start(
                    out=x_out[kc], in_=xst[:, sl, :]), inc=f"xso{sl}", n=16)
            p.fence()
            if tap and idx == 0:
                p.step("sync", lambda e: e.dma_start(out=dram["__tav"], in_=av_all[:, :, :]), inc="tap", n=16)
                p.step("sync", lambda e: e.dma_start(out=dram["__tpmy"], in_=pmy), inc="tap", n=16)
                p.step("sync", lambda e: e.dma_start(out=dram["__tpart"], in_=part), inc="tap", n=16)
                p.step("sync", lambda e: e.dma_start(out=dram["__tpmd"], in_=pmd), inc="tap", n=16)
                p.step("sync", lambda e: e.dma_start(out=dram["__tppd"], in_=ppd), inc="tap", n=16)
                p.fence()

    def emit_ffn(idx, x_in, x_out):
        with contextlib.ExitStack() as S:
            ln = p.sb(S, f"fln{idx}", [128, KC, TOWN], BF16)

            S1 = S.enter_context(contextlib.ExitStack())
            x_bf = p.sb(S1, f"fxbf{idx}", [128, KC, TOWN], BF16)

            def sink(kc):
                return ln[:, kc, :], []

            def sdone(kc, tok):
                pass

            sink_done_cell[0] = sdone
            emit_ln(S1, x_in, x_bf, sink)
            t_ln = p.now("dve")
            p.fence()
            S1.close()

            acc = p.sb(S, f"facc{idx}", [128, KC, TOWN], BF16)
            hng = p.sb(S, f"fh{idx}", [128, 2, 8, 512], BF16)
            ang = p.sb(S, f"fa{idx}", [128, 2, 8, 512], BF16)
            tau = p.sb(S, f"ftau{idx}", [128, 2, 8, 128], BF16)
            tsc = p.sb(S, f"ftsc{idx}", [128, 2, 2, 8, 128], BF16)
            sgr = p.sb(S, f"fsg{idx}", [128, 8, 512], BF16)
            hfm = p.sb(S, f"fhfm{idx}", [128, 16, TOWN], BF16)
            wgu = p.sb(S, f"fwgu{idx}", [128, 2, KC, 512], BF16)
            wdr = p.sb(S, f"fwd{idx}", [128, 2, 16, 128], BF16)

            with (nc.psum_tensor(f"fps_g{idx}", [128, 2, 512], F32) as ps_g,
                  nc.psum_tensor(f"fps_u{idx}", [128, 2, 512], F32) as ps_u,
                  nc.psum_tensor(f"fps_t{idx}", [128, 2, 1024], BF16) as ps_t,
                  nc.psum_tensor(f"fps_d{idx}", [128, 2, 512], F32) as ps_d):
                wl_tok = {}
                gmm_tok = {}
                umm_tok = {}
                sa_tok = {}
                hm_tok = {}
                mask_tok = {}
                tr_evac = {}
                dn_evac = {}
                for qtr in range(4):
                    for ngl in range(4):
                        ng = qtr * 4 + ngl
                        hsl = ng % 2
                        p.waits("pool", [gmm_tok.get((ng - 1, 7))])
                        wl_tok[(ng, 0)] = p.step("pool", lambda e, ng=ng: e.dma_start(
                            out=wgu[:, 0, :, :],
                            in_=wg_f[:, :, ng * 512:(ng + 1) * 512]), inc="wg0", n=16)
                        p.waits("pool", [umm_tok.get((ng - 1, 7))])
                        wl_tok[(ng, 1)] = p.step("pool", lambda e, ng=ng: e.dma_start(
                            out=wgu[:, 1, :, :],
                            in_=wu_f[:, :, ng * 512:(ng + 1) * 512]), inc="wg1", n=16)
                        for tt in range(8):
                            gsl = tt % 2
                            p.waits("pe", [wl_tok[(ng, 0)], t_ln,
                                           sa_tok.get((ng, tt - 2) if tt >= 2 else (ng - 1, tt + 6)),
                                           dn_evac.get(("last", qtr - 1)) if (ngl == 0 and tt == 0) else None])
                            for kc in range(KC):
                                tok = p.step("pe", lambda e, ng=ng, tt=tt, kc=kc, gsl=gsl: e.matmul(
                                    ps_g[:, gsl, :], ln[:, kc, tt * 128:(tt + 1) * 128],
                                    wgu[:, 0, kc, :],
                                    start=(kc == 0), stop=(kc == KC - 1)),
                                    inc="pe" if kc == KC - 1 else None)
                            gmm_tok[(ng, tt)] = tok
                            p.waits("act", [tok, hm_tok.get((ng - 1, tt))])
                            sa_tok[(ng, tt)] = p.step("act", lambda e, tt=tt, gsl=gsl: e.activation(
                                sgr[:, tt, :], ps_g[:, gsl, :], AF.Silu,
                                bias=0.0, scale=1.0), inc="act")
                        for tt in range(8):
                            gsl = tt % 2
                            p.waits("pe", [wl_tok[(ng, 1)],
                                           hm_tok.get((ng, tt - 2) if tt >= 2 else (ng - 1, tt + 6))])
                            for kc in range(KC):
                                tok = p.step("pe", lambda e, ng=ng, tt=tt, kc=kc, gsl=gsl: e.matmul(
                                    ps_u[:, gsl, :], ln[:, kc, tt * 128:(tt + 1) * 128],
                                    wgu[:, 1, kc, :],
                                    start=(kc == 0), stop=(kc == KC - 1)),
                                    inc="pe" if kc == KC - 1 else None)
                            umm_tok[(ng, tt)] = tok
                            p.waits("dve", [tok, sa_tok[(ng, tt)]])
                            hm_tok[(ng, tt)] = p.step("dve", lambda e, tt=tt, hsl=hsl, gsl=gsl: e.tensor_tensor(
                                hng[:, hsl, tt, :], sgr[:, tt, :], ps_u[:, gsl, :],
                                ALU.mult), inc="dve")
                        heng = "dve"
                        esl = 0 if heng == "dve" else 1
                        hfl = hng[:, hsl, :, :]
                        afl = ang[:, hsl, :, :]
                        a4 = afl.rearrange("p t (g j) -> p t g j", j=4)
                        tq = tau[:, esl, :, :]
                        s1 = tsc[:, esl, 0, :, :]
                        s2 = tsc[:, esl, 1, :, :]
                        p.waits("act", [hm_tok[(ng, 7)], mask_tok.get(ng - 2)])
                        aa = p.step("act", lambda e, hfl=hfl, afl=afl: e.activation(
                            afl, hfl, AF.Abs, bias=0.0, scale=1.0), inc="act")
                        E = heng
                        p.waits(E, [aa])
                        p.step(E, lambda e, a4=a4, s1=s1: e.tensor_tensor(s1, a4[:, :, :, 0], a4[:, :, :, 1], ALU.max), inc=E)
                        p.step(E, lambda e, a4=a4, s2=s2: e.tensor_tensor(s2, a4[:, :, :, 2], a4[:, :, :, 3], ALU.max), inc=E)
                        p.step(E, lambda e, s1=s1, s2=s2, tq=tq: e.tensor_tensor(tq, s1, s2, ALU.min), inc=E)
                        p.step(E, lambda e, a4=a4, s1=s1: e.tensor_tensor(s1, a4[:, :, :, 0], a4[:, :, :, 1], ALU.min), inc=E)
                        p.step(E, lambda e, a4=a4, s2=s2: e.tensor_tensor(s2, a4[:, :, :, 2], a4[:, :, :, 3], ALU.min), inc=E)
                        p.step(E, lambda e, s1=s1, s2=s2: e.tensor_tensor(s1, s1, s2, ALU.max), inc=E)
                        p.step(E, lambda e, s1=s1, tq=tq: e.tensor_tensor(tq, tq, s1, ALU.max), inc=E)
                        taub = tq.unsqueeze(3).broadcast_to([128, 8, 128, 4])
                        p.step(E, lambda e, a4=a4, taub=taub: e.tensor_tensor(a4, a4, taub, ALU.is_ge), inc=E)
                        mask_tok[ng] = p.step(E, lambda e, hfl=hfl, afl=afl: e.tensor_tensor(
                            hfl, hfl, afl, ALU.mult), inc=E)
                        for tt in range(8):
                            tsl = tt % 2
                            p.waits("pe", [mask_tok[ng], t_idn,
                                           tr_evac.get((ng, tt - 2) if tt >= 2 else (ng - 1, tt + 6))])
                            for j in range(4):
                                tok = p.step("pe", lambda e, hsl=hsl, tt=tt, j=j, tsl=tsl: e.transpose(
                                    ps_t[:, tsl, j * 128:(j + 1) * 128],
                                    hng[:, hsl, tt, j * 128:(j + 1) * 128], idn[:, :]),
                                    inc="pe" if j == 3 else None)
                            p.wait("dve", tok)
                            tr_evac[(ng, tt)] = p.step("dve", lambda e, ngl=ngl, tt=tt, tsl=tsl: e.tensor_copy(
                                hfm[:, 4 * ngl:4 * ngl + 4, tt * 128:(tt + 1) * 128],
                                ps_t[:, tsl, 0:512].rearrange("p (a b) -> p a b", b=128)),
                                inc="dve")
                    wd_tok = {}
                    for mt in range(16):
                        wsl = mt % 2
                        wprev = (mt - 2, 1) if mt >= 2 else ((mt + 14, 1) if qtr else None)
                        p.waits("pool", [dn_evac.get(wprev) if wprev else None])
                        wd_tok[mt] = p.step("pool", lambda e, qtr=qtr, mt=mt, wsl=wsl: e.dma_start(
                            out=wdr[:, wsl, :, :], in_=wd_f[:, mt, qtr * 16:(qtr + 1) * 16, :]),
                            inc=f"wd{wsl}", n=16)
                        for slab in range(2):
                            psl = (mt * 2 + slab) % 2
                            dprev = (mt - 1, slab) if mt else ((15, slab) if qtr else None)
                            p.waits("pe", [wd_tok[mt], tr_evac[(qtr * 4 + 3, 7)],
                                           dn_evac.get(dprev) if dprev else None])
                            for kq in range(16):
                                tok = p.step("pe", lambda e, mt=mt, slab=slab, kq=kq, wsl=wsl, psl=psl: e.matmul(
                                    ps_d[:, psl, :], wdr[:, wsl, kq, :],
                                    hfm[:, kq, slab * 512:(slab + 1) * 512],
                                    start=(kq == 0), stop=(kq == 15)),
                                    inc="pe" if kq == 15 else None)
                            p.waits("dve", [tok])
                            if qtr == 0:
                                dn_evac[(mt, slab)] = p.step("dve", lambda e, mt=mt, slab=slab, psl=psl: e.tensor_copy(
                                    acc[:, mt, slab * 512:(slab + 1) * 512], ps_d[:, psl, :]),
                                    inc="dve")
                            else:
                                dn_evac[(mt, slab)] = p.step("dve", lambda e, mt=mt, slab=slab, psl=psl: e.tensor_tensor(
                                    acc[:, mt, slab * 512:(slab + 1) * 512],
                                    acc[:, mt, slab * 512:(slab + 1) * 512],
                                    ps_d[:, psl, :], ALU.add), inc="dve")
                    dn_evac[("last", qtr)] = dn_evac[(15, 1)]
                p.fence()

            rtok = {}
            for kc in range(KC):
                sl = kc % 2
                p.waits("sync", [rtok.get(kc - 2)])
                l1 = p.step("sync", lambda e, kc=kc, sl=sl: e.dma_start(
                    out=xst[:, sl, :], in_=x_in[kc]), inc=f"xld{sl}", n=16)
                p.waits("dve", [l1])
                dtok = p.step("dve", lambda e, kc=kc, sl=sl: e.tensor_tensor(
                    xst[:, sl, :], xst[:, sl, :], acc[:, kc, :], ALU.add), inc="dve")
                p.wait("sync", dtok)
                rtok[kc] = p.step("sync", lambda e, kc=kc, sl=sl: e.dma_start(
                    out=x_out[kc], in_=xst[:, sl, :]), inc=f"xso{sl}", n=16)
            p.fence()

    subs = [("attn", "a"), ("ffn", None), ("attn", "b"), ("ffn", None)]
    for i in range(n_sub):
        x_in, x_out = chain[i]
        kind, var = subs[i]
        if kind == "attn":
            emit_attn(i, var, _ChunkWriter(x_in), _ChunkWriter(x_out))
        else:
            emit_ffn(i, _ChunkWriter(x_in), _ChunkWriter(x_out))

    p.fence()

    for _s in p.sems.values():
        nc.gpsimd.sem_clear(_s)
    nc.all_engine_barrier()

    with nc.Block() as block:
        @block.tensor
        def _(t):
            for fn in p.ev["pe"]:
                fn(t)

        @block.scalar
        def _(s):
            for fn in p.ev["act"]:
                fn(s)

        @block.vector
        def _(v):
            for fn in p.ev["dve"]:
                fn(v)

        @block.gpsimd
        def _(g):
            for fn in p.ev["pool"]:
                fn(g)

        @block.sync
        def _(sy):
            for fn in p.ev["sync"]:
                fn(sy)

    p.st.close()
    return nc


class _ChunkWriter:
    def __init__(self, x):
        self.x = x

    def __call__(self, kc):
        return self.x[kc, :, :]

    def __getitem__(self, kc):
        return self.x[kc, :, :]



_cached = {}


def _get_program(n_sub=N_SUB):
    if n_sub not in _cached:
        _cached[n_sub] = build_program(n_sub)
    return _cached[n_sub]


def make_in_maps(inputs):
    w_qkv = inputs["L0_attn_a_w_qkv"]
    w_o = inputs["L0_attn_a_w_o"]
    q_w = inputs["L0_attn_b_q_w"]
    k_w = inputs["L0_attn_b_k_w"]
    v_w = inputs["L0_attn_b_v_w"]
    o_w = inputs["L0_attn_b_o_w"]
    gate_w = inputs["L0_attn_b_gate_w"]
    gate_b = inputs["L0_attn_b_gate_b"]
    w_gate = inputs["L0_ffn_w_gate"]
    w_up = inputs["L0_ffn_w_up"]
    w_down = inputs["L0_ffn_w_down"]
    x = inputs["L0_attn_a_x"]

    cosf, sinS = _rope_tables()
    ident = np.eye(128, dtype=np.float32)
    shared = {
        "wgt": np.ascontiguousarray(gate_w.T),
        "gbi": np.ascontiguousarray(gate_b.reshape(128, 1)),
        "wg_f": _b_layout(w_gate),
        "wu_f": _b_layout(w_up),
        "wd_f": _a_layout(w_down),
        "cosf": cosf,
        "sinSf": sinS,
        "identf": ident,
    }
    qT, kT, vT, oT = q_w.T, k_w.T, v_w.T, o_w.T
    maps = []
    for c in range(8):
        b, h = c // 2, c % 2
        cs = slice(h * 1024, (h + 1) * 1024)
        m = dict(shared)
        m["wqk_a"] = _a_layout(np.concatenate(
            [w_qkv[:, h * 1024:(h + 1) * 1024],
             w_qkv[:, 2048 + h * 1024:2048 + (h + 1) * 1024]], axis=1))
        m["wv_a"] = _b_layout(w_qkv[:, 4096 + h * 1024:4096 + (h + 1) * 1024])
        m["wo_a"] = _a_layout(w_o[h * 1024:(h + 1) * 1024, :])
        m["wqk_b"] = _a_layout(np.concatenate([qT[:, cs], kT[:, cs]], axis=1))
        m["wv_b"] = _b_layout(vT[:, cs])
        m["wo_b"] = _a_layout(oT[h * 1024:(h + 1) * 1024, :])
        m["x0"] = _fm(x[b, h * 1024:(h + 1) * 1024, :])
        maps.append(m)
    return maps


def kernel(**inputs):
    nc = _get_program(N_SUB)
    maps = make_in_maps(inputs)
    res = run_bass_kernel_spmd(nc, maps, core_ids=list(range(8)))
    out = np.empty((B, T, D), dtype=np.float32)
    for c in range(8):
        b, h = c // 2, c % 2
        yc = np.asarray(res.results[c]["y"])
        out[b, h * 1024:(h + 1) * 1024, :] = yc.transpose(2, 0, 1).reshape(TOWN, D)
    return out


if __name__ == "__main__":
    nc = build_program(1)
    print("build ok")


# version 02
# speedup vs baseline: 1.0023x; 1.0023x over previous
import math
import numpy as np
import ml_dtypes
import contextlib
import concourse.bass as bass
import concourse.mybir as mybir
from concourse.bass import ds
from concourse.bass_utils import run_bass_kernel_spmd

F32 = mybir.dt.float32
BF16 = mybir.dt.bfloat16
AF = mybir.ActivationFunctionType
ALU = mybir.AluOpType
BFNP = ml_dtypes.bfloat16

D = 2048
NH = 16
HD = 128
T = 2048
B = 4
II = 8192
MYH = 8
TOWN = 1024
KC = 16
ICH = 64
EPS = 1e-5
SCALE = 1.0 / math.sqrt(HD)

N_SUB = 4

ENGS = ("pe", "act", "dve", "pool", "sync")



def _a_layout(w):
    din, dout = w.shape
    return np.ascontiguousarray(
        w.reshape(din // 128, 128, dout // 128, 128).transpose(1, 2, 0, 3))


def _b_layout(w):
    din, dout = w.shape
    return np.ascontiguousarray(w.reshape(din // 128, 128, dout).transpose(1, 0, 2))


def _fm(x):
    t, d = x.shape
    return np.ascontiguousarray(x.T.reshape(d // 128, 128, t))


def _rope_tables():
    inv = 1.0 / 10000.0 ** (np.arange(0, HD, 2, dtype=np.float32) / HD)
    t = np.arange(T, dtype=np.float32)
    fr = np.outer(t, inv)
    c = np.cos(fr).T
    s = np.sin(fr).T
    cosf = np.concatenate([c, c], 0).astype(np.float32)
    sinS = np.concatenate([-s, s], 0).astype(np.float32)
    return cosf, sinS



class Pb:

    def __init__(self, nc):
        self.nc = nc
        self.ev = {e: [] for e in ENGS}
        self.cnt = {}
        self.sems = {}
        self.st = contextlib.ExitStack()

    def sem(self, name):
        if name not in self.sems:
            self.sems[name] = self.st.enter_context(self.nc.semaphore(name))
            self.cnt[name] = 0
        return self.sems[name]

    def on(self, eng, fn):
        self.ev[eng].append(fn)

    def wait(self, eng, token):
        if token is None:
            return
        sname, val = token
        sem = self.sem(sname)
        self.on(eng, lambda e, s=sem, v=val: e.wait_ge(s, v))

    def waits(self, eng, tokens):
        best = {}
        for t in tokens:
            if t is None:
                continue
            s, v = t
            best[s] = max(best.get(s, 0), v)
        for s, v in best.items():
            self.wait(eng, (s, v))

    def step(self, eng, build, inc=None, n=1):
        tok = None
        if inc is not None:
            sem = self.sem(inc)
            self.cnt[inc] += n
            tok = (inc, self.cnt[inc])
            def fn(e, b=build, s=sem, nn=n):
                b(e).then_inc(s, nn)
        else:
            def fn(e, b=build):
                b(e)
        self.on(eng, fn)
        return tok

    def now(self, sem):
        self.sem(sem)
        return (sem, self.cnt[sem])

    def fence(self):
        toks = [self.now(s) for s in list(self.sems)]
        for e in ENGS:
            self.waits(e, toks)

    def sb(self, scope, name, shape, dtype):
        return scope.enter_context(self.nc.sbuf_tensor(name, shape, dtype))

    def ps(self, scope, name, shape, dtype=F32):
        return scope.enter_context(self.nc.psum_tensor(name, shape, dtype))


def build_program(n_sub=N_SUB, tap=None):
    nc = bass.Bass("TRN2", target_bir_lowering=False)
    p = Pb(nc)

    dram = {}
    def din(name, shape, dtype=F32):
        dram[name] = nc.dram_tensor(name, shape, dtype, kind="ExternalInput").ap()
        return dram[name]

    wqk_a = din("wqk_a", [128, 16, 16, 128])
    wv_a = din("wv_a", [128, 16, 1024])
    wo_a = din("wo_a", [128, 16, 8, 128])
    wqk_b = din("wqk_b", [128, 16, 16, 128])
    wv_b = din("wv_b", [128, 16, 1024])
    wo_b = din("wo_b", [128, 16, 8, 128])
    wgt = din("wgt", [128, 128])
    gbi = din("gbi", [128, 1])
    wg_f = din("wg_f", [128, 16, II])
    wu_f = din("wu_f", [128, 16, II])
    wd_f = din("wd_f", [128, 16, 64, 128])
    x0 = din("x0", [KC, 128, TOWN])
    cosf = din("cosf", [128, T])
    sinSf = din("sinSf", [128, T])
    identf = din("identf", [128, 128])

    y = nc.dram_tensor("y", [KC, 128, TOWN], F32, kind="ExternalOutput").ap()
    if tap:
        dram["__tav"] = nc.dram_tensor("t_av", [128, MYH, T], BF16, kind="ExternalOutput").ap()
        dram["__tpmy"] = nc.dram_tensor("t_pmy", [KC * 128, T], F32, kind="ExternalOutput").ap()
        dram["__tpart"] = nc.dram_tensor("t_part", [2 * KC * 128, TOWN], F32, kind="ExternalOutput").ap()
        dram["__tpmd"] = nc.dram_tensor("t_pmd", [KC * 128, TOWN], F32, kind="ExternalOutput").ap()
        dram["__tppd"] = nc.dram_tensor("t_ppd", [KC * 128, TOWN], F32, kind="ExternalOutput").ap()

    xs = [x0] + [nc.dram_tensor(f"x{i}", [KC, 128, TOWN], F32, kind="Internal").ap()
                 for i in (1, 2, 3)]
    lnsh = [nc.dram_tensor(f"lnsh{i}", [KC * 128, T], BF16, kind="Internal",
                           addr_space="Shared").ap() for i in (0, 1)]
    partial = [nc.dram_tensor(f"part{i}", [2 * KC * 128, TOWN], F32, kind="Internal",
                              addr_space="Shared").ap() for i in (0, 1)]
    lnmy = [nc.dram_tensor(f"lnmy{i}", [KC * 128, TOWN], BF16, kind="Internal").ap()
            for i in (0, 1)]
    part_my = [nc.dram_tensor(f"pmy{i}", [KC * 128, T], F32, kind="Internal").ap()
               for i in (0, 1)]
    pm_dram = [nc.dram_tensor(f"pmd{i}", [KC * 128, TOWN], F32, kind="Internal").ap()
               for i in (0, 1)]
    pp_dram = [nc.dram_tensor(f"ppd{i}", [KC * 128, TOWN], F32, kind="Internal").ap()
               for i in (0, 1)]
    bar_in = nc.dram_tensor("bar_in", [1, 4], F32, kind="Internal").ap()
    bar_out = nc.dram_tensor("bar_out", [1, 4], F32, kind="Internal").ap()

    chain = [(xs[0], xs[1]), (xs[1], xs[2]), (xs[2], xs[3]), (xs[3], y)]
    chain[n_sub - 1] = (chain[n_sub - 1][0], y)

    top = p.st

    cos_sb = p.sb(top, "cos_sb", [128, T], BF16)
    sin_sb = p.sb(top, "sin_sb", [128, T], BF16)
    idn = p.sb(top, "idn", [128, 128], BF16)
    ones = p.sb(top, "ones", [128, 128], BF16)
    gwt_sb = p.sb(top, "gwt_sb", [128, 128], BF16)
    gb_sb = p.sb(top, "gb_sb", [128, 1], F32)
    xst = p.sb(top, "xst", [128, 2, TOWN], F32)
    barsb = p.sb(top, "barsb", [1, 4], F32)
    eps_sb = p.sb(top, "eps_sb", [128, 1], F32)

    t_cos = p.step("pool", lambda e: e.dma_start(out=cos_sb[:, :], in_=cosf), inc="cst", n=16)
    t_sin = p.step("pool", lambda e: e.dma_start(out=sin_sb[:, :], in_=sinSf), inc="cst", n=16)
    t_idn = p.step("pool", lambda e: e.dma_start(out=idn[:, :], in_=identf), inc="cst", n=16)
    t_gwt = p.step("pool", lambda e: e.dma_start(out=gwt_sb[:, :], in_=wgt), inc="cst", n=16)
    t_gb = p.step("sync", lambda e: e.dma_start(out=gb_sb[:, :], in_=gbi), inc="cst2", n=16)
    t_ones = p.step("dve", lambda e: e.memset(ones[:, :], 1.0), inc="dve")
    t_eps = p.step("dve", lambda e: e.memset(eps_sb[:, :], EPS), inc="dve")
    t_bar0 = p.step("dve", lambda e: e.memset(barsb[:, :], 1.0), inc="dve")
    p.wait("sync", t_bar0)
    t_barin = p.step("sync", lambda e: e.dma_start(out=bar_in, in_=barsb[:, :]), inc="cst2", n=16)

    creg = {}
    def conds(eng):
        if eng not in creg:
            box = {}
            def mk(e, bx=box):
                hv = e.snap(e.partition_id() % 2)
                bx[1] = hv
                bx[0] = e.snap((hv + 1) % 2)
            p.on(eng, mk)
            creg[eng] = box
        return creg[eng]

    for e in ("sync", "act", "pool"):
        conds(e)

    def cdma(eng, mkfn, inc):
        p.step(eng, lambda e, m=mkfn: m(e, 0, conds(eng)[0]), inc=inc, n=16)
        return p.step(eng, lambda e, m=mkfn: m(e, 1, conds(eng)[1]), inc=inc, n=16)

    nbar = [0]

    def barrier(wait_tokens):
        p.waits("pool", wait_tokens + [t_barin])
        tok = p.step(
            "pool",
            lambda e: e.collective_compute(
                "AllReduce", ALU.add, [[0, 1], [2, 3], [4, 5], [6, 7]],
                [bar_in], [bar_out]),
            inc="bar")
        nbar[0] += 1
        return tok

    ln_ctr = [0]

    def emit_ln(scope, x_in, x_bf, sink):
        ln_ctr[0] += 1
        u = str(ln_ctr[0])
        st_su = p.sb(scope, "st_su" + u, [1, TOWN], F32)
        st_sq = p.sb(scope, "st_sq" + u, [1, TOWN], F32)
        st_mu = p.sb(scope, "st_mu" + u, [1, TOWN], F32)
        st_t = p.sb(scope, "st_t" + u, [1, TOWN], F32)
        st_rb = p.sb(scope, "st_rb" + u, [1, TOWN], BF16)
        st_m2 = p.sb(scope, "st_m2" + u, [1, TOWN], BF16)
        R_sb = p.sb(scope, "R_sb" + u, [128, TOWN], BF16)
        M2_sb = p.sb(scope, "M2_sb" + u, [128, TOWN], BF16)
        x2r = p.sb(scope, "x2r" + u, [128, 2, TOWN], BF16)
        with (nc.psum_tensor("ps_su" + u, [1, TOWN], F32) as ps_su,
              nc.psum_tensor("ps_sq" + u, [1, TOWN], F32) as ps_sq,
              nc.psum_tensor("ps_bc" + u, [128, TOWN], F32) as ps_bc,
              nc.psum_tensor("ps_bc2" + u, [128, TOWN], F32) as ps_bc2):
            ld = [None] * KC
            casts = [None] * KC
            sqs = [None] * KC
            pes = [None] * KC
            for kc in range(KC):
                sl = kc % 2
                p.waits("sync", [casts[kc - 2]] if kc >= 2 else [])
                ld[kc] = p.step("sync", lambda e, sl=sl, kc=kc: e.dma_start(
                    out=xst[:, sl, :], in_=x_in[kc]), inc=f"xld{sl}", n=16)
                p.wait("dve", ld[kc])
                casts[kc] = p.step("dve", lambda e, sl=sl, kc=kc: e.tensor_copy(
                    x_bf[:, kc, :], xst[:, sl, :]), inc="dve")
                p.waits("dve", [pes[kc - 2]] if kc >= 2 else [])
                sqs[kc] = p.step("dve", lambda e, sl=sl, kc=kc: e.tensor_tensor(
                    x2r[:, sl, :], x_bf[:, kc, :], x_bf[:, kc, :], ALU.mult),
                    inc="dve")
                p.waits("pe", [casts[kc], sqs[kc], t_ones])
                for ts in range(2):
                    p.step("pe", lambda e, kc=kc, ts=ts: e.matmul(
                        ps_su[0:1, ts * 512:(ts + 1) * 512], ones[:, 0:1],
                        x_bf[:, kc, ts * 512:(ts + 1) * 512],
                        start=(kc == 0), stop=(kc == KC - 1)))
                    last = (ts == 1)
                    pes[kc] = p.step("pe", lambda e, kc=kc, ts=ts, sl=sl: e.matmul(
                        ps_sq[0:1, ts * 512:(ts + 1) * 512], ones[:, 0:1],
                        x2r[:, sl, ts * 512:(ts + 1) * 512],
                        start=(kc == 0), stop=(kc == KC - 1)), inc="pe")
            t_stats = pes[KC - 1]
            p.wait("act", t_stats)
            a1 = p.step("act", lambda e: e.copy(st_su[:, :], ps_su[:, :]), inc="act")
            a2 = p.step("act", lambda e: e.copy(st_sq[:, :], ps_sq[:, :]), inc="act")
            p.waits("dve", [a1, a2])
            d1 = p.step("dve", lambda e: e.tensor_scalar_mul(st_mu[:, :], st_su[:, :], 1.0 / D), inc="dve")
            d2 = p.step("dve", lambda e: e.tensor_scalar_mul(st_sq[:, :], st_sq[:, :], 1.0 / D), inc="dve")
            d3 = p.step("dve", lambda e: e.tensor_tensor(st_su[:, :], st_mu[:, :], st_mu[:, :], ALU.mult), inc="dve")
            d4 = p.step("dve", lambda e: e.tensor_tensor(st_sq[:, :], st_sq[:, :], st_su[:, :], ALU.subtract), inc="dve")
            p.waits("act", [d4, t_eps])
            a3 = p.step("act", lambda e: e.activation(st_t[:, :], st_sq[:, :], AF.Ln, bias=eps_sb[0:1, :], scale=1.0), inc="act")
            a4 = p.step("act", lambda e: e.activation(st_su[:, :], st_t[:, :], AF.Exp, bias=0.0, scale=-0.5), inc="act")
            p.waits("dve", [a4])
            d5 = p.step("dve", lambda e: e.tensor_copy(st_rb[:, :], st_su[:, :]), inc="dve")
            d6 = p.step("dve", lambda e: e.tensor_tensor(st_m2[:, :], st_mu[:, :], st_su[:, :], ALU.mult), inc="dve")
            p.waits("pe", [d5, d6])
            for ts in range(2):
                p.step("pe", lambda e, ts=ts: e.matmul(
                    ps_bc[:, ts * 512:(ts + 1) * 512], ones[0:1, :],
                    st_rb[0:1, ts * 512:(ts + 1) * 512], start=True, stop=True))
                tb = p.step("pe", lambda e, ts=ts: e.matmul(
                    ps_bc2[:, ts * 512:(ts + 1) * 512], ones[0:1, :],
                    st_m2[0:1, ts * 512:(ts + 1) * 512], start=True, stop=True), inc="pe")
            p.wait("act", tb)
            a5 = p.step("act", lambda e: e.copy(R_sb[:, :], ps_bc[:, :]), inc="act")
            a6 = p.step("act", lambda e: e.copy(M2_sb[:, :], ps_bc2[:, :]), inc="act")
            p.waits("dve", [a5, a6])
            for kc in range(KC):
                sink_wait = sink(kc)
                lnap, extra = sink_wait
                p.waits("dve", extra)
                p.step("dve", lambda e, kc=kc, lnap=lnap: e.tensor_tensor(
                    lnap, x_bf[:, kc, :], R_sb[:, :], ALU.mult), inc="dve")
                tok = p.step("dve", lambda e, kc=kc, lnap=lnap: e.tensor_tensor(
                    lnap, lnap, M2_sb[:, :], ALU.subtract), inc="dve")
                sink_done(kc, tok)
        return

    sink_done_cell = [None]
    def sink_done(kc, tok):
        sink_done_cell[0](kc, tok)

    def emit_attn(idx, variant, x_in, x_out):
        li = 0 if variant == "a" else 1
        lsh = lnsh[li]
        part = partial[li]
        lmy = lnmy[li]
        pmy = part_my[li]
        pmd = pm_dram[li]
        ppd = pp_dram[li]
        wqk = wqk_a if variant == "a" else wqk_b
        wv = wv_a if variant == "a" else wv_b
        wo = wo_a if variant == "a" else wo_b

        with contextlib.ExitStack() as S:
            q_fm = p.sb(S, f"qfm{idx}", [128, MYH, T], BF16)
            k_fm = p.sb(S, f"kfm{idx}", [128, MYH, T], BF16)
            v_ext = p.sb(S, f"vex{idx}", [128, KC, MYH, 129], BF16)

            S1 = S.enter_context(contextlib.ExitStack())
            x_bf = p.sb(S1, f"xbf{idx}", [128, KC, TOWN], BF16)
            lnr = p.sb(S1, f"lnr{idx}", [128, 2, TOWN], BF16)
            ln_dma = [None] * KC

            def sink(kc):
                extra = [ln_dma[kc - 2]] if kc >= 2 else []
                return lnr[:, kc % 2, :], extra

            def sdone(kc, tok):
                p.wait("sync", tok)
                ln_dma[kc] = p.step("sync", lambda e, kc=kc: e.dma_start(
                    out=lmy[kc * 128:(kc + 1) * 128, :],
                    in_=lnr[:, kc % 2, :]), inc=f"lnw{kc % 2}", n=16)

            sink_done_cell[0] = sdone
            emit_ln(S1, x_in, x_bf, sink)
            p.waits("sync", [ln_dma[KC - 1], ln_dma[KC - 2]])
            t_lnd2d = cdma("sync", lambda e, h, c: e.dma_start(
                out=lsh[:, h * TOWN:(h + 1) * TOWN], in_=lmy[:, :], cond=c),
                inc="lnd")
            t_bar = barrier([t_lnd2d])
            p.fence()
            S1.close()

            S2 = S.enter_context(contextlib.ExitStack())
            ln_all = p.sb(S2, f"lna{idx}", [128, KC, T], BF16)

            t_vones = p.step("dve", lambda e: e.memset(
                v_ext[:, :, :, 128:129], 1.0), inc="dve")

            p.wait("sync", t_bar)
            lnld = [None] * KC
            for kc in range(KC):
                lnld[kc] = p.step("sync", lambda e, kc=kc: e.dma_start(
                    out=ln_all[:, kc, :], in_=lsh[kc * 128:(kc + 1) * 128, :]),
                    inc="lnl", n=16)

            with (contextlib.ExitStack() as S2a,
                  nc.psum_tensor(f"ps_qk{idx}", [128, 2, 512], F32) as ps_qk):
                wring = p.sb(S2a, f"wr{idx}", [128, 2, KC, 128], BF16)
                wld = [None] * 16
                evac = {}
                for mt in range(16):
                    sl = mt % 2
                    p.waits("pool", [evac.get((mt - 2, 3))])
                    wld[mt] = p.step("pool", lambda e, mt=mt, sl=sl: e.dma_start(
                        out=wring[:, sl, :, :], in_=wqk[:, mt, :, :]),
                        inc=f"wq{sl}", n=16)
                    for ts in range(4):
                        psl = (mt * 4 + ts) % 2
                        p.waits("pe", [wld[mt], lnld[KC - 1], evac.get((mt, ts - 2) if ts >= 2 else (mt - 1, ts + 2))])
                        for kc in range(KC):
                            tok = p.step("pe", lambda e, mt=mt, ts=ts, kc=kc, sl=sl, psl=psl: e.matmul(
                                ps_qk[:, psl, :], wring[:, sl, kc, :],
                                ln_all[:, kc, ts * 512:(ts + 1) * 512],
                                start=(kc == 0), stop=(kc == KC - 1)),
                                inc="pe" if kc == KC - 1 else None)
                        dst = q_fm if mt < 8 else k_fm
                        hh = mt % 8
                        p.wait("act", tok)
                        evac[(mt, ts)] = p.step("act", lambda e, dst=dst, hh=hh, ts=ts, psl=psl: e.copy(
                            dst[:, hh, ts * 512:(ts + 1) * 512], ps_qk[:, psl, :]),
                            inc="act")
                p.fence()

            with (contextlib.ExitStack() as S2b,
                  nc.psum_tensor(f"ps_v{idx}", [128, 2, 512], F32) as ps_v):
                wvs = p.sb(S2b, f"wvs{idx}", [128, KC, 512], BF16)
                vevac = {}
                vmm = {}
                for ng in range(2):
                    p.waits("pool", [vmm.get((ng - 1, KC - 1))])
                    t_wv = p.step("pool", lambda e, ng=ng: e.dma_start(
                        out=wvs[:, :, :], in_=wv[:, :, ng * 512:(ng + 1) * 512]),
                        inc="wvv", n=16)
                    for tt in range(KC):
                        psl = tt % 2
                        p.waits("pe", [t_wv, vevac.get((ng, tt - 2) if tt >= 2 else (ng - 1, tt + 14))])
                        for kc in range(KC):
                            tok = p.step("pe", lambda e, tt=tt, kc=kc, psl=psl: e.matmul(
                                ps_v[:, psl, :], ln_all[:, kc, tt * 128:(tt + 1) * 128],
                                wvs[:, kc, :],
                                start=(kc == 0), stop=(kc == KC - 1)),
                                inc="pe" if kc == KC - 1 else None)
                        vmm[(ng, tt)] = tok
                        p.waits("dve", [tok, t_vones])
                        vevac[(ng, tt)] = p.step("dve", lambda e, tt=tt, ng=ng, psl=psl: e.tensor_copy(
                            v_ext[:, tt, 4 * ng:4 * ng + 4, 0:128],
                            ps_v[:, psl, :].rearrange("p (h m) -> p h m", m=128)),
                            inc="dve")
                p.fence()

            S2.close()
            with contextlib.ExitStack() as S3:
                if variant == "a":
                    qsw = p.sb(S3, f"qsw{idx}", [128, MYH, T], BF16)
                    for tgt in (q_fm, k_fm):
                        s1 = p.step("sync", lambda e, tgt=tgt: e.dma_start(
                            out=qsw[0:64, :, :], in_=tgt[64:128, :, :]), inc="rsw", n=16)
                        s2 = p.step("sync", lambda e, tgt=tgt: e.dma_start(
                            out=qsw[64:128, :, :], in_=tgt[0:64, :, :]), inc="rsw", n=16)
                        p.waits("dve", [s1, s2, t_cos, t_sin])
                        cosb = cos_sb[:, :].unsqueeze(1).broadcast_to([128, MYH, T])
                        sinb = sin_sb[:, :].unsqueeze(1).broadcast_to([128, MYH, T])
                        p.step("dve", lambda e, tgt=tgt, c=cosb: e.tensor_tensor(
                            tgt[:, :, :], tgt[:, :, :], c, ALU.mult), inc="dve")
                        p.step("dve", lambda e, s=sinb: e.tensor_tensor(
                            qsw[:, :, :], qsw[:, :, :], s, ALU.mult), inc="dve")
                        tok = p.step("dve", lambda e, tgt=tgt: e.tensor_tensor(
                            tgt[:, :, :], tgt[:, :, :], qsw[:, :, :], ALU.add), inc="dve")
                        p.wait("sync", tok)
                    p.fence()

            av_all = p.sb(S, f"av{idx}", [128, MYH, T], BF16)
            p_sb = p.sb(S, f"pp{idx}", [128, 2, 1024], BF16)
            sinv = p.sb(S, f"sinv{idx}", [128, 512], F32)
            brow = p.sb(S, f"brow{idx}", [1, 1024], BF16)
            sbg = p.sb(S, f"sbg{idx}", [128, 2, 512], BF16)

            with (nc.psum_tensor(f"ps_s{idx}", [128, 2, 1024], F32) as ps_s,
                  nc.psum_tensor(f"ps_av{idx}", [128, 2, 512], F32) as ps_av,
                  nc.psum_tensor(f"ps_sum{idx}", [128, 512], F32) as ps_sum,
                  nc.psum_tensor(f"ps_sb{idx}", [128, 512], F32) as ps_sb):
                exp_tok = {}
                af_tok = {}
                av_tok = {}
                avn_tok = {}
                last_sb = [None]
                last_sg = []
                last_sinv = []
                it = 0
                for slab in range(4):
                    nk = 4 * (slab + 1)
                    for hg in range(4):
                        h0 = 2 * hg
                        for kt in range(nk):
                            sl = it % 2
                            p.waits("pe", [exp_tok.get(it - 2), avn_tok.get((slab, hg - 1) if hg else (slab - 1, 3))])
                            for hh in range(2):
                                tok = p.step("pe", lambda e, sl=sl, hh=hh, h0=h0, kt=kt, slab=slab: e.matmul(
                                    ps_s[:, sl, hh * 512:(hh + 1) * 512],
                                    k_fm[:, h0 + hh, kt * 128:(kt + 1) * 128],
                                    q_fm[:, h0 + hh, slab * 512:(slab + 1) * 512],
                                    start=True, stop=True), inc="pe")
                            p.waits("act", [tok, av_tok.get(it - 2)])
                            exp_tok[it] = p.step("act", lambda e, sl=sl: e.activation(
                                p_sb[:, sl, :], ps_s[:, sl, :], AF.Exp,
                                bias=0.0, scale=SCALE), inc="act")
                            dep = exp_tok[it]
                            if kt >= nk - 4:
                                jrel = 512 * slab - 128 * kt
                                p.wait("pool", exp_tok[it])
                                af_tok[it] = p.step("pool", lambda e, sl=sl, jrel=jrel: e.affine_select(
                                    p_sb[:, sl, :], p_sb[:, sl, :],
                                    [[0, 2], [1, 512]], ALU.is_ge, 0.0,
                                    base=jrel, channel_multiplier=-1), inc="pool")
                                dep = af_tok[it]
                            p.waits("pe", [dep])
                            for hh in range(2):
                                p.step("pe", lambda e, sl=sl, hh=hh, h0=h0, kt=kt: e.matmul(
                                    ps_av[:, hh, :], v_ext[:, kt, h0 + hh, 0:128],
                                    p_sb[:, sl, hh * 512:(hh + 1) * 512],
                                    start=(kt == 0), stop=(kt == nk - 1)))
                                tp = (0, 32 * hh) if hh else None
                                av_tok[it] = p.step("pe", lambda e, sl=sl, hh=hh, h0=h0, kt=kt, tp=tp: e.matmul(
                                    ps_sum[32 * hh:32 * hh + 1, :],
                                    v_ext[:, kt, h0 + hh, 128:129],
                                    p_sb[:, sl, hh * 512:(hh + 1) * 512],
                                    start=(kt == 0), stop=(kt == nk - 1),
                                    tile_position=tp), inc="pe")
                            it += 1
                        tlast = av_tok[it - 1]
                        p.waits("act", [tlast, last_sb[0]] + last_sinv)
                        ac0 = p.step("act", lambda e: e.copy(
                            sinv[0:1, :], ps_sum[0:1, :]), inc="act")
                        ac1 = p.step("act", lambda e: e.copy(
                            sinv[32:33, :], ps_sum[32:33, :]), inc="act")
                        p.waits("dve", [ac0, ac1])
                        dr0 = p.step("dve", lambda e: e.reciprocal(
                            sinv[0:1, :], sinv[0:1, :]), inc="dve")
                        dr1 = p.step("dve", lambda e: e.reciprocal(
                            sinv[32:33, :], sinv[32:33, :]), inc="dve")
                        p.waits("pool", [dr0, dr1] + last_sg)
                        sg0 = p.step("pool", lambda e: e.dma_start(
                            out=brow[0:1, 0:512], in_=sinv[0:1, :]), inc="sgg", n=16)
                        sg = p.step("pool", lambda e: e.dma_start(
                            out=brow[0:1, 512:1024], in_=sinv[32:33, :]), inc="sgg", n=16)
                        last_sg = []
                        last_sinv = [sg0, sg]
                        for hh in range(2):
                            head = h0 + hh
                            p.waits("pe", [sg0, sg, t_ones, last_sb[0]])
                            bc = p.step("pe", lambda e, hh=hh: e.matmul(
                                ps_sb[:, :], ones[0:1, :], brow[0:1, hh * 512:(hh + 1) * 512],
                                start=True, stop=True), inc="pe")
                            last_sg.append(bc)
                            p.wait("act", bc)
                            sb1 = p.step("act", lambda e: e.copy(
                                sbg[:, 0, :], ps_sb[:, :]), inc="act")
                            last_sb[0] = sb1
                            if variant == "b":
                                p.waits("pe", [sb1, t_gwt])
                                gm = p.step("pe", lambda e, head=head, slab=slab: e.matmul(
                                    ps_sb[:, :], gwt_sb[:, :],
                                    q_fm[:, head, slab * 512:(slab + 1) * 512],
                                    start=True, stop=True), inc="pe")
                                p.waits("act", [gm, t_gb])
                                sb2 = p.step("act", lambda e: e.activation(
                                    sbg[:, 1, :], ps_sb[:, :], AF.Sigmoid,
                                    bias=gb_sb[:, :], scale=1.0), inc="act")
                                last_sb[0] = sb2
                            p.waits("dve", [sb1, tlast])
                            dn = p.step("dve", lambda e, head=head, slab=slab, hh=hh: e.tensor_tensor(
                                av_all[:, head, slab * 512:(slab + 1) * 512],
                                ps_av[:, hh, :], sbg[:, 0, :], ALU.mult), inc="dve")
                            if variant == "b":
                                p.waits("dve", [sb2])
                                dn = p.step("dve", lambda e, head=head, slab=slab: e.tensor_tensor(
                                    av_all[:, head, slab * 512:(slab + 1) * 512],
                                    av_all[:, head, slab * 512:(slab + 1) * 512],
                                    sbg[:, 1, :], ALU.mult), inc="dve")
                            avn_tok[(slab, hg)] = dn
                p.fence()

            ostg = p.sb(S, f"ostg{idx}", [128, 2, 512], F32)
            wro = p.sb(S, f"wro{idx}", [128, 2, 8, 128], BF16)
            with nc.psum_tensor(f"ps_o{idx}", [128, 2, 512], F32) as ps_o:
                wld = [None] * 16
                odma = {}
                for mt in range(16):
                    sl = mt % 2
                    p.waits("pool", [odma.get((mt - 2, 3))])
                    wld[mt] = p.step("pool", lambda e, mt=mt, sl=sl: e.dma_start(
                        out=wro[:, sl, 0:8, :], in_=wo[:, mt, :, :]),
                        inc=f"wo{sl}", n=16)
                    for slab in range(4):
                        psl = (mt * 4 + slab) % 2
                        p.waits("pe", [wld[mt], odma.get((mt, slab - 2) if slab >= 2 else (mt - 1, slab + 2))])
                        for hc in range(8):
                            tok = p.step("pe", lambda e, mt=mt, slab=slab, hc=hc, sl=sl, psl=psl: e.matmul(
                                ps_o[:, psl, :], wro[:, sl, hc, :],
                                av_all[:, hc, slab * 512:(slab + 1) * 512],
                                start=(hc == 0), stop=(hc == 7)),
                                inc="pe" if hc == 7 else None)
                        p.waits("act", [tok, odma.get((mt, slab - 2) if slab >= 2 else (mt - 1, slab + 2))])
                        ev = p.step("act", lambda e, psl=psl: e.copy(
                            ostg[:, psl, :], ps_o[:, psl, :]), inc="act")
                        p.wait("sync", ev)
                        odma[(mt, slab)] = p.step("sync", lambda e, mt=mt, slab=slab, psl=psl: e.dma_start(
                            out=pmy[mt * 128:(mt + 1) * 128, slab * 512:(slab + 1) * 512],
                            in_=ostg[:, psl, :]), inc=f"odm{psl}", n=16)
            p.waits("act", [p.now("odm0"), p.now("odm1")])
            t_pw = cdma("act", lambda e, h, c: e.dma_start(
                out=part[h * (KC * 128):(h + 1) * (KC * 128), :],
                in_=pmy[:, (1 - h) * TOWN:(1 - h) * TOWN + TOWN], cond=c),
                inc="tpw")
            p.waits("pool", [p.now("odm0"), p.now("odm1")])
            t_pmx = cdma("pool", lambda e, h, c: e.dma_start(
                out=pmd[:, :], in_=pmy[:, h * TOWN:h * TOWN + TOWN], cond=c),
                inc="pxx")
            t_bar2 = barrier([t_pw, t_pmx])
            p.wait("pool", t_bar2)
            t_ppx = cdma("pool", lambda e, h, c: e.dma_start(
                out=ppd[:, :], in_=part[(1 - h) * (KC * 128):(2 - h) * (KC * 128), :],
                cond=c), inc="pxx")
            p.fence()

            prr = p.sb(S, f"prr{idx}", [128, 4, TOWN], F32)
            p.wait("act", t_bar2)
            p.wait("sync", t_bar2)
            rtok = {}
            for kc in range(KC):
                sl = kc % 2
                p.waits("sync", [rtok.get(kc - 2)])
                l1 = p.step("sync", lambda e, kc=kc, sl=sl: e.dma_start(
                    out=xst[:, sl, :], in_=x_in[kc]), inc=f"xld{sl}", n=16)
                p.waits("act", [rtok.get(kc - 2)])
                l2 = p.step("act", lambda e, kc=kc, sl=sl: e.dma_start(
                    out=prr[:, 2 * sl, :],
                    in_=pmd[kc * 128:(kc + 1) * 128, :]), inc=f"prr{sl}", n=16)
                l3 = p.step("act", lambda e, kc=kc, sl=sl: e.dma_start(
                    out=prr[:, 2 * sl + 1, :],
                    in_=ppd[kc * 128:(kc + 1) * 128, :]), inc=f"prr{sl}", n=16)
                p.waits("dve", [l1, l2, l3])
                p.step("dve", lambda e, sl=sl: e.tensor_tensor(
                    prr[:, 2 * sl, :], prr[:, 2 * sl, :], prr[:, 2 * sl + 1, :],
                    ALU.add), inc="dve")
                dtok = p.step("dve", lambda e, sl=sl: e.tensor_tensor(
                    xst[:, sl, :], xst[:, sl, :], prr[:, 2 * sl, :], ALU.add),
                    inc="dve")
                p.wait("sync", dtok)
                rtok[kc] = p.step("sync", lambda e, kc=kc, sl=sl: e.dma_start(
                    out=x_out[kc], in_=xst[:, sl, :]), inc=f"xso{sl}", n=16)
            p.fence()
            if tap and idx == 0:
                p.step("sync", lambda e: e.dma_start(out=dram["__tav"], in_=av_all[:, :, :]), inc="tap", n=16)
                p.step("sync", lambda e: e.dma_start(out=dram["__tpmy"], in_=pmy), inc="tap", n=16)
                p.step("sync", lambda e: e.dma_start(out=dram["__tpart"], in_=part), inc="tap", n=16)
                p.step("sync", lambda e: e.dma_start(out=dram["__tpmd"], in_=pmd), inc="tap", n=16)
                p.step("sync", lambda e: e.dma_start(out=dram["__tppd"], in_=ppd), inc="tap", n=16)
                p.fence()

    def emit_ffn(idx, x_in, x_out):
        with contextlib.ExitStack() as S:
            ln = p.sb(S, f"fln{idx}", [128, KC, TOWN], BF16)

            S1 = S.enter_context(contextlib.ExitStack())
            x_bf = p.sb(S1, f"fxbf{idx}", [128, KC, TOWN], BF16)

            def sink(kc):
                return ln[:, kc, :], []

            def sdone(kc, tok):
                pass

            sink_done_cell[0] = sdone
            emit_ln(S1, x_in, x_bf, sink)
            t_ln = p.now("dve")
            p.fence()
            S1.close()

            acc = p.sb(S, f"facc{idx}", [128, KC, TOWN], BF16)
            hng = p.sb(S, f"fh{idx}", [128, 2, 8, 512], BF16)
            ang = p.sb(S, f"fa{idx}", [128, 2, 8, 512], BF16)
            tau = p.sb(S, f"ftau{idx}", [128, 2, 8, 128], BF16)
            tsc = p.sb(S, f"ftsc{idx}", [128, 2, 2, 8, 128], BF16)
            sgr = p.sb(S, f"fsg{idx}", [128, 8, 512], BF16)
            hfm = p.sb(S, f"fhfm{idx}", [128, 16, TOWN], BF16)
            wgu = p.sb(S, f"fwgu{idx}", [128, 2, KC, 512], BF16)
            wdr = p.sb(S, f"fwd{idx}", [128, 2, 16, 128], BF16)

            with (nc.psum_tensor(f"fps_g{idx}", [128, 2, 512], F32) as ps_g,
                  nc.psum_tensor(f"fps_u{idx}", [128, 2, 512], F32) as ps_u,
                  nc.psum_tensor(f"fps_t{idx}", [128, 2, 1024], BF16) as ps_t,
                  nc.psum_tensor(f"fps_d{idx}", [128, 2, 512], F32) as ps_d):
                wl_tok = {}
                gmm_tok = {}
                umm_tok = {}
                sa_tok = {}
                hm_tok = {}
                mask_tok = {}
                tr_evac = {}
                dn_evac = {}

                def emit_transposes(tng):
                    tngl = tng % 4
                    thsl = tng % 2
                    for tt in range(8):
                        tsl = tt % 2
                        p.waits("pe", [mask_tok[tng], t_idn,
                                       tr_evac.get((tng, tt - 2) if tt >= 2 else (tng - 1, tt + 6))])
                        for j in range(4):
                            tok = p.step("pe", lambda e, thsl=thsl, tt=tt, j=j, tsl=tsl: e.transpose(
                                ps_t[:, tsl, j * 128:(j + 1) * 128],
                                hng[:, thsl, tt, j * 128:(j + 1) * 128], idn[:, :]),
                                inc="pe" if j == 3 else None)
                        p.wait("dve", tok)
                        tr_evac[(tng, tt)] = p.step("dve", lambda e, tngl=tngl, tt=tt, tsl=tsl: e.tensor_copy(
                            hfm[:, 4 * tngl:4 * tngl + 4, tt * 128:(tt + 1) * 128],
                            ps_t[:, tsl, 0:512].rearrange("p (a b) -> p a b", b=128)),
                            inc="dve")

                for qtr in range(4):
                    for ngl in range(4):
                        ng = qtr * 4 + ngl
                        hsl = ng % 2
                        p.waits("pool", [gmm_tok.get((ng - 1, 7))])
                        wl_tok[(ng, 0)] = p.step("pool", lambda e, ng=ng: e.dma_start(
                            out=wgu[:, 0, :, :],
                            in_=wg_f[:, :, ng * 512:(ng + 1) * 512]), inc="wg0", n=16)
                        p.waits("pool", [umm_tok.get((ng - 1, 7))])
                        wl_tok[(ng, 1)] = p.step("pool", lambda e, ng=ng: e.dma_start(
                            out=wgu[:, 1, :, :],
                            in_=wu_f[:, :, ng * 512:(ng + 1) * 512]), inc="wg1", n=16)
                        for tt in range(8):
                            gsl = tt % 2
                            p.waits("pe", [wl_tok[(ng, 0)], t_ln,
                                           sa_tok.get((ng, tt - 2) if tt >= 2 else (ng - 1, tt + 6)),
                                           dn_evac.get(("last", qtr - 1)) if (ngl == 0 and tt == 0) else None])
                            for kc in range(KC):
                                tok = p.step("pe", lambda e, ng=ng, tt=tt, kc=kc, gsl=gsl: e.matmul(
                                    ps_g[:, gsl, :], ln[:, kc, tt * 128:(tt + 1) * 128],
                                    wgu[:, 0, kc, :],
                                    start=(kc == 0), stop=(kc == KC - 1)),
                                    inc="pe" if kc == KC - 1 else None)
                            gmm_tok[(ng, tt)] = tok
                            p.waits("act", [tok, hm_tok.get((ng - 1, tt))])
                            sa_tok[(ng, tt)] = p.step("act", lambda e, tt=tt, gsl=gsl: e.activation(
                                sgr[:, tt, :], ps_g[:, gsl, :], AF.Silu,
                                bias=0.0, scale=1.0), inc="act")
                        for tt in range(8):
                            gsl = tt % 2
                            p.waits("pe", [wl_tok[(ng, 1)],
                                           hm_tok.get((ng, tt - 2) if tt >= 2 else (ng - 1, tt + 6))])
                            for kc in range(KC):
                                tok = p.step("pe", lambda e, ng=ng, tt=tt, kc=kc, gsl=gsl: e.matmul(
                                    ps_u[:, gsl, :], ln[:, kc, tt * 128:(tt + 1) * 128],
                                    wgu[:, 1, kc, :],
                                    start=(kc == 0), stop=(kc == KC - 1)),
                                    inc="pe" if kc == KC - 1 else None)
                            umm_tok[(ng, tt)] = tok
                            p.waits("dve", [tok, sa_tok[(ng, tt)]])
                            hm_tok[(ng, tt)] = p.step("dve", lambda e, tt=tt, hsl=hsl, gsl=gsl: e.tensor_tensor(
                                hng[:, hsl, tt, :], sgr[:, tt, :], ps_u[:, gsl, :],
                                ALU.mult), inc="dve")
                        if ngl > 0:
                            emit_transposes(ng - 1)
                        heng = "dve"
                        esl = 0 if heng == "dve" else 1
                        hfl = hng[:, hsl, :, :]
                        afl = ang[:, hsl, :, :]
                        a4 = afl.rearrange("p t (g j) -> p t g j", j=4)
                        tq = tau[:, esl, :, :]
                        s1 = tsc[:, esl, 0, :, :]
                        s2 = tsc[:, esl, 1, :, :]
                        p.waits("act", [hm_tok[(ng, 7)], mask_tok.get(ng - 2)])
                        aa = p.step("act", lambda e, hfl=hfl, afl=afl: e.activation(
                            afl, hfl, AF.Abs, bias=0.0, scale=1.0), inc="act")
                        E = heng
                        p.waits(E, [aa])
                        p.step(E, lambda e, a4=a4, s1=s1: e.tensor_tensor(s1, a4[:, :, :, 0], a4[:, :, :, 1], ALU.max), inc=E)
                        p.step(E, lambda e, a4=a4, s2=s2: e.tensor_tensor(s2, a4[:, :, :, 2], a4[:, :, :, 3], ALU.max), inc=E)
                        p.step(E, lambda e, s1=s1, s2=s2, tq=tq: e.tensor_tensor(tq, s1, s2, ALU.min), inc=E)
                        p.step(E, lambda e, a4=a4, s1=s1: e.tensor_tensor(s1, a4[:, :, :, 0], a4[:, :, :, 1], ALU.min), inc=E)
                        p.step(E, lambda e, a4=a4, s2=s2: e.tensor_tensor(s2, a4[:, :, :, 2], a4[:, :, :, 3], ALU.min), inc=E)
                        p.step(E, lambda e, s1=s1, s2=s2: e.tensor_tensor(s1, s1, s2, ALU.max), inc=E)
                        p.step(E, lambda e, s1=s1, tq=tq: e.tensor_tensor(tq, tq, s1, ALU.max), inc=E)
                        taub = tq.unsqueeze(3).broadcast_to([128, 8, 128, 4])
                        p.step(E, lambda e, a4=a4, taub=taub: e.tensor_tensor(a4, a4, taub, ALU.is_ge), inc=E)
                        mask_tok[ng] = p.step(E, lambda e, hfl=hfl, afl=afl: e.tensor_tensor(
                            hfl, hfl, afl, ALU.mult), inc=E)
                    emit_transposes(qtr * 4 + 3)
                    wd_tok = {}
                    for mt in range(16):
                        wsl = mt % 2
                        wprev = (mt - 2, 1) if mt >= 2 else ((mt + 14, 1) if qtr else None)
                        p.waits("pool", [dn_evac.get(wprev) if wprev else None])
                        wd_tok[mt] = p.step("pool", lambda e, qtr=qtr, mt=mt, wsl=wsl: e.dma_start(
                            out=wdr[:, wsl, :, :], in_=wd_f[:, mt, qtr * 16:(qtr + 1) * 16, :]),
                            inc=f"wd{wsl}", n=16)
                        for slab in range(2):
                            psl = (mt * 2 + slab) % 2
                            dprev = (mt - 1, slab) if mt else ((15, slab) if qtr else None)
                            p.waits("pe", [wd_tok[mt], tr_evac[(qtr * 4 + 3, 7)],
                                           dn_evac.get(dprev) if dprev else None])
                            for kq in range(16):
                                tok = p.step("pe", lambda e, mt=mt, slab=slab, kq=kq, wsl=wsl, psl=psl: e.matmul(
                                    ps_d[:, psl, :], wdr[:, wsl, kq, :],
                                    hfm[:, kq, slab * 512:(slab + 1) * 512],
                                    start=(kq == 0), stop=(kq == 15)),
                                    inc="pe" if kq == 15 else None)
                            p.waits("dve", [tok])
                            if qtr == 0:
                                dn_evac[(mt, slab)] = p.step("dve", lambda e, mt=mt, slab=slab, psl=psl: e.tensor_copy(
                                    acc[:, mt, slab * 512:(slab + 1) * 512], ps_d[:, psl, :]),
                                    inc="dve")
                            else:
                                dn_evac[(mt, slab)] = p.step("dve", lambda e, mt=mt, slab=slab, psl=psl: e.tensor_tensor(
                                    acc[:, mt, slab * 512:(slab + 1) * 512],
                                    acc[:, mt, slab * 512:(slab + 1) * 512],
                                    ps_d[:, psl, :], ALU.add), inc="dve")
                    dn_evac[("last", qtr)] = dn_evac[(15, 1)]
                p.fence()

            rtok = {}
            for kc in range(KC):
                sl = kc % 2
                p.waits("sync", [rtok.get(kc - 2)])
                l1 = p.step("sync", lambda e, kc=kc, sl=sl: e.dma_start(
                    out=xst[:, sl, :], in_=x_in[kc]), inc=f"xld{sl}", n=16)
                p.waits("dve", [l1])
                dtok = p.step("dve", lambda e, kc=kc, sl=sl: e.tensor_tensor(
                    xst[:, sl, :], xst[:, sl, :], acc[:, kc, :], ALU.add), inc="dve")
                p.wait("sync", dtok)
                rtok[kc] = p.step("sync", lambda e, kc=kc, sl=sl: e.dma_start(
                    out=x_out[kc], in_=xst[:, sl, :]), inc=f"xso{sl}", n=16)
            p.fence()

    subs = [("attn", "a"), ("ffn", None), ("attn", "b"), ("ffn", None)]
    for i in range(n_sub):
        x_in, x_out = chain[i]
        kind, var = subs[i]
        if kind == "attn":
            emit_attn(i, var, _ChunkWriter(x_in), _ChunkWriter(x_out))
        else:
            emit_ffn(i, _ChunkWriter(x_in), _ChunkWriter(x_out))

    p.fence()

    for _s in p.sems.values():
        nc.gpsimd.sem_clear(_s)
    nc.all_engine_barrier()

    with nc.Block() as block:
        @block.tensor
        def _(t):
            for fn in p.ev["pe"]:
                fn(t)

        @block.scalar
        def _(s):
            for fn in p.ev["act"]:
                fn(s)

        @block.vector
        def _(v):
            for fn in p.ev["dve"]:
                fn(v)

        @block.gpsimd
        def _(g):
            for fn in p.ev["pool"]:
                fn(g)

        @block.sync
        def _(sy):
            for fn in p.ev["sync"]:
                fn(sy)

    p.st.close()
    return nc


class _ChunkWriter:
    def __init__(self, x):
        self.x = x

    def __call__(self, kc):
        return self.x[kc, :, :]

    def __getitem__(self, kc):
        return self.x[kc, :, :]



_cached = {}


def _get_program(n_sub=N_SUB):
    if n_sub not in _cached:
        _cached[n_sub] = build_program(n_sub)
    return _cached[n_sub]


def make_in_maps(inputs):
    w_qkv = inputs["L0_attn_a_w_qkv"]
    w_o = inputs["L0_attn_a_w_o"]
    q_w = inputs["L0_attn_b_q_w"]
    k_w = inputs["L0_attn_b_k_w"]
    v_w = inputs["L0_attn_b_v_w"]
    o_w = inputs["L0_attn_b_o_w"]
    gate_w = inputs["L0_attn_b_gate_w"]
    gate_b = inputs["L0_attn_b_gate_b"]
    w_gate = inputs["L0_ffn_w_gate"]
    w_up = inputs["L0_ffn_w_up"]
    w_down = inputs["L0_ffn_w_down"]
    x = inputs["L0_attn_a_x"]

    cosf, sinS = _rope_tables()
    ident = np.eye(128, dtype=np.float32)
    shared = {
        "wgt": np.ascontiguousarray(gate_w.T),
        "gbi": np.ascontiguousarray(gate_b.reshape(128, 1)),
        "wg_f": _b_layout(w_gate),
        "wu_f": _b_layout(w_up),
        "wd_f": _a_layout(w_down),
        "cosf": cosf,
        "sinSf": sinS,
        "identf": ident,
    }
    qT, kT, vT, oT = q_w.T, k_w.T, v_w.T, o_w.T
    maps = []
    for c in range(8):
        b, h = c // 2, c % 2
        cs = slice(h * 1024, (h + 1) * 1024)
        m = dict(shared)
        m["wqk_a"] = _a_layout(np.concatenate(
            [w_qkv[:, h * 1024:(h + 1) * 1024],
             w_qkv[:, 2048 + h * 1024:2048 + (h + 1) * 1024]], axis=1))
        m["wv_a"] = _b_layout(w_qkv[:, 4096 + h * 1024:4096 + (h + 1) * 1024])
        m["wo_a"] = _a_layout(w_o[h * 1024:(h + 1) * 1024, :])
        m["wqk_b"] = _a_layout(np.concatenate([qT[:, cs], kT[:, cs]], axis=1))
        m["wv_b"] = _b_layout(vT[:, cs])
        m["wo_b"] = _a_layout(oT[h * 1024:(h + 1) * 1024, :])
        m["x0"] = _fm(x[b, h * 1024:(h + 1) * 1024, :])
        maps.append(m)
    return maps


def kernel(**inputs):
    nc = _get_program(N_SUB)
    maps = make_in_maps(inputs)
    res = run_bass_kernel_spmd(nc, maps, core_ids=list(range(8)))
    out = np.empty((B, T, D), dtype=np.float32)
    for c in range(8):
        b, h = c // 2, c % 2
        yc = np.asarray(res.results[c]["y"])
        out[b, h * 1024:(h + 1) * 1024, :] = yc.transpose(2, 0, 1).reshape(TOWN, D)
    return out


if __name__ == "__main__":
    nc = build_program(1)
    print("build ok")


# version 04
# speedup vs baseline: 1.0835x; 1.0810x over previous
import math
import numpy as np
import ml_dtypes
import contextlib
import concourse.bass as bass
import concourse.mybir as mybir
from concourse.bass import ds
from concourse.bass_utils import run_bass_kernel_spmd

F32 = mybir.dt.float32
BF16 = mybir.dt.bfloat16
AF = mybir.ActivationFunctionType
ALU = mybir.AluOpType
BFNP = ml_dtypes.bfloat16

D = 2048
NH = 16
HD = 128
T = 2048
B = 4
II = 8192
MYH = 8
TOWN = 1024
KC = 16
ICH = 64
EPS = 1e-5
SCALE = 1.0 / math.sqrt(HD)

N_SUB = 4

ENGS = ("pe", "act", "dve", "pool", "sync")



def _a_layout(w):
    din, dout = w.shape
    return np.ascontiguousarray(
        w.reshape(din // 128, 128, dout // 128, 128).transpose(1, 2, 0, 3))


def _b_layout(w):
    din, dout = w.shape
    return np.ascontiguousarray(w.reshape(din // 128, 128, dout).transpose(1, 0, 2))


def _fm(x):
    t, d = x.shape
    return np.ascontiguousarray(x.T.reshape(d // 128, 128, t))


def _rope_tables():
    inv = 1.0 / 10000.0 ** (np.arange(0, HD, 2, dtype=np.float32) / HD)
    t = np.arange(T, dtype=np.float32)
    fr = np.outer(t, inv)
    c = np.cos(fr).T
    s = np.sin(fr).T
    cosf = np.concatenate([c, c], 0).astype(np.float32)
    sinS = np.concatenate([-s, s], 0).astype(np.float32)
    return cosf, sinS



class Pb:

    def __init__(self, nc):
        self.nc = nc
        self.ev = {e: [] for e in ENGS}
        self.cnt = {}
        self.sems = {}
        self.st = contextlib.ExitStack()

    def sem(self, name):
        if name not in self.sems:
            self.sems[name] = self.st.enter_context(self.nc.semaphore(name))
            self.cnt[name] = 0
        return self.sems[name]

    def on(self, eng, fn):
        self.ev[eng].append(fn)

    def wait(self, eng, token):
        if token is None:
            return
        sname, val = token
        sem = self.sem(sname)
        self.on(eng, lambda e, s=sem, v=val: e.wait_ge(s, v))

    def waits(self, eng, tokens):
        best = {}
        for t in tokens:
            if t is None:
                continue
            s, v = t
            best[s] = max(best.get(s, 0), v)
        for s, v in best.items():
            self.wait(eng, (s, v))

    def step(self, eng, build, inc=None, n=1):
        tok = None
        if inc is not None:
            sem = self.sem(inc)
            self.cnt[inc] += n
            tok = (inc, self.cnt[inc])
            def fn(e, b=build, s=sem, nn=n):
                b(e).then_inc(s, nn)
        else:
            def fn(e, b=build):
                b(e)
        self.on(eng, fn)
        return tok

    def now(self, sem):
        self.sem(sem)
        return (sem, self.cnt[sem])

    def fence(self):
        toks = [self.now(s) for s in list(self.sems)]
        for e in ENGS:
            self.waits(e, toks)

    def sb(self, scope, name, shape, dtype):
        return scope.enter_context(self.nc.sbuf_tensor(name, shape, dtype))

    def ps(self, scope, name, shape, dtype=F32):
        return scope.enter_context(self.nc.psum_tensor(name, shape, dtype))


def build_program(n_sub=N_SUB, tap=None):
    nc = bass.Bass("TRN2", target_bir_lowering=False)
    p = Pb(nc)

    dram = {}
    def din(name, shape, dtype=F32):
        dram[name] = nc.dram_tensor(name, shape, dtype, kind="ExternalInput").ap()
        return dram[name]

    wqk_a = din("wqk_a", [128, 16, 16, 128])
    wv_a = din("wv_a", [128, 16, 1024])
    wo_a = din("wo_a", [128, 16, 8, 128])
    wqk_b = din("wqk_b", [128, 16, 16, 128])
    wv_b = din("wv_b", [128, 16, 1024])
    wo_b = din("wo_b", [128, 16, 8, 128])
    wgt = din("wgt", [128, 128])
    gbi = din("gbi", [128, 1])
    wg_f = din("wg_f", [128, 16, II])
    wu_f = din("wu_f", [128, 16, II])
    wd_f = din("wd_f", [128, 16, 64, 128])
    x0 = din("x0", [KC, 128, TOWN])
    cosf = din("cosf", [128, T])
    sinSf = din("sinSf", [128, T])
    identf = din("identf", [128, 128])

    y = nc.dram_tensor("y", [KC, 128, TOWN], F32, kind="ExternalOutput").ap()
    if tap:
        dram["__tav"] = nc.dram_tensor("t_av", [128, MYH, T], BF16, kind="ExternalOutput").ap()
        dram["__tpmy"] = nc.dram_tensor("t_pmy", [KC * 128, T], F32, kind="ExternalOutput").ap()
        dram["__tpart"] = nc.dram_tensor("t_part", [2 * KC * 128, TOWN], F32, kind="ExternalOutput").ap()
        dram["__tpmd"] = nc.dram_tensor("t_pmd", [KC * 128, TOWN], F32, kind="ExternalOutput").ap()
        dram["__tppd"] = nc.dram_tensor("t_ppd", [KC * 128, TOWN], F32, kind="ExternalOutput").ap()

    xs = [x0] + [nc.dram_tensor(f"x{i}", [KC, 128, TOWN], F32, kind="Internal").ap()
                 for i in (1, 2, 3)]
    lnsh = [nc.dram_tensor(f"lnsh{i}", [KC * 128, T], BF16, kind="Internal",
                           addr_space="Shared").ap() for i in (0, 1)]
    partial = [nc.dram_tensor(f"part{i}", [2 * KC * 128, TOWN], BF16, kind="Internal",
                              addr_space="Shared").ap() for i in (0, 1)]
    lnmy = [nc.dram_tensor(f"lnmy{i}", [KC * 128, TOWN], BF16, kind="Internal").ap()
            for i in (0, 1)]
    part_my = [nc.dram_tensor(f"pmy{i}", [KC * 128, T], BF16, kind="Internal").ap()
               for i in (0, 1)]
    pm_dram = [nc.dram_tensor(f"pmd{i}", [KC * 128, TOWN], BF16, kind="Internal").ap()
               for i in (0, 1)]
    pp_dram = [nc.dram_tensor(f"ppd{i}", [KC * 128, TOWN], BF16, kind="Internal").ap()
               for i in (0, 1)]
    bar_in = nc.dram_tensor("bar_in", [1, 4], F32, kind="Internal").ap()
    bar_out = nc.dram_tensor("bar_out", [1, 4], F32, kind="Internal").ap()

    chain = [(xs[0], xs[1]), (xs[1], xs[2]), (xs[2], xs[3]), (xs[3], y)]
    chain[n_sub - 1] = (chain[n_sub - 1][0], y)

    top = p.st

    cos_sb = p.sb(top, "cos_sb", [128, T], BF16)
    sin_sb = p.sb(top, "sin_sb", [128, T], BF16)
    idn = p.sb(top, "idn", [128, 128], BF16)
    ones = p.sb(top, "ones", [128, 128], BF16)
    gwt_sb = p.sb(top, "gwt_sb", [128, 128], BF16)
    gb_sb = p.sb(top, "gb_sb", [128, 1], F32)
    xst = p.sb(top, "xst", [128, 2, TOWN], F32)
    barsb = p.sb(top, "barsb", [1, 4], F32)
    eps_sb = p.sb(top, "eps_sb", [128, 1], F32)

    t_cos = p.step("pool", lambda e: e.dma_start(out=cos_sb[:, :], in_=cosf), inc="cst", n=16)
    t_sin = p.step("pool", lambda e: e.dma_start(out=sin_sb[:, :], in_=sinSf), inc="cst", n=16)
    t_idn = p.step("pool", lambda e: e.dma_start(out=idn[:, :], in_=identf), inc="cst", n=16)
    t_gwt = p.step("pool", lambda e: e.dma_start(out=gwt_sb[:, :], in_=wgt), inc="cst", n=16)
    t_gb = p.step("sync", lambda e: e.dma_start(out=gb_sb[:, :], in_=gbi), inc="cst2", n=16)
    t_ones = p.step("dve", lambda e: e.memset(ones[:, :], 1.0), inc="dve")
    t_eps = p.step("dve", lambda e: e.memset(eps_sb[:, :], EPS), inc="dve")
    t_bar0 = p.step("dve", lambda e: e.memset(barsb[:, :], 1.0), inc="dve")
    p.wait("sync", t_bar0)
    t_barin = p.step("sync", lambda e: e.dma_start(out=bar_in, in_=barsb[:, :]), inc="cst2", n=16)

    creg = {}
    def conds(eng):
        if eng not in creg:
            box = {}
            def mk(e, bx=box):
                hv = e.snap(e.partition_id() % 2)
                bx[1] = hv
                bx[0] = e.snap((hv + 1) % 2)
            p.on(eng, mk)
            creg[eng] = box
        return creg[eng]

    for e in ("sync", "act", "pool"):
        conds(e)

    def cdma(eng, mkfn, inc):
        p.step(eng, lambda e, m=mkfn: m(e, 0, conds(eng)[0]), inc=inc, n=16)
        return p.step(eng, lambda e, m=mkfn: m(e, 1, conds(eng)[1]), inc=inc, n=16)

    nbar = [0]

    def barrier(wait_tokens):
        p.waits("pool", wait_tokens + [t_barin])
        tok = p.step(
            "pool",
            lambda e: e.collective_compute(
                "AllReduce", ALU.add, [[0, 1], [2, 3], [4, 5], [6, 7]],
                [bar_in], [bar_out]),
            inc="bar")
        nbar[0] += 1
        return tok

    ln_ctr = [0]

    def emit_ln(scope, x_in, x_bf, sink):
        ln_ctr[0] += 1
        u = str(ln_ctr[0])
        st_su = p.sb(scope, "st_su" + u, [1, TOWN], F32)
        st_sq = p.sb(scope, "st_sq" + u, [1, TOWN], F32)
        st_mu = p.sb(scope, "st_mu" + u, [1, TOWN], F32)
        st_t = p.sb(scope, "st_t" + u, [1, TOWN], F32)
        st_rb = p.sb(scope, "st_rb" + u, [1, TOWN], BF16)
        st_m2 = p.sb(scope, "st_m2" + u, [1, TOWN], BF16)
        R_sb = p.sb(scope, "R_sb" + u, [128, TOWN], BF16)
        M2_sb = p.sb(scope, "M2_sb" + u, [128, TOWN], BF16)
        x2r = p.sb(scope, "x2r" + u, [128, 2, 2, TOWN], BF16)
        with (nc.psum_tensor("ps_su" + u, [1, TOWN], F32) as ps_su,
              nc.psum_tensor("ps_sq" + u, [1, TOWN], F32) as ps_sq,
              nc.psum_tensor("ps_bc" + u, [128, TOWN], F32) as ps_bc,
              nc.psum_tensor("ps_bc2" + u, [128, TOWN], F32) as ps_bc2):
            ld = [None] * 4
            sqs = [None] * 8
            pes = [None] * 8
            for g in range(4):
                ld[g] = p.step("pool", lambda e, g=g: e.dma_start(
                    out=x_bf[:, 4 * g:4 * g + 4, :],
                    in_=x_in.group4(g)), inc=f"xbf{g}", n=16)
                for hf in range(2):
                    i2 = 2 * g + hf
                    sl = i2 % 2
                    p.waits("dve", [ld[g]] + ([pes[i2 - 2]] if i2 >= 2 else []))
                    sqs[i2] = p.step("dve", lambda e, sl=sl, g=g, hf=hf: e.tensor_tensor(
                        x2r[:, sl, :, :], x_bf[:, 4 * g + 2 * hf:4 * g + 2 * hf + 2, :],
                        x_bf[:, 4 * g + 2 * hf:4 * g + 2 * hf + 2, :], ALU.mult), inc="dve")
                    p.waits("pe", [sqs[i2], t_ones])
                    for kk in range(2):
                        kc = 4 * g + 2 * hf + kk
                        for ts in range(2):
                            p.step("pe", lambda e, kc=kc, ts=ts: e.matmul(
                                ps_su[0:1, ts * 512:(ts + 1) * 512], ones[:, 0:1],
                                x_bf[:, kc, ts * 512:(ts + 1) * 512],
                                start=(kc == 0), stop=(kc == KC - 1)))
                            pes[i2] = p.step("pe", lambda e, kc=kc, ts=ts, sl=sl, kk=kk: e.matmul(
                                ps_sq[0:1, ts * 512:(ts + 1) * 512], ones[:, 0:1],
                                x2r[:, sl, kk, ts * 512:(ts + 1) * 512],
                                start=(kc == 0), stop=(kc == KC - 1)), inc="pe")
            t_stats = pes[7]
            p.wait("act", t_stats)
            a1 = p.step("act", lambda e: e.copy(st_su[:, :], ps_su[:, :]), inc="act")
            a2 = p.step("act", lambda e: e.copy(st_sq[:, :], ps_sq[:, :]), inc="act")
            p.waits("dve", [a1, a2])
            d1 = p.step("dve", lambda e: e.tensor_scalar_mul(st_mu[:, :], st_su[:, :], 1.0 / D), inc="dve")
            d2 = p.step("dve", lambda e: e.tensor_scalar_mul(st_sq[:, :], st_sq[:, :], 1.0 / D), inc="dve")
            d3 = p.step("dve", lambda e: e.tensor_tensor(st_su[:, :], st_mu[:, :], st_mu[:, :], ALU.mult), inc="dve")
            d4 = p.step("dve", lambda e: e.tensor_tensor(st_sq[:, :], st_sq[:, :], st_su[:, :], ALU.subtract), inc="dve")
            p.waits("act", [d4, t_eps])
            a3 = p.step("act", lambda e: e.activation(st_t[:, :], st_sq[:, :], AF.Ln, bias=eps_sb[0:1, :], scale=1.0), inc="act")
            a4 = p.step("act", lambda e: e.activation(st_su[:, :], st_t[:, :], AF.Exp, bias=0.0, scale=-0.5), inc="act")
            p.waits("dve", [a4])
            d5 = p.step("dve", lambda e: e.tensor_copy(st_rb[:, :], st_su[:, :]), inc="dve")
            d6 = p.step("dve", lambda e: e.tensor_tensor(st_m2[:, :], st_mu[:, :], st_su[:, :], ALU.mult), inc="dve")
            p.waits("pe", [d5, d6])
            for ts in range(2):
                p.step("pe", lambda e, ts=ts: e.matmul(
                    ps_bc[:, ts * 512:(ts + 1) * 512], ones[0:1, :],
                    st_rb[0:1, ts * 512:(ts + 1) * 512], start=True, stop=True))
                tb = p.step("pe", lambda e, ts=ts: e.matmul(
                    ps_bc2[:, ts * 512:(ts + 1) * 512], ones[0:1, :],
                    st_m2[0:1, ts * 512:(ts + 1) * 512], start=True, stop=True), inc="pe")
            p.wait("act", tb)
            a5 = p.step("act", lambda e: e.copy(R_sb[:, :], ps_bc[:, :]), inc="act")
            a6 = p.step("act", lambda e: e.copy(M2_sb[:, :], ps_bc2[:, :]), inc="act")
            p.waits("dve", [a5, a6])
            Rb = R_sb[:, :].unsqueeze(1).broadcast_to([128, 4, TOWN])
            Mb = M2_sb[:, :].unsqueeze(1).broadcast_to([128, 4, TOWN])
            for g in range(4):
                lnap, extra = sink(g)
                p.waits("dve", extra)
                p.step("dve", lambda e, g=g, lnap=lnap, Rb=Rb: e.tensor_tensor(
                    lnap, x_bf[:, 4 * g:4 * g + 4, :], Rb, ALU.mult), inc="dve")
                tok = p.step("dve", lambda e, lnap=lnap, Mb=Mb: e.tensor_tensor(
                    lnap, lnap, Mb, ALU.subtract), inc="dve")
                sink_done(g, tok)
        return

    sink_done_cell = [None]
    def sink_done(kc, tok):
        sink_done_cell[0](kc, tok)

    def emit_attn(idx, variant, x_in, x_out):
        li = 0 if variant == "a" else 1
        lsh = lnsh[li]
        part = partial[li]
        lmy = lnmy[li]
        pmy = part_my[li]
        pmd = pm_dram[li]
        ppd = pp_dram[li]
        wqk = wqk_a if variant == "a" else wqk_b
        wv = wv_a if variant == "a" else wv_b
        wo = wo_a if variant == "a" else wo_b

        with contextlib.ExitStack() as S:
            SA = S.enter_context(contextlib.ExitStack())
            q_fm = p.sb(SA, f"qfm{idx}", [128, MYH, T], BF16)
            k_fm = p.sb(SA, f"kfm{idx}", [128, MYH, T], BF16)
            v_ext = p.sb(SA, f"vex{idx}", [128, KC, MYH, 129], BF16)
            wring = p.sb(SA, f"wr{idx}", [128, 2, KC, 128], BF16)

            S1 = S.enter_context(contextlib.ExitStack())
            x_bf = p.sb(S1, f"xbf{idx}", [128, KC, TOWN], BF16)
            lnr = p.sb(S1, f"lnr{idx}", [128, 2, 4, TOWN], BF16)
            ln_dma = [None] * 4

            def sink(g):
                extra = [ln_dma[g - 2]] if g >= 2 else []
                return lnr[:, g % 2, :, :], extra

            def sdone(g, tok):
                p.wait("sync", tok)
                ln_dma[g] = p.step("sync", lambda e, g=g: e.dma_start(
                    out=lmy[g * 512:(g + 1) * 512, :].rearrange("(k p) t -> p k t", p=128),
                    in_=lnr[:, g % 2, :, :]), inc=f"lnw{g % 2}", n=16)

            sink_done_cell[0] = sdone
            emit_ln(S1, x_in, x_bf, sink)
            p.waits("sync", [ln_dma[3], ln_dma[2]])
            t_lnd2d = cdma("sync", lambda e, h, c: e.dma_start(
                out=lsh[:, h * TOWN:(h + 1) * TOWN], in_=lmy[:, :], cond=c),
                inc="lnd")
            t_bar = barrier([t_lnd2d])
            S1.close()

            S2 = S.enter_context(contextlib.ExitStack())
            ln_all = p.sb(S2, f"lna{idx}", [128, KC, T], BF16)

            t_vones = p.step("dve", lambda e: e.memset(
                v_ext[:, :, :, 128:129], 1.0), inc="dve")

            p.wait("sync", t_bar)
            lnld = [None] * KC
            for kc in range(KC):
                lnld[kc] = p.step("sync", lambda e, kc=kc: e.dma_start(
                    out=ln_all[:, kc, :], in_=lsh[kc * 128:(kc + 1) * 128, :]),
                    inc="lnl", n=16)

            with (contextlib.ExitStack() as S2a,
                  nc.psum_tensor(f"ps_qk{idx}", [128, 2, 512], F32) as ps_qk):
                wld = [None] * 16
                evac = {}
                for mt in range(16):
                    sl = mt % 2
                    p.waits("pool", [evac.get((mt - 2, 3))])
                    wld[mt] = p.step("pool", lambda e, mt=mt, sl=sl: e.dma_start(
                        out=wring[:, sl, :, :], in_=wqk[:, mt, :, :]),
                        inc=f"wq{sl}", n=16)
                    for ts in range(4):
                        psl = (mt * 4 + ts) % 2
                        p.waits("pe", [wld[mt], lnld[KC - 1], evac.get((mt, ts - 2) if ts >= 2 else (mt - 1, ts + 2))])
                        for kc in range(KC):
                            tok = p.step("pe", lambda e, mt=mt, ts=ts, kc=kc, sl=sl, psl=psl: e.matmul(
                                ps_qk[:, psl, :], wring[:, sl, kc, :],
                                ln_all[:, kc, ts * 512:(ts + 1) * 512],
                                start=(kc == 0), stop=(kc == KC - 1)),
                                inc="pe" if kc == KC - 1 else None)
                        dst = q_fm if mt < 8 else k_fm
                        hh = mt % 8
                        p.wait("act", tok)
                        evac[(mt, ts)] = p.step("act", lambda e, dst=dst, hh=hh, ts=ts, psl=psl: e.copy(
                            dst[:, hh, ts * 512:(ts + 1) * 512], ps_qk[:, psl, :]),
                            inc="act")
                p.fence()

            with (contextlib.ExitStack() as S2b,
                  nc.psum_tensor(f"ps_v{idx}", [128, 2, 512], F32) as ps_v):
                wvs = p.sb(S2b, f"wvs{idx}", [128, KC, 512], BF16)
                vevac = {}
                vmm = {}
                for ng in range(2):
                    p.waits("pool", [vmm.get((ng - 1, KC - 1))])
                    t_wv = p.step("pool", lambda e, ng=ng: e.dma_start(
                        out=wvs[:, :, :], in_=wv[:, :, ng * 512:(ng + 1) * 512]),
                        inc="wvv", n=16)
                    for tt in range(KC):
                        psl = tt % 2
                        p.waits("pe", [t_wv, vevac.get((ng, tt - 2) if tt >= 2 else (ng - 1, tt + 14))])
                        for kc in range(KC):
                            tok = p.step("pe", lambda e, tt=tt, kc=kc, psl=psl: e.matmul(
                                ps_v[:, psl, :], ln_all[:, kc, tt * 128:(tt + 1) * 128],
                                wvs[:, kc, :],
                                start=(kc == 0), stop=(kc == KC - 1)),
                                inc="pe" if kc == KC - 1 else None)
                        vmm[(ng, tt)] = tok
                        p.waits("dve", [tok, t_vones])
                        vevac[(ng, tt)] = p.step("dve", lambda e, tt=tt, ng=ng, psl=psl: e.tensor_copy(
                            v_ext[:, tt, 4 * ng:4 * ng + 4, 0:128],
                            ps_v[:, psl, :].rearrange("p (h m) -> p h m", m=128)),
                            inc="dve")
                p.fence()

            S2.close()
            with contextlib.ExitStack() as S3:
                if variant == "a":
                    qsw = p.sb(S3, f"qsw{idx}", [128, MYH, T], BF16)
                    for tgt in (q_fm, k_fm):
                        s1 = p.step("sync", lambda e, tgt=tgt: e.dma_start(
                            out=qsw[0:64, :, :], in_=tgt[64:128, :, :]), inc="rsw", n=16)
                        s2 = p.step("sync", lambda e, tgt=tgt: e.dma_start(
                            out=qsw[64:128, :, :], in_=tgt[0:64, :, :]), inc="rsw", n=16)
                        p.waits("dve", [s1, s2, t_cos, t_sin])
                        cosb = cos_sb[:, :].unsqueeze(1).broadcast_to([128, MYH, T])
                        sinb = sin_sb[:, :].unsqueeze(1).broadcast_to([128, MYH, T])
                        p.step("dve", lambda e, tgt=tgt, c=cosb: e.tensor_tensor(
                            tgt[:, :, :], tgt[:, :, :], c, ALU.mult), inc="dve")
                        p.step("dve", lambda e, s=sinb: e.tensor_tensor(
                            qsw[:, :, :], qsw[:, :, :], s, ALU.mult), inc="dve")
                        tok = p.step("dve", lambda e, tgt=tgt: e.tensor_tensor(
                            tgt[:, :, :], tgt[:, :, :], qsw[:, :, :], ALU.add), inc="dve")
                        p.wait("sync", tok)
                    p.fence()

            av_all = p.sb(SA, f"av{idx}", [128, MYH, T], BF16)
            p_sb = p.sb(SA, f"pp{idx}", [128, 2, 1024], BF16)
            sinv = p.sb(SA, f"sinv{idx}", [128, 512], F32)
            brow = p.sb(SA, f"brow{idx}", [1, 1024], BF16)
            sbg = p.sb(SA, f"sbg{idx}", [128, 2, 512], BF16)

            with (nc.psum_tensor(f"ps_s{idx}", [128, 2, 1024], F32) as ps_s,
                  nc.psum_tensor(f"ps_av{idx}", [128, 2, 512], F32) as ps_av,
                  nc.psum_tensor(f"ps_sum{idx}", [128, 512], F32) as ps_sum,
                  nc.psum_tensor(f"ps_sb{idx}", [128, 512], F32) as ps_sb):
                exp_tok = {}
                af_tok = {}
                av_tok = {}
                avn_tok = {}
                last_sb = [None]
                last_sg = []
                last_sinv = []
                it = 0
                for slab in range(4):
                    nk = 4 * (slab + 1)
                    for hg in range(4):
                        h0 = 2 * hg
                        for kt in range(nk):
                            sl = it % 2
                            p.waits("pe", [exp_tok.get(it - 2), avn_tok.get((slab, hg - 1) if hg else (slab - 1, 3))])
                            for hh in range(2):
                                tok = p.step("pe", lambda e, sl=sl, hh=hh, h0=h0, kt=kt, slab=slab: e.matmul(
                                    ps_s[:, sl, hh * 512:(hh + 1) * 512],
                                    k_fm[:, h0 + hh, kt * 128:(kt + 1) * 128],
                                    q_fm[:, h0 + hh, slab * 512:(slab + 1) * 512],
                                    start=True, stop=True), inc="pe")
                            p.waits("act", [tok, av_tok.get(it - 2)])
                            exp_tok[it] = p.step("act", lambda e, sl=sl: e.activation(
                                p_sb[:, sl, :], ps_s[:, sl, :], AF.Exp,
                                bias=0.0, scale=SCALE), inc="act")
                            dep = exp_tok[it]
                            if kt >= nk - 4:
                                jrel = 512 * slab - 128 * kt
                                p.wait("pool", exp_tok[it])
                                af_tok[it] = p.step("pool", lambda e, sl=sl, jrel=jrel: e.affine_select(
                                    p_sb[:, sl, :], p_sb[:, sl, :],
                                    [[0, 2], [1, 512]], ALU.is_ge, 0.0,
                                    base=jrel, channel_multiplier=-1), inc="pool")
                                dep = af_tok[it]
                            p.waits("pe", [dep])
                            for hh in range(2):
                                p.step("pe", lambda e, sl=sl, hh=hh, h0=h0, kt=kt: e.matmul(
                                    ps_av[:, hh, :], v_ext[:, kt, h0 + hh, 0:128],
                                    p_sb[:, sl, hh * 512:(hh + 1) * 512],
                                    start=(kt == 0), stop=(kt == nk - 1)))
                                tp = (0, 32 * hh) if hh else None
                                av_tok[it] = p.step("pe", lambda e, sl=sl, hh=hh, h0=h0, kt=kt, tp=tp: e.matmul(
                                    ps_sum[32 * hh:32 * hh + 1, :],
                                    v_ext[:, kt, h0 + hh, 128:129],
                                    p_sb[:, sl, hh * 512:(hh + 1) * 512],
                                    start=(kt == 0), stop=(kt == nk - 1),
                                    tile_position=tp), inc="pe")
                            it += 1
                        tlast = av_tok[it - 1]
                        p.waits("act", [tlast, last_sb[0]] + last_sinv)
                        ac0 = p.step("act", lambda e: e.copy(
                            sinv[0:1, :], ps_sum[0:1, :]), inc="act")
                        ac1 = p.step("act", lambda e: e.copy(
                            sinv[32:33, :], ps_sum[32:33, :]), inc="act")
                        p.waits("dve", [ac0, ac1])
                        dr0 = p.step("dve", lambda e: e.reciprocal(
                            sinv[0:1, :], sinv[0:1, :]), inc="dve")
                        dr1 = p.step("dve", lambda e: e.reciprocal(
                            sinv[32:33, :], sinv[32:33, :]), inc="dve")
                        p.waits("pool", [dr0, dr1] + last_sg)
                        sg0 = p.step("pool", lambda e: e.dma_start(
                            out=brow[0:1, 0:512], in_=sinv[0:1, :]), inc="sgg", n=16)
                        sg = p.step("pool", lambda e: e.dma_start(
                            out=brow[0:1, 512:1024], in_=sinv[32:33, :]), inc="sgg", n=16)
                        last_sg = []
                        last_sinv = [sg0, sg]
                        for hh in range(2):
                            head = h0 + hh
                            p.waits("pe", [sg0, sg, t_ones, last_sb[0]])
                            bc = p.step("pe", lambda e, hh=hh: e.matmul(
                                ps_sb[:, :], ones[0:1, :], brow[0:1, hh * 512:(hh + 1) * 512],
                                start=True, stop=True), inc="pe")
                            last_sg.append(bc)
                            p.wait("act", bc)
                            sb1 = p.step("act", lambda e: e.copy(
                                sbg[:, 0, :], ps_sb[:, :]), inc="act")
                            last_sb[0] = sb1
                            if variant == "b":
                                p.waits("pe", [sb1, t_gwt])
                                gm = p.step("pe", lambda e, head=head, slab=slab: e.matmul(
                                    ps_sb[:, :], gwt_sb[:, :],
                                    q_fm[:, head, slab * 512:(slab + 1) * 512],
                                    start=True, stop=True), inc="pe")
                                p.waits("act", [gm, t_gb])
                                sb2 = p.step("act", lambda e: e.activation(
                                    sbg[:, 1, :], ps_sb[:, :], AF.Sigmoid,
                                    bias=gb_sb[:, :], scale=1.0), inc="act")
                                last_sb[0] = sb2
                            p.waits("dve", [sb1, tlast])
                            dn = p.step("dve", lambda e, head=head, slab=slab, hh=hh: e.tensor_tensor(
                                av_all[:, head, slab * 512:(slab + 1) * 512],
                                ps_av[:, hh, :], sbg[:, 0, :], ALU.mult), inc="dve")
                            if variant == "b":
                                p.waits("dve", [sb2])
                                dn = p.step("dve", lambda e, head=head, slab=slab: e.tensor_tensor(
                                    av_all[:, head, slab * 512:(slab + 1) * 512],
                                    av_all[:, head, slab * 512:(slab + 1) * 512],
                                    sbg[:, 1, :], ALU.mult), inc="dve")
                            avn_tok[(slab, hg)] = dn
                p.fence()

            ostg = p.sb(SA, f"ostg{idx}", [128, 2, 512], BF16)
            wro = p.sb(SA, f"wro{idx}", [128, 2, 8, 128], BF16)
            with nc.psum_tensor(f"ps_o{idx}", [128, 2, 512], F32) as ps_o:
                wld = [None] * 16
                odma = {}
                for mt in range(16):
                    sl = mt % 2
                    p.waits("pool", [odma.get((mt - 2, 3))])
                    wld[mt] = p.step("pool", lambda e, mt=mt, sl=sl: e.dma_start(
                        out=wro[:, sl, 0:8, :], in_=wo[:, mt, :, :]),
                        inc=f"wo{sl}", n=16)
                    for slab in range(4):
                        psl = (mt * 4 + slab) % 2
                        p.waits("pe", [wld[mt], odma.get((mt, slab - 2) if slab >= 2 else (mt - 1, slab + 2))])
                        for hc in range(8):
                            tok = p.step("pe", lambda e, mt=mt, slab=slab, hc=hc, sl=sl, psl=psl: e.matmul(
                                ps_o[:, psl, :], wro[:, sl, hc, :],
                                av_all[:, hc, slab * 512:(slab + 1) * 512],
                                start=(hc == 0), stop=(hc == 7)),
                                inc="pe" if hc == 7 else None)
                        p.waits("act", [tok, odma.get((mt, slab - 2) if slab >= 2 else (mt - 1, slab + 2))])
                        ev = p.step("act", lambda e, psl=psl: e.copy(
                            ostg[:, psl, :], ps_o[:, psl, :]), inc="act")
                        p.wait("sync", ev)
                        odma[(mt, slab)] = p.step("sync", lambda e, mt=mt, slab=slab, psl=psl: e.dma_start(
                            out=pmy[mt * 128:(mt + 1) * 128, slab * 512:(slab + 1) * 512],
                            in_=ostg[:, psl, :]), inc=f"odm{psl}", n=16)
            p.waits("act", [p.now("odm0"), p.now("odm1")])
            t_pw = cdma("act", lambda e, h, c: e.dma_start(
                out=part[h * (KC * 128):(h + 1) * (KC * 128), :],
                in_=pmy[:, (1 - h) * TOWN:(1 - h) * TOWN + TOWN], cond=c),
                inc="tpw")
            p.waits("pool", [p.now("odm0"), p.now("odm1")])
            t_pmx = cdma("pool", lambda e, h, c: e.dma_start(
                out=pmd[:, :], in_=pmy[:, h * TOWN:h * TOWN + TOWN], cond=c),
                inc="pxx")
            t_bar2 = barrier([t_pw, t_pmx])
            p.wait("pool", t_bar2)
            t_ppx = cdma("pool", lambda e, h, c: e.dma_start(
                out=ppd[:, :], in_=part[(1 - h) * (KC * 128):(2 - h) * (KC * 128), :],
                cond=c), inc="pxx")

            SA.close()
            prr = p.sb(S, f"prr{idx}", [128, 2, 2, 2, TOWN], BF16)
            xstr = p.sb(S, f"xstr{idx}", [128, 2, 2, TOWN], F32)
            rtok = {}
            for g in range(8):
                sl = g % 2
                p.waits("sync", [rtok.get(g - 2)])
                l1 = p.step("sync", lambda e, g=g, sl=sl: e.dma_start(
                    out=xstr[:, sl, :, :], in_=x_in.group2(g)), inc=f"xld{sl}", n=16)
                p.waits("act", [rtok.get(g - 2), t_pmx if g == 0 else None])
                l2 = p.step("act", lambda e, g=g, sl=sl: e.dma_start(
                    out=prr[:, sl, 0, :, :],
                    in_=pmd[g * 256:(g + 1) * 256, :].rearrange("(k p) t -> p k t", p=128)),
                    inc=f"prr{sl}", n=16)
                p.waits("act", [t_ppx if g == 0 else None])
                l3 = p.step("act", lambda e, g=g, sl=sl: e.dma_start(
                    out=prr[:, sl, 1, :, :],
                    in_=ppd[g * 256:(g + 1) * 256, :].rearrange("(k p) t -> p k t", p=128)),
                    inc=f"prr{sl}", n=16)
                p.waits("dve", [l1, l2, l3])
                p.step("dve", lambda e, sl=sl: e.tensor_tensor(
                    prr[:, sl, 0, :, :], prr[:, sl, 0, :, :], prr[:, sl, 1, :, :],
                    ALU.add), inc="dve")
                dtok = p.step("dve", lambda e, sl=sl: e.tensor_tensor(
                    xstr[:, sl, :, :], xstr[:, sl, :, :], prr[:, sl, 0, :, :], ALU.add),
                    inc="dve")
                p.wait("sync", dtok)
                rtok[g] = p.step("sync", lambda e, g=g, sl=sl: e.dma_start(
                    out=x_out.group2(g), in_=xstr[:, sl, :, :]), inc=f"xso{sl}", n=16)
            p.fence()
            if tap and idx == 0:
                p.step("sync", lambda e: e.dma_start(out=dram["__tav"], in_=av_all[:, :, :]), inc="tap", n=16)
                p.step("sync", lambda e: e.dma_start(out=dram["__tpmy"], in_=pmy), inc="tap", n=16)
                p.step("sync", lambda e: e.dma_start(out=dram["__tpart"], in_=part), inc="tap", n=16)
                p.step("sync", lambda e: e.dma_start(out=dram["__tpmd"], in_=pmd), inc="tap", n=16)
                p.step("sync", lambda e: e.dma_start(out=dram["__tppd"], in_=ppd), inc="tap", n=16)
                p.fence()

    def emit_ffn(idx, x_in, x_out):
        with contextlib.ExitStack() as S:
            acc = p.sb(S, f"facc{idx}", [128, KC, TOWN], BF16)
            wgu = p.sb(S, f"fwgu{idx}", [128, 2, KC, 512], BF16)
            SF = S.enter_context(contextlib.ExitStack())
            ln = p.sb(SF, f"fln{idx}", [128, KC, TOWN], BF16)

            S1 = S.enter_context(contextlib.ExitStack())
            x_bf = p.sb(S1, f"fxbf{idx}", [128, KC, TOWN], BF16)

            def sink(g):
                return ln[:, 4 * g:4 * g + 4, :], []

            def sdone(g, tok):
                pass

            sink_done_cell[0] = sdone
            emit_ln(S1, x_in, x_bf, sink)
            t_ln = p.now("dve")
            S1.close()

            hng = p.sb(SF, f"fh{idx}", [128, 2, 8, 512], BF16)
            ang = p.sb(SF, f"fa{idx}", [128, 2, 8, 512], BF16)
            tau = p.sb(SF, f"ftau{idx}", [128, 2, 8, 128], BF16)
            tsc = p.sb(SF, f"ftsc{idx}", [128, 2, 2, 8, 128], BF16)
            sgr = p.sb(SF, f"fsg{idx}", [128, 8, 512], BF16)
            hfm = p.sb(SF, f"fhfm{idx}", [128, 16, TOWN], BF16)
            wdr = p.sb(SF, f"fwd{idx}", [128, 2, 16, 128], BF16)

            with (nc.psum_tensor(f"fps_g{idx}", [128, 2, 512], F32) as ps_g,
                  nc.psum_tensor(f"fps_u{idx}", [128, 2, 512], F32) as ps_u,
                  nc.psum_tensor(f"fps_t{idx}", [128, 2, 1024], BF16) as ps_t,
                  nc.psum_tensor(f"fps_d{idx}", [128, 2, 512], F32) as ps_d):
                wl_tok = {}
                gmm_tok = {}
                umm_tok = {}
                sa_tok = {}
                hm_tok = {}
                mask_tok = {}
                tr_evac = {}
                dn_evac = {}

                def emit_transposes(tng):
                    tngl = tng % 4
                    thsl = tng % 2
                    for tt in range(8):
                        tsl = tt % 2
                        p.waits("pe", [mask_tok[tng], t_idn,
                                       tr_evac.get((tng, tt - 2) if tt >= 2 else (tng - 1, tt + 6))])
                        for j in range(4):
                            tok = p.step("pe", lambda e, thsl=thsl, tt=tt, j=j, tsl=tsl: e.transpose(
                                ps_t[:, tsl, j * 128:(j + 1) * 128],
                                hng[:, thsl, tt, j * 128:(j + 1) * 128], idn[:, :]),
                                inc="pe" if j == 3 else None)
                        p.wait("dve", tok)
                        tr_evac[(tng, tt)] = p.step("dve", lambda e, tngl=tngl, tt=tt, tsl=tsl: e.tensor_copy(
                            hfm[:, 4 * tngl:4 * tngl + 4, tt * 128:(tt + 1) * 128],
                            ps_t[:, tsl, 0:512].rearrange("p (a b) -> p a b", b=128)),
                            inc="dve")

                for qtr in range(4):
                    for ngl in range(4):
                        ng = qtr * 4 + ngl
                        hsl = ng % 2
                        p.waits("pool", [gmm_tok.get((ng - 1, 7))])
                        wl_tok[(ng, 0)] = p.step("pool", lambda e, ng=ng: e.dma_start(
                            out=wgu[:, 0, :, :],
                            in_=wg_f[:, :, ng * 512:(ng + 1) * 512]), inc="wg0", n=16)
                        p.waits("pool", [umm_tok.get((ng - 1, 7))])
                        wl_tok[(ng, 1)] = p.step("pool", lambda e, ng=ng: e.dma_start(
                            out=wgu[:, 1, :, :],
                            in_=wu_f[:, :, ng * 512:(ng + 1) * 512]), inc="wg1", n=16)
                        for tt in range(8):
                            gsl = tt % 2
                            p.waits("pe", [wl_tok[(ng, 0)], t_ln,
                                           sa_tok.get((ng, tt - 2) if tt >= 2 else (ng - 1, tt + 6)),
                                           dn_evac.get(("last", qtr - 1)) if (ngl == 0 and tt == 0) else None])
                            for kc in range(KC):
                                tok = p.step("pe", lambda e, ng=ng, tt=tt, kc=kc, gsl=gsl: e.matmul(
                                    ps_g[:, gsl, :], ln[:, kc, tt * 128:(tt + 1) * 128],
                                    wgu[:, 0, kc, :],
                                    start=(kc == 0), stop=(kc == KC - 1)),
                                    inc="pe" if kc == KC - 1 else None)
                            gmm_tok[(ng, tt)] = tok
                            p.waits("act", [tok, hm_tok.get((ng - 1, tt))])
                            sa_tok[(ng, tt)] = p.step("act", lambda e, tt=tt, gsl=gsl: e.activation(
                                sgr[:, tt, :], ps_g[:, gsl, :], AF.Silu,
                                bias=0.0, scale=1.0), inc="act")
                        for tt in range(8):
                            gsl = tt % 2
                            p.waits("pe", [wl_tok[(ng, 1)],
                                           hm_tok.get((ng, tt - 2) if tt >= 2 else (ng - 1, tt + 6))])
                            for kc in range(KC):
                                tok = p.step("pe", lambda e, ng=ng, tt=tt, kc=kc, gsl=gsl: e.matmul(
                                    ps_u[:, gsl, :], ln[:, kc, tt * 128:(tt + 1) * 128],
                                    wgu[:, 1, kc, :],
                                    start=(kc == 0), stop=(kc == KC - 1)),
                                    inc="pe" if kc == KC - 1 else None)
                            umm_tok[(ng, tt)] = tok
                            p.waits("dve", [tok, sa_tok[(ng, tt)]])
                            hm_tok[(ng, tt)] = p.step("dve", lambda e, tt=tt, hsl=hsl, gsl=gsl: e.tensor_tensor(
                                hng[:, hsl, tt, :], sgr[:, tt, :], ps_u[:, gsl, :],
                                ALU.mult), inc="dve")
                        if ngl > 0:
                            emit_transposes(ng - 1)
                        heng = "dve"
                        esl = 0 if heng == "dve" else 1
                        hfl = hng[:, hsl, :, :]
                        afl = ang[:, hsl, :, :]
                        a4 = afl.rearrange("p t (g j) -> p t g j", j=4)
                        tq = tau[:, esl, :, :]
                        s1 = tsc[:, esl, 0, :, :]
                        s2 = tsc[:, esl, 1, :, :]
                        p.waits("act", [hm_tok[(ng, 7)], mask_tok.get(ng - 2)])
                        aa = p.step("act", lambda e, hfl=hfl, afl=afl: e.activation(
                            afl, hfl, AF.Abs, bias=0.0, scale=1.0), inc="act")
                        E = heng
                        p.waits(E, [aa])
                        p.step(E, lambda e, a4=a4, s1=s1: e.tensor_tensor(s1, a4[:, :, :, 0], a4[:, :, :, 1], ALU.max), inc=E)
                        p.step(E, lambda e, a4=a4, s2=s2: e.tensor_tensor(s2, a4[:, :, :, 2], a4[:, :, :, 3], ALU.max), inc=E)
                        p.step(E, lambda e, s1=s1, s2=s2, tq=tq: e.tensor_tensor(tq, s1, s2, ALU.min), inc=E)
                        p.step(E, lambda e, a4=a4, s1=s1: e.tensor_tensor(s1, a4[:, :, :, 0], a4[:, :, :, 1], ALU.min), inc=E)
                        p.step(E, lambda e, a4=a4, s2=s2: e.tensor_tensor(s2, a4[:, :, :, 2], a4[:, :, :, 3], ALU.min), inc=E)
                        p.step(E, lambda e, s1=s1, s2=s2: e.tensor_tensor(s1, s1, s2, ALU.max), inc=E)
                        p.step(E, lambda e, s1=s1, tq=tq: e.tensor_tensor(tq, tq, s1, ALU.max), inc=E)
                        taub = tq.unsqueeze(3).broadcast_to([128, 8, 128, 4])
                        p.step(E, lambda e, a4=a4, taub=taub: e.tensor_tensor(a4, a4, taub, ALU.is_ge), inc=E)
                        mask_tok[ng] = p.step(E, lambda e, hfl=hfl, afl=afl: e.tensor_tensor(
                            hfl, hfl, afl, ALU.mult), inc=E)
                    emit_transposes(qtr * 4 + 3)
                    wd_tok = {}
                    for mt in range(16):
                        wsl = mt % 2
                        wprev = (mt - 2, 1) if mt >= 2 else ((mt + 14, 1) if qtr else None)
                        p.waits("pool", [dn_evac.get(wprev) if wprev else None])
                        wd_tok[mt] = p.step("pool", lambda e, qtr=qtr, mt=mt, wsl=wsl: e.dma_start(
                            out=wdr[:, wsl, :, :], in_=wd_f[:, mt, qtr * 16:(qtr + 1) * 16, :]),
                            inc=f"wd{wsl}", n=16)
                        for slab in range(2):
                            psl = (mt * 2 + slab) % 2
                            dprev = (mt - 1, slab) if mt else ((15, slab) if qtr else None)
                            p.waits("pe", [wd_tok[mt], tr_evac[(qtr * 4 + 3, 7)],
                                           dn_evac.get(dprev) if dprev else None])
                            for kq in range(16):
                                tok = p.step("pe", lambda e, mt=mt, slab=slab, kq=kq, wsl=wsl, psl=psl: e.matmul(
                                    ps_d[:, psl, :], wdr[:, wsl, kq, :],
                                    hfm[:, kq, slab * 512:(slab + 1) * 512],
                                    start=(kq == 0), stop=(kq == 15)),
                                    inc="pe" if kq == 15 else None)
                            p.waits("dve", [tok])
                            if qtr == 0:
                                dn_evac[(mt, slab)] = p.step("dve", lambda e, mt=mt, slab=slab, psl=psl: e.tensor_copy(
                                    acc[:, mt, slab * 512:(slab + 1) * 512], ps_d[:, psl, :]),
                                    inc="dve")
                            else:
                                dn_evac[(mt, slab)] = p.step("dve", lambda e, mt=mt, slab=slab, psl=psl: e.tensor_tensor(
                                    acc[:, mt, slab * 512:(slab + 1) * 512],
                                    acc[:, mt, slab * 512:(slab + 1) * 512],
                                    ps_d[:, psl, :], ALU.add), inc="dve")
                    dn_evac[("last", qtr)] = dn_evac[(15, 1)]
                p.fence()

            SF.close()
            rtok = {}
            xstr = p.sb(S, f"fxstr{idx}", [128, 2, 2, TOWN], F32)
            for g in range(8):
                sl = g % 2
                p.waits("sync", [rtok.get(g - 2)])
                l1 = p.step("sync", lambda e, g=g, sl=sl: e.dma_start(
                    out=xstr[:, sl, :, :], in_=x_in.group2(g)), inc=f"xld{sl}", n=16)
                p.waits("dve", [l1])
                dtok = p.step("dve", lambda e, g=g, sl=sl: e.tensor_tensor(
                    xstr[:, sl, :, :], xstr[:, sl, :, :], acc[:, 2 * g:2 * g + 2, :],
                    ALU.add), inc="dve")
                p.wait("sync", dtok)
                rtok[g] = p.step("sync", lambda e, g=g, sl=sl: e.dma_start(
                    out=x_out.group2(g), in_=xstr[:, sl, :, :]), inc=f"xso{sl}", n=16)
            p.fence()

    subs = [("attn", "a"), ("ffn", None), ("attn", "b"), ("ffn", None)]
    for i in range(n_sub):
        x_in, x_out = chain[i]
        kind, var = subs[i]
        if kind == "attn":
            emit_attn(i, var, _ChunkWriter(x_in), _ChunkWriter(x_out))
        else:
            emit_ffn(i, _ChunkWriter(x_in), _ChunkWriter(x_out))

    p.fence()

    for _s in p.sems.values():
        nc.gpsimd.sem_clear(_s)
    nc.all_engine_barrier()

    with nc.Block() as block:
        @block.tensor
        def _(t):
            for fn in p.ev["pe"]:
                fn(t)

        @block.scalar
        def _(s):
            for fn in p.ev["act"]:
                fn(s)

        @block.vector
        def _(v):
            for fn in p.ev["dve"]:
                fn(v)

        @block.gpsimd
        def _(g):
            for fn in p.ev["pool"]:
                fn(g)

        @block.sync
        def _(sy):
            for fn in p.ev["sync"]:
                fn(sy)

    p.st.close()
    return nc


class _ChunkWriter:
    def __init__(self, x):
        self.x = x

    def __call__(self, kc):
        return self.x[kc, :, :]

    def __getitem__(self, kc):
        return self.x[kc, :, :]

    def group4(self, g):
        return self.x[4 * g:4 * g + 4, :, :].rearrange("k p t -> p k t")

    def group2(self, g):
        return self.x[2 * g:2 * g + 2, :, :].rearrange("k p t -> p k t")



_cached = {}


def _get_program(n_sub=N_SUB):
    if n_sub not in _cached:
        _cached[n_sub] = build_program(n_sub)
    return _cached[n_sub]


def make_in_maps(inputs):
    w_qkv = inputs["L0_attn_a_w_qkv"]
    w_o = inputs["L0_attn_a_w_o"]
    q_w = inputs["L0_attn_b_q_w"]
    k_w = inputs["L0_attn_b_k_w"]
    v_w = inputs["L0_attn_b_v_w"]
    o_w = inputs["L0_attn_b_o_w"]
    gate_w = inputs["L0_attn_b_gate_w"]
    gate_b = inputs["L0_attn_b_gate_b"]
    w_gate = inputs["L0_ffn_w_gate"]
    w_up = inputs["L0_ffn_w_up"]
    w_down = inputs["L0_ffn_w_down"]
    x = inputs["L0_attn_a_x"]

    cosf, sinS = _rope_tables()
    ident = np.eye(128, dtype=np.float32)
    shared = {
        "wgt": np.ascontiguousarray(gate_w.T),
        "gbi": np.ascontiguousarray(gate_b.reshape(128, 1)),
        "wg_f": _b_layout(w_gate),
        "wu_f": _b_layout(w_up),
        "wd_f": _a_layout(w_down),
        "cosf": cosf,
        "sinSf": sinS,
        "identf": ident,
    }
    qT, kT, vT, oT = q_w.T, k_w.T, v_w.T, o_w.T
    maps = []
    for c in range(8):
        b, h = c // 2, c % 2
        cs = slice(h * 1024, (h + 1) * 1024)
        m = dict(shared)
        m["wqk_a"] = _a_layout(np.concatenate(
            [w_qkv[:, h * 1024:(h + 1) * 1024],
             w_qkv[:, 2048 + h * 1024:2048 + (h + 1) * 1024]], axis=1))
        m["wv_a"] = _b_layout(w_qkv[:, 4096 + h * 1024:4096 + (h + 1) * 1024])
        m["wo_a"] = _a_layout(w_o[h * 1024:(h + 1) * 1024, :])
        m["wqk_b"] = _a_layout(np.concatenate([qT[:, cs], kT[:, cs]], axis=1))
        m["wv_b"] = _b_layout(vT[:, cs])
        m["wo_b"] = _a_layout(oT[h * 1024:(h + 1) * 1024, :])
        m["x0"] = _fm(x[b, h * 1024:(h + 1) * 1024, :])
        maps.append(m)
    return maps


def kernel(**inputs):
    nc = _get_program(N_SUB)
    maps = make_in_maps(inputs)
    res = run_bass_kernel_spmd(nc, maps, core_ids=list(range(8)))
    out = np.empty((B, T, D), dtype=np.float32)
    for c in range(8):
        b, h = c // 2, c % 2
        yc = np.asarray(res.results[c]["y"])
        out[b, h * 1024:(h + 1) * 1024, :] = yc.transpose(2, 0, 1).reshape(TOWN, D)
    return out


if __name__ == "__main__":
    nc = build_program(1)
    print("build ok")


# version 05
# speedup vs baseline: 1.0839x; 1.0004x over previous
import math
import numpy as np
import ml_dtypes
import contextlib
import concourse.bass as bass
import concourse.mybir as mybir
from concourse.bass import ds
from concourse.bass_utils import run_bass_kernel_spmd

F32 = mybir.dt.float32
BF16 = mybir.dt.bfloat16
AF = mybir.ActivationFunctionType
ALU = mybir.AluOpType
BFNP = ml_dtypes.bfloat16

D = 2048
NH = 16
HD = 128
T = 2048
B = 4
II = 8192
MYH = 8
TOWN = 1024
KC = 16
ICH = 64
EPS = 1e-5
SCALE = 1.0 / math.sqrt(HD)

N_SUB = 4

ENGS = ("pe", "act", "dve", "pool", "sync")



def _a_layout(w):
    din, dout = w.shape
    return np.ascontiguousarray(
        w.reshape(din // 128, 128, dout // 128, 128).transpose(1, 2, 0, 3))


def _b_layout(w):
    din, dout = w.shape
    return np.ascontiguousarray(w.reshape(din // 128, 128, dout).transpose(1, 0, 2))


def _fm(x):
    t, d = x.shape
    return np.ascontiguousarray(x.T.reshape(d // 128, 128, t))


def _rope_tables():
    inv = 1.0 / 10000.0 ** (np.arange(0, HD, 2, dtype=np.float32) / HD)
    t = np.arange(T, dtype=np.float32)
    fr = np.outer(t, inv)
    c = np.cos(fr).T
    s = np.sin(fr).T
    cosf = np.concatenate([c, c], 0).astype(np.float32)
    sinS = np.concatenate([-s, s], 0).astype(np.float32)
    return cosf, sinS



class Pb:

    def __init__(self, nc):
        self.nc = nc
        self.ev = {e: [] for e in ENGS}
        self.cnt = {}
        self.sems = {}
        self.st = contextlib.ExitStack()

    def sem(self, name):
        if name not in self.sems:
            self.sems[name] = self.st.enter_context(self.nc.semaphore(name))
            self.cnt[name] = 0
        return self.sems[name]

    def on(self, eng, fn):
        self.ev[eng].append(fn)

    def wait(self, eng, token):
        if token is None:
            return
        sname, val = token
        sem = self.sem(sname)
        self.on(eng, lambda e, s=sem, v=val: e.wait_ge(s, v))

    def waits(self, eng, tokens):
        best = {}
        for t in tokens:
            if t is None:
                continue
            s, v = t
            best[s] = max(best.get(s, 0), v)
        for s, v in best.items():
            self.wait(eng, (s, v))

    def step(self, eng, build, inc=None, n=1):
        tok = None
        if inc is not None:
            sem = self.sem(inc)
            self.cnt[inc] += n
            tok = (inc, self.cnt[inc])
            def fn(e, b=build, s=sem, nn=n):
                b(e).then_inc(s, nn)
        else:
            def fn(e, b=build):
                b(e)
        self.on(eng, fn)
        return tok

    def now(self, sem):
        self.sem(sem)
        return (sem, self.cnt[sem])

    def fence(self):
        toks = [self.now(s) for s in list(self.sems)]
        for e in ENGS:
            self.waits(e, toks)

    def sb(self, scope, name, shape, dtype):
        return scope.enter_context(self.nc.sbuf_tensor(name, shape, dtype))

    def ps(self, scope, name, shape, dtype=F32):
        return scope.enter_context(self.nc.psum_tensor(name, shape, dtype))


def build_program(n_sub=N_SUB, tap=None):
    nc = bass.Bass("TRN2", target_bir_lowering=False)
    p = Pb(nc)

    dram = {}
    def din(name, shape, dtype=F32):
        dram[name] = nc.dram_tensor(name, shape, dtype, kind="ExternalInput").ap()
        return dram[name]

    wqk_a = din("wqk_a", [128, 16, 16, 128])
    wv_a = din("wv_a", [128, 16, 1024])
    wo_a = din("wo_a", [128, 16, 8, 128])
    wqk_b = din("wqk_b", [128, 16, 16, 128])
    wv_b = din("wv_b", [128, 16, 1024])
    wo_b = din("wo_b", [128, 16, 8, 128])
    wgt = din("wgt", [128, 128])
    gbi = din("gbi", [128, 1])
    wg_f = din("wg_f", [128, 16, II])
    wu_f = din("wu_f", [128, 16, II])
    wd_f = din("wd_f", [128, 16, 64, 128])
    x0 = din("x0", [KC, 128, TOWN])
    cosf = din("cosf", [128, T])
    sinSf = din("sinSf", [128, T])
    identf = din("identf", [128, 128])

    y = nc.dram_tensor("y", [KC, 128, TOWN], F32, kind="ExternalOutput").ap()
    if tap:
        dram["__tav"] = nc.dram_tensor("t_av", [128, MYH, T], BF16, kind="ExternalOutput").ap()
        dram["__tpmy"] = nc.dram_tensor("t_pmy", [KC * 128, T], F32, kind="ExternalOutput").ap()
        dram["__tpart"] = nc.dram_tensor("t_part", [2 * KC * 128, TOWN], F32, kind="ExternalOutput").ap()
        dram["__tpmd"] = nc.dram_tensor("t_pmd", [KC * 128, TOWN], F32, kind="ExternalOutput").ap()
        dram["__tppd"] = nc.dram_tensor("t_ppd", [KC * 128, TOWN], F32, kind="ExternalOutput").ap()

    xs = [x0] + [nc.dram_tensor(f"x{i}", [KC, 128, TOWN], F32, kind="Internal").ap()
                 for i in (1, 2, 3)]
    lnsh = [nc.dram_tensor(f"lnsh{i}", [KC * 128, T], BF16, kind="Internal",
                           addr_space="Shared").ap() for i in (0, 1)]
    partial = [nc.dram_tensor(f"part{i}", [2 * KC * 128, TOWN], BF16, kind="Internal",
                              addr_space="Shared").ap() for i in (0, 1)]
    lnmy = [nc.dram_tensor(f"lnmy{i}", [KC * 128, TOWN], BF16, kind="Internal").ap()
            for i in (0, 1)]
    part_my = [nc.dram_tensor(f"pmy{i}", [KC * 128, T], BF16, kind="Internal").ap()
               for i in (0, 1)]
    pm_dram = [nc.dram_tensor(f"pmd{i}", [KC * 128, TOWN], BF16, kind="Internal").ap()
               for i in (0, 1)]
    pp_dram = [nc.dram_tensor(f"ppd{i}", [KC * 128, TOWN], BF16, kind="Internal").ap()
               for i in (0, 1)]
    bar_in = nc.dram_tensor("bar_in", [1, 4], F32, kind="Internal").ap()
    bar_out = nc.dram_tensor("bar_out", [1, 4], F32, kind="Internal").ap()

    chain = [(xs[0], xs[1]), (xs[1], xs[2]), (xs[2], xs[3]), (xs[3], y)]
    chain[n_sub - 1] = (chain[n_sub - 1][0], y)

    top = p.st

    cos_sb = p.sb(top, "cos_sb", [128, T], BF16)
    sin_sb = p.sb(top, "sin_sb", [128, T], BF16)
    idn = p.sb(top, "idn", [128, 128], BF16)
    ones = p.sb(top, "ones", [128, 128], BF16)
    gwt_sb = p.sb(top, "gwt_sb", [128, 128], BF16)
    gb_sb = p.sb(top, "gb_sb", [128, 1], F32)
    xst = p.sb(top, "xst", [128, 2, TOWN], F32)
    barsb = p.sb(top, "barsb", [1, 4], F32)
    eps_sb = p.sb(top, "eps_sb", [128, 1], F32)

    t_cos = p.step("pool", lambda e: e.dma_start(out=cos_sb[:, :], in_=cosf), inc="cst", n=16)
    t_sin = p.step("pool", lambda e: e.dma_start(out=sin_sb[:, :], in_=sinSf), inc="cst", n=16)
    t_idn = p.step("pool", lambda e: e.dma_start(out=idn[:, :], in_=identf), inc="cst", n=16)
    t_gwt = p.step("pool", lambda e: e.dma_start(out=gwt_sb[:, :], in_=wgt), inc="cst", n=16)
    t_gb = p.step("sync", lambda e: e.dma_start(out=gb_sb[:, :], in_=gbi), inc="cst2", n=16)
    t_ones = p.step("dve", lambda e: e.memset(ones[:, :], 1.0), inc="dve")
    t_eps = p.step("dve", lambda e: e.memset(eps_sb[:, :], EPS), inc="dve")
    t_bar0 = p.step("dve", lambda e: e.memset(barsb[:, :], 1.0), inc="dve")
    p.wait("sync", t_bar0)
    t_barin = p.step("sync", lambda e: e.dma_start(out=bar_in, in_=barsb[:, :]), inc="cst2", n=16)

    creg = {}
    def conds(eng):
        if eng not in creg:
            box = {}
            def mk(e, bx=box):
                hv = e.snap(e.partition_id() % 2)
                bx[1] = hv
                bx[0] = e.snap((hv + 1) % 2)
            p.on(eng, mk)
            creg[eng] = box
        return creg[eng]

    for e in ("sync", "act", "pool"):
        conds(e)

    def cdma(eng, mkfn, inc):
        p.step(eng, lambda e, m=mkfn: m(e, 0, conds(eng)[0]), inc=inc, n=16)
        return p.step(eng, lambda e, m=mkfn: m(e, 1, conds(eng)[1]), inc=inc, n=16)

    nbar = [0]

    def barrier(wait_tokens):
        p.waits("pool", wait_tokens + [t_barin])
        tok = p.step(
            "pool",
            lambda e: e.collective_compute(
                "AllReduce", ALU.add, [[0, 1], [2, 3], [4, 5], [6, 7]],
                [bar_in], [bar_out]),
            inc="bar")
        nbar[0] += 1
        return tok

    ln_ctr = [0]

    def emit_ln(scope, x_in, x_bf, sink):
        ln_ctr[0] += 1
        u = str(ln_ctr[0])
        st_su = p.sb(scope, "st_su" + u, [1, TOWN], F32)
        st_sq = p.sb(scope, "st_sq" + u, [1, TOWN], F32)
        st_mu = p.sb(scope, "st_mu" + u, [1, TOWN], F32)
        st_t = p.sb(scope, "st_t" + u, [1, TOWN], F32)
        st_rb = p.sb(scope, "st_rb" + u, [1, TOWN], BF16)
        st_m2 = p.sb(scope, "st_m2" + u, [1, TOWN], BF16)
        R_sb = p.sb(scope, "R_sb" + u, [128, TOWN], BF16)
        M2_sb = p.sb(scope, "M2_sb" + u, [128, TOWN], BF16)
        x2r = p.sb(scope, "x2r" + u, [128, 2, 2, TOWN], BF16)
        with (nc.psum_tensor("ps_su" + u, [1, TOWN], F32) as ps_su,
              nc.psum_tensor("ps_sq" + u, [1, TOWN], F32) as ps_sq,
              nc.psum_tensor("ps_bc" + u, [128, TOWN], F32) as ps_bc,
              nc.psum_tensor("ps_bc2" + u, [128, TOWN], F32) as ps_bc2):
            ld = [None] * 4
            sqs = [None] * 8
            pes = [None] * 8
            for g in range(4):
                ld[g] = p.step("pool", lambda e, g=g: e.dma_start(
                    out=x_bf[:, 4 * g:4 * g + 4, :],
                    in_=x_in.group4(g)), inc=f"xbf{g}", n=16)
                for hf in range(2):
                    i2 = 2 * g + hf
                    sl = i2 % 2
                    p.waits("dve", [ld[g]] + ([pes[i2 - 2]] if i2 >= 2 else []))
                    sqs[i2] = p.step("dve", lambda e, sl=sl, g=g, hf=hf: e.tensor_tensor(
                        x2r[:, sl, :, :], x_bf[:, 4 * g + 2 * hf:4 * g + 2 * hf + 2, :],
                        x_bf[:, 4 * g + 2 * hf:4 * g + 2 * hf + 2, :], ALU.mult), inc="dve")
                    p.waits("pe", [sqs[i2], t_ones])
                    for kk in range(2):
                        kc = 4 * g + 2 * hf + kk
                        for ts in range(2):
                            p.step("pe", lambda e, kc=kc, ts=ts: e.matmul(
                                ps_su[0:1, ts * 512:(ts + 1) * 512], ones[:, 0:1],
                                x_bf[:, kc, ts * 512:(ts + 1) * 512],
                                start=(kc == 0), stop=(kc == KC - 1)))
                            pes[i2] = p.step("pe", lambda e, kc=kc, ts=ts, sl=sl, kk=kk: e.matmul(
                                ps_sq[0:1, ts * 512:(ts + 1) * 512], ones[:, 0:1],
                                x2r[:, sl, kk, ts * 512:(ts + 1) * 512],
                                start=(kc == 0), stop=(kc == KC - 1)), inc="pe")
            t_stats = pes[7]
            p.wait("act", t_stats)
            a1 = p.step("act", lambda e: e.copy(st_su[:, :], ps_su[:, :]), inc="act")
            a2 = p.step("act", lambda e: e.copy(st_sq[:, :], ps_sq[:, :]), inc="act")
            p.waits("dve", [a1, a2])
            d1 = p.step("dve", lambda e: e.tensor_scalar_mul(st_mu[:, :], st_su[:, :], 1.0 / D), inc="dve")
            d2 = p.step("dve", lambda e: e.tensor_scalar_mul(st_sq[:, :], st_sq[:, :], 1.0 / D), inc="dve")
            d3 = p.step("dve", lambda e: e.tensor_tensor(st_su[:, :], st_mu[:, :], st_mu[:, :], ALU.mult), inc="dve")
            d4 = p.step("dve", lambda e: e.tensor_tensor(st_sq[:, :], st_sq[:, :], st_su[:, :], ALU.subtract), inc="dve")
            p.waits("act", [d4, t_eps])
            a3 = p.step("act", lambda e: e.activation(st_t[:, :], st_sq[:, :], AF.Ln, bias=eps_sb[0:1, :], scale=1.0), inc="act")
            a4 = p.step("act", lambda e: e.activation(st_su[:, :], st_t[:, :], AF.Exp, bias=0.0, scale=-0.5), inc="act")
            p.waits("dve", [a4])
            d5 = p.step("dve", lambda e: e.tensor_copy(st_rb[:, :], st_su[:, :]), inc="dve")
            d6 = p.step("dve", lambda e: e.tensor_tensor(st_m2[:, :], st_mu[:, :], st_su[:, :], ALU.mult), inc="dve")
            p.waits("pe", [d5, d6])
            for ts in range(2):
                p.step("pe", lambda e, ts=ts: e.matmul(
                    ps_bc[:, ts * 512:(ts + 1) * 512], ones[0:1, :],
                    st_rb[0:1, ts * 512:(ts + 1) * 512], start=True, stop=True))
                tb = p.step("pe", lambda e, ts=ts: e.matmul(
                    ps_bc2[:, ts * 512:(ts + 1) * 512], ones[0:1, :],
                    st_m2[0:1, ts * 512:(ts + 1) * 512], start=True, stop=True), inc="pe")
            p.wait("act", tb)
            a5 = p.step("act", lambda e: e.copy(R_sb[:, :], ps_bc[:, :]), inc="act")
            a6 = p.step("act", lambda e: e.copy(M2_sb[:, :], ps_bc2[:, :]), inc="act")
            p.waits("dve", [a5, a6])
            Rb = R_sb[:, :].unsqueeze(1).broadcast_to([128, 4, TOWN])
            Mb = M2_sb[:, :].unsqueeze(1).broadcast_to([128, 4, TOWN])
            for g in range(4):
                lnap, extra = sink(g)
                p.waits("dve", extra)
                p.step("dve", lambda e, g=g, lnap=lnap, Rb=Rb: e.tensor_tensor(
                    lnap, x_bf[:, 4 * g:4 * g + 4, :], Rb, ALU.mult), inc="dve")
                tok = p.step("dve", lambda e, lnap=lnap, Mb=Mb: e.tensor_tensor(
                    lnap, lnap, Mb, ALU.subtract), inc="dve")
                sink_done(g, tok)
        return

    sink_done_cell = [None]
    def sink_done(kc, tok):
        sink_done_cell[0](kc, tok)

    def emit_attn(idx, variant, x_in, x_out):
        li = 0 if variant == "a" else 1
        lsh = lnsh[li]
        part = partial[li]
        lmy = lnmy[li]
        pmy = part_my[li]
        pmd = pm_dram[li]
        ppd = pp_dram[li]
        wqk = wqk_a if variant == "a" else wqk_b
        wv = wv_a if variant == "a" else wv_b
        wo = wo_a if variant == "a" else wo_b

        with contextlib.ExitStack() as S:
            SA = S.enter_context(contextlib.ExitStack())
            q_fm = p.sb(SA, f"qfm{idx}", [128, MYH, T], BF16)
            k_fm = p.sb(SA, f"kfm{idx}", [128, MYH, T], BF16)
            v_ext = p.sb(SA, f"vex{idx}", [128, KC, MYH, 129], BF16)
            wring = p.sb(SA, f"wr{idx}", [128, 2, KC, 128], BF16)

            S1 = S.enter_context(contextlib.ExitStack())
            x_bf = p.sb(S1, f"xbf{idx}", [128, KC, TOWN], BF16)
            lnr = p.sb(S1, f"lnr{idx}", [128, 2, 4, TOWN], BF16)
            ln_dma = [None] * 4

            def sink(g):
                extra = [ln_dma[g - 2]] if g >= 2 else []
                return lnr[:, g % 2, :, :], extra

            def sdone(g, tok):
                p.wait("sync", tok)
                ln_dma[g] = p.step("sync", lambda e, g=g: e.dma_start(
                    out=lmy[g * 512:(g + 1) * 512, :].rearrange("(k p) t -> p k t", p=128),
                    in_=lnr[:, g % 2, :, :]), inc=f"lnw{g % 2}", n=16)

            sink_done_cell[0] = sdone
            emit_ln(S1, x_in, x_bf, sink)
            p.waits("sync", [ln_dma[3], ln_dma[2]])
            t_lnd2d = cdma("sync", lambda e, h, c: e.dma_start(
                out=lsh[:, h * TOWN:(h + 1) * TOWN], in_=lmy[:, :], cond=c),
                inc="lnd")
            t_bar = barrier([t_lnd2d])
            S1.close()

            S2 = S.enter_context(contextlib.ExitStack())
            ln_all = p.sb(S2, f"lna{idx}", [128, KC, T], BF16)

            t_vones = p.step("dve", lambda e: e.memset(
                v_ext[:, :, :, 128:129], 1.0), inc="dve")

            p.wait("sync", t_bar)
            lnld = [None] * KC
            for kc in range(KC):
                lnld[kc] = p.step("sync", lambda e, kc=kc: e.dma_start(
                    out=ln_all[:, kc, :], in_=lsh[kc * 128:(kc + 1) * 128, :]),
                    inc="lnl", n=16)

            with (contextlib.ExitStack() as S2a,
                  nc.psum_tensor(f"ps_qk{idx}", [128, 2, 512], F32) as ps_qk):
                wld = [None] * 16
                evac = {}
                for mt in range(16):
                    sl = mt % 2
                    p.waits("pool", [evac.get((mt - 2, 3))])
                    wld[mt] = p.step("pool", lambda e, mt=mt, sl=sl: e.dma_start(
                        out=wring[:, sl, :, :], in_=wqk[:, mt, :, :]),
                        inc=f"wq{sl}", n=16)
                    for ts in range(4):
                        psl = (mt * 4 + ts) % 2
                        p.waits("pe", [wld[mt], lnld[KC - 1], evac.get((mt, ts - 2) if ts >= 2 else (mt - 1, ts + 2))])
                        for kc in range(KC):
                            tok = p.step("pe", lambda e, mt=mt, ts=ts, kc=kc, sl=sl, psl=psl: e.matmul(
                                ps_qk[:, psl, :], wring[:, sl, kc, :],
                                ln_all[:, kc, ts * 512:(ts + 1) * 512],
                                start=(kc == 0), stop=(kc == KC - 1)),
                                inc="pe" if kc == KC - 1 else None)
                        dst = q_fm if mt < 8 else k_fm
                        hh = mt % 8
                        p.wait("act", tok)
                        evac[(mt, ts)] = p.step("act", lambda e, dst=dst, hh=hh, ts=ts, psl=psl: e.copy(
                            dst[:, hh, ts * 512:(ts + 1) * 512], ps_qk[:, psl, :]),
                            inc="act")
                p.fence()

            with (contextlib.ExitStack() as S2b,
                  nc.psum_tensor(f"ps_v{idx}", [128, 2, 512], F32) as ps_v):
                wvs = p.sb(S2b, f"wvs{idx}", [128, KC, 512], BF16)
                vevac = {}
                vmm = {}
                for ng in range(2):
                    p.waits("pool", [vmm.get((ng - 1, KC - 1))])
                    t_wv = p.step("pool", lambda e, ng=ng: e.dma_start(
                        out=wvs[:, :, :], in_=wv[:, :, ng * 512:(ng + 1) * 512]),
                        inc="wvv", n=16)
                    for tt in range(KC):
                        psl = tt % 2
                        p.waits("pe", [t_wv, vevac.get((ng, tt - 2) if tt >= 2 else (ng - 1, tt + 14))])
                        for kc in range(KC):
                            tok = p.step("pe", lambda e, tt=tt, kc=kc, psl=psl: e.matmul(
                                ps_v[:, psl, :], ln_all[:, kc, tt * 128:(tt + 1) * 128],
                                wvs[:, kc, :],
                                start=(kc == 0), stop=(kc == KC - 1)),
                                inc="pe" if kc == KC - 1 else None)
                        vmm[(ng, tt)] = tok
                        p.waits("dve", [tok, t_vones])
                        vevac[(ng, tt)] = p.step("dve", lambda e, tt=tt, ng=ng, psl=psl: e.tensor_copy(
                            v_ext[:, tt, 4 * ng:4 * ng + 4, 0:128],
                            ps_v[:, psl, :].rearrange("p (h m) -> p h m", m=128)),
                            inc="dve")
                p.fence()

            S2.close()
            with contextlib.ExitStack() as S3:
                if variant == "a":
                    qsw = p.sb(S3, f"qsw{idx}", [128, MYH, T], BF16)
                    for tgt in (q_fm, k_fm):
                        s1 = p.step("sync", lambda e, tgt=tgt: e.dma_start(
                            out=qsw[0:64, :, :], in_=tgt[64:128, :, :]), inc="rsw", n=16)
                        s2 = p.step("sync", lambda e, tgt=tgt: e.dma_start(
                            out=qsw[64:128, :, :], in_=tgt[0:64, :, :]), inc="rsw", n=16)
                        p.waits("dve", [s1, s2, t_cos, t_sin])
                        cosb = cos_sb[:, :].unsqueeze(1).broadcast_to([128, MYH, T])
                        sinb = sin_sb[:, :].unsqueeze(1).broadcast_to([128, MYH, T])
                        p.step("dve", lambda e, tgt=tgt, c=cosb: e.tensor_tensor(
                            tgt[:, :, :], tgt[:, :, :], c, ALU.mult), inc="dve")
                        p.step("dve", lambda e, s=sinb: e.tensor_tensor(
                            qsw[:, :, :], qsw[:, :, :], s, ALU.mult), inc="dve")
                        tok = p.step("dve", lambda e, tgt=tgt: e.tensor_tensor(
                            tgt[:, :, :], tgt[:, :, :], qsw[:, :, :], ALU.add), inc="dve")
                        p.wait("sync", tok)
                    p.fence()

            av_all = p.sb(SA, f"av{idx}", [128, MYH, T], BF16)
            p_sb = p.sb(SA, f"pp{idx}", [128, 3, 512], BF16)
            sinv = p.sb(SA, f"sinv{idx}", [128, 512], F32)
            brow = p.sb(SA, f"brow{idx}", [1, 2048], BF16)
            sbg = p.sb(SA, f"sbg{idx}", [128, 2, 512], BF16)

            with (nc.psum_tensor(f"ps_s{idx}", [128, 2, 512], F32) as ps_s,
                  nc.psum_tensor(f"ps_av{idx}", [128, 4, 512], F32) as ps_av,
                  nc.psum_tensor(f"ps_sum{idx}", [128, 512], F32) as ps_sum,
                  nc.psum_tensor(f"ps_sb{idx}", [128, 512], F32) as ps_sb):
                exp_tok = {}
                av_tok = {}
                avn_tok = {}
                last_sb = [None]
                last_sg = []
                last_sinv = [[], []]
                it = [0]
                pend = [None]

                def emit_epilogue(slab, hg, tlast):
                    gpar = (slab * 4 + hg) % 2
                    h0 = 2 * hg
                    rbase = 64 * gpar
                    p.waits("act", [tlast, last_sb[0]] + last_sinv[gpar])
                    ac0 = p.step("act", lambda e, rbase=rbase: e.copy(
                        sinv[rbase:rbase + 1, :], ps_sum[rbase:rbase + 1, :]), inc="act")
                    ac1 = p.step("act", lambda e, rbase=rbase: e.copy(
                        sinv[rbase + 32:rbase + 33, :], ps_sum[rbase + 32:rbase + 33, :]), inc="act")
                    p.waits("dve", [ac0, ac1])
                    dr0 = p.step("dve", lambda e, rbase=rbase: e.reciprocal(
                        sinv[rbase:rbase + 1, :], sinv[rbase:rbase + 1, :]), inc="dve")
                    dr1 = p.step("dve", lambda e, rbase=rbase: e.reciprocal(
                        sinv[rbase + 32:rbase + 33, :], sinv[rbase + 32:rbase + 33, :]), inc="dve")
                    p.waits("pool", [dr0, dr1] + last_sg)
                    sg0 = p.step("pool", lambda e, rbase=rbase, gpar=gpar: e.dma_start(
                        out=brow[0:1, gpar * 1024:gpar * 1024 + 512],
                        in_=sinv[rbase:rbase + 1, :]), inc="sgg", n=16)
                    sg = p.step("pool", lambda e, rbase=rbase, gpar=gpar: e.dma_start(
                        out=brow[0:1, gpar * 1024 + 512:gpar * 1024 + 1024],
                        in_=sinv[rbase + 32:rbase + 33, :]), inc="sgg", n=16)
                    last_sg.clear()
                    last_sinv[gpar] = [sg0, sg]
                    for hh in range(2):
                        head = h0 + hh
                        p.waits("pe", [sg0, sg, t_ones, last_sb[0]])
                        bc = p.step("pe", lambda e, hh=hh, gpar=gpar: e.matmul(
                            ps_sb[:, :], ones[0:1, :],
                            brow[0:1, gpar * 1024 + hh * 512:gpar * 1024 + (hh + 1) * 512],
                            start=True, stop=True), inc="pe")
                        last_sg.append(bc)
                        p.wait("act", bc)
                        sb1 = p.step("act", lambda e: e.copy(
                            sbg[:, 0, :], ps_sb[:, :]), inc="act")
                        last_sb[0] = sb1
                        if variant == "b":
                            p.waits("pe", [sb1, t_gwt])
                            gm = p.step("pe", lambda e, head=head, slab=slab: e.matmul(
                                ps_sb[:, :], gwt_sb[:, :],
                                q_fm[:, head, slab * 512:(slab + 1) * 512],
                                start=True, stop=True), inc="pe")
                            p.waits("act", [gm, t_gb])
                            sb2 = p.step("act", lambda e: e.activation(
                                sbg[:, 1, :], ps_sb[:, :], AF.Sigmoid,
                                bias=gb_sb[:, :], scale=1.0), inc="act")
                            last_sb[0] = sb2
                        p.waits("dve", [sb1, tlast])
                        dn = p.step("dve", lambda e, head=head, slab=slab, hh=hh, gpar=gpar: e.tensor_tensor(
                            av_all[:, head, slab * 512:(slab + 1) * 512],
                            ps_av[:, 2 * gpar + hh, :], sbg[:, 0, :], ALU.mult), inc="dve")
                        if variant == "b":
                            p.waits("dve", [sb2])
                            dn = p.step("dve", lambda e, head=head, slab=slab: e.tensor_tensor(
                                av_all[:, head, slab * 512:(slab + 1) * 512],
                                av_all[:, head, slab * 512:(slab + 1) * 512],
                                sbg[:, 1, :], ALU.mult), inc="dve")
                        avn_tok[(slab, hg)] = dn

                for slab in range(4):
                    nk = 4 * (slab + 1)
                    for hg in range(4):
                        g = slab * 4 + hg
                        gpar = g % 2
                        h0 = 2 * hg
                        rbase = 64 * gpar
                        gprev2 = ((g - 2) // 4, (g - 2) % 4) if g >= 2 else None
                        for kt in range(nk):
                            for hh in range(2):
                                i = it[0]
                                sl = i % 2
                                psl = 2 * gpar + hh
                                p.waits("pe", [exp_tok.get(i - 2)] +
                                        ([avn_tok.get(gprev2)] if (kt == 0 and hh == 0) else []))
                                tok = p.step("pe", lambda e, sl=sl, hh=hh, h0=h0, kt=kt, slab=slab: e.matmul(
                                    ps_s[:, sl, :],
                                    k_fm[:, h0 + hh, kt * 128:(kt + 1) * 128],
                                    q_fm[:, h0 + hh, slab * 512:(slab + 1) * 512],
                                    start=True, stop=True), inc="pe")
                                p.waits("act", [tok, av_tok.get(i - 3)])
                                exp_tok[i] = p.step("act", lambda e, sl=sl, pslot=i % 3: e.activation(
                                    p_sb[:, pslot, :], ps_s[:, sl, :], AF.Exp,
                                    bias=0.0, scale=SCALE), inc="act")
                                dep = exp_tok[i]
                                if kt >= nk - 4:
                                    jrel = 512 * slab - 128 * kt
                                    p.wait("pool", dep)
                                    dep = p.step("pool", lambda e, pslot=i % 3, jrel=jrel: e.affine_select(
                                        p_sb[:, pslot, :], p_sb[:, pslot, :],
                                        [[1, 512]], ALU.is_ge, 0.0,
                                        base=jrel, channel_multiplier=-1), inc="pool")
                                p.waits("pe", [dep])
                                p.step("pe", lambda e, pslot=i % 3, hh=hh, h0=h0, kt=kt, psl=psl, nk=nk: e.matmul(
                                    ps_av[:, psl, :], v_ext[:, kt, h0 + hh, 0:128],
                                    p_sb[:, pslot, :],
                                    start=(kt == 0), stop=(kt == nk - 1)))
                                srow = rbase + 32 * hh
                                tp = (0, srow) if srow else None
                                av_tok[i] = p.step("pe", lambda e, pslot=i % 3, hh=hh, h0=h0, kt=kt, srow=srow, tp=tp, nk=nk: e.matmul(
                                    ps_sum[srow:srow + 1, :],
                                    v_ext[:, kt, h0 + hh, 128:129],
                                    p_sb[:, pslot, :],
                                    start=(kt == 0), stop=(kt == nk - 1),
                                    tile_position=tp), inc="pe")
                                it[0] += 1
                        if pend[0] is not None:
                            emit_epilogue(*pend[0])
                        pend[0] = (slab, hg, av_tok[it[0] - 1])
                emit_epilogue(*pend[0])
                pend[0] = None
                p.fence()

            ostg = p.sb(SA, f"ostg{idx}", [128, 2, 512], BF16)
            wro = p.sb(SA, f"wro{idx}", [128, 2, 8, 128], BF16)
            with nc.psum_tensor(f"ps_o{idx}", [128, 2, 512], F32) as ps_o:
                wld = [None] * 16
                odma = {}
                for mt in range(16):
                    sl = mt % 2
                    p.waits("pool", [odma.get((mt - 2, 3))])
                    wld[mt] = p.step("pool", lambda e, mt=mt, sl=sl: e.dma_start(
                        out=wro[:, sl, 0:8, :], in_=wo[:, mt, :, :]),
                        inc=f"wo{sl}", n=16)
                    for slab in range(4):
                        psl = (mt * 4 + slab) % 2
                        p.waits("pe", [wld[mt], odma.get((mt, slab - 2) if slab >= 2 else (mt - 1, slab + 2))])
                        for hc in range(8):
                            tok = p.step("pe", lambda e, mt=mt, slab=slab, hc=hc, sl=sl, psl=psl: e.matmul(
                                ps_o[:, psl, :], wro[:, sl, hc, :],
                                av_all[:, hc, slab * 512:(slab + 1) * 512],
                                start=(hc == 0), stop=(hc == 7)),
                                inc="pe" if hc == 7 else None)
                        p.waits("act", [tok, odma.get((mt, slab - 2) if slab >= 2 else (mt - 1, slab + 2))])
                        ev = p.step("act", lambda e, psl=psl: e.copy(
                            ostg[:, psl, :], ps_o[:, psl, :]), inc="act")
                        p.wait("sync", ev)
                        odma[(mt, slab)] = p.step("sync", lambda e, mt=mt, slab=slab, psl=psl: e.dma_start(
                            out=pmy[mt * 128:(mt + 1) * 128, slab * 512:(slab + 1) * 512],
                            in_=ostg[:, psl, :]), inc=f"odm{psl}", n=16)
            p.waits("act", [p.now("odm0"), p.now("odm1")])
            t_pw = cdma("act", lambda e, h, c: e.dma_start(
                out=part[h * (KC * 128):(h + 1) * (KC * 128), :],
                in_=pmy[:, (1 - h) * TOWN:(1 - h) * TOWN + TOWN], cond=c),
                inc="tpw")
            p.waits("pool", [p.now("odm0"), p.now("odm1")])
            t_pmx = cdma("pool", lambda e, h, c: e.dma_start(
                out=pmd[:, :], in_=pmy[:, h * TOWN:h * TOWN + TOWN], cond=c),
                inc="pxx")
            t_bar2 = barrier([t_pw, t_pmx])
            p.wait("pool", t_bar2)
            t_ppx = cdma("pool", lambda e, h, c: e.dma_start(
                out=ppd[:, :], in_=part[(1 - h) * (KC * 128):(2 - h) * (KC * 128), :],
                cond=c), inc="pxx")

            SA.close()
            prr = p.sb(S, f"prr{idx}", [128, 2, 2, 2, TOWN], BF16)
            xstr = p.sb(S, f"xstr{idx}", [128, 2, 2, TOWN], F32)
            rtok = {}
            for g in range(8):
                sl = g % 2
                p.waits("sync", [rtok.get(g - 2)])
                l1 = p.step("sync", lambda e, g=g, sl=sl: e.dma_start(
                    out=xstr[:, sl, :, :], in_=x_in.group2(g)), inc=f"xld{sl}", n=16)
                p.waits("act", [rtok.get(g - 2), t_pmx if g == 0 else None])
                l2 = p.step("act", lambda e, g=g, sl=sl: e.dma_start(
                    out=prr[:, sl, 0, :, :],
                    in_=pmd[g * 256:(g + 1) * 256, :].rearrange("(k p) t -> p k t", p=128)),
                    inc=f"prr{sl}", n=16)
                p.waits("act", [t_ppx if g == 0 else None])
                l3 = p.step("act", lambda e, g=g, sl=sl: e.dma_start(
                    out=prr[:, sl, 1, :, :],
                    in_=ppd[g * 256:(g + 1) * 256, :].rearrange("(k p) t -> p k t", p=128)),
                    inc=f"prr{sl}", n=16)
                p.waits("dve", [l1, l2, l3])
                p.step("dve", lambda e, sl=sl: e.tensor_tensor(
                    prr[:, sl, 0, :, :], prr[:, sl, 0, :, :], prr[:, sl, 1, :, :],
                    ALU.add), inc="dve")
                dtok = p.step("dve", lambda e, sl=sl: e.tensor_tensor(
                    xstr[:, sl, :, :], xstr[:, sl, :, :], prr[:, sl, 0, :, :], ALU.add),
                    inc="dve")
                p.wait("sync", dtok)
                rtok[g] = p.step("sync", lambda e, g=g, sl=sl: e.dma_start(
                    out=x_out.group2(g), in_=xstr[:, sl, :, :]), inc=f"xso{sl}", n=16)
            p.fence()
            if tap and idx == 0:
                p.step("sync", lambda e: e.dma_start(out=dram["__tav"], in_=av_all[:, :, :]), inc="tap", n=16)
                p.step("sync", lambda e: e.dma_start(out=dram["__tpmy"], in_=pmy), inc="tap", n=16)
                p.step("sync", lambda e: e.dma_start(out=dram["__tpart"], in_=part), inc="tap", n=16)
                p.step("sync", lambda e: e.dma_start(out=dram["__tpmd"], in_=pmd), inc="tap", n=16)
                p.step("sync", lambda e: e.dma_start(out=dram["__tppd"], in_=ppd), inc="tap", n=16)
                p.fence()

    def emit_ffn(idx, x_in, x_out):
        with contextlib.ExitStack() as S:
            acc = p.sb(S, f"facc{idx}", [128, KC, TOWN], BF16)
            wgu = p.sb(S, f"fwgu{idx}", [128, 2, KC, 512], BF16)
            SF = S.enter_context(contextlib.ExitStack())
            ln = p.sb(SF, f"fln{idx}", [128, KC, TOWN], BF16)

            S1 = S.enter_context(contextlib.ExitStack())
            x_bf = p.sb(S1, f"fxbf{idx}", [128, KC, TOWN], BF16)

            def sink(g):
                return ln[:, 4 * g:4 * g + 4, :], []

            def sdone(g, tok):
                pass

            sink_done_cell[0] = sdone
            emit_ln(S1, x_in, x_bf, sink)
            t_ln = p.now("dve")
            S1.close()

            hng = p.sb(SF, f"fh{idx}", [128, 2, 8, 512], BF16)
            ang = p.sb(SF, f"fa{idx}", [128, 2, 8, 512], BF16)
            tau = p.sb(SF, f"ftau{idx}", [128, 2, 8, 128], BF16)
            tsc = p.sb(SF, f"ftsc{idx}", [128, 2, 2, 8, 128], BF16)
            sgr = p.sb(SF, f"fsg{idx}", [128, 8, 512], BF16)
            hfm = p.sb(SF, f"fhfm{idx}", [128, 16, TOWN], BF16)
            wdr = p.sb(SF, f"fwd{idx}", [128, 2, 16, 128], BF16)

            with (nc.psum_tensor(f"fps_g{idx}", [128, 2, 512], F32) as ps_g,
                  nc.psum_tensor(f"fps_u{idx}", [128, 2, 512], F32) as ps_u,
                  nc.psum_tensor(f"fps_t{idx}", [128, 2, 1024], BF16) as ps_t,
                  nc.psum_tensor(f"fps_d{idx}", [128, 2, 512], F32) as ps_d):
                wl_tok = {}
                gmm_tok = {}
                umm_tok = {}
                sa_tok = {}
                hm_tok = {}
                mask_tok = {}
                tr_evac = {}
                dn_evac = {}

                def emit_transposes(tng):
                    tngl = tng % 4
                    thsl = tng % 2
                    for tt in range(8):
                        tsl = tt % 2
                        p.waits("pe", [mask_tok[tng], t_idn,
                                       tr_evac.get((tng, tt - 2) if tt >= 2 else (tng - 1, tt + 6))])
                        for j in range(4):
                            tok = p.step("pe", lambda e, thsl=thsl, tt=tt, j=j, tsl=tsl: e.transpose(
                                ps_t[:, tsl, j * 128:(j + 1) * 128],
                                hng[:, thsl, tt, j * 128:(j + 1) * 128], idn[:, :]),
                                inc="pe" if j == 3 else None)
                        p.wait("dve", tok)
                        tr_evac[(tng, tt)] = p.step("dve", lambda e, tngl=tngl, tt=tt, tsl=tsl: e.tensor_copy(
                            hfm[:, 4 * tngl:4 * tngl + 4, tt * 128:(tt + 1) * 128],
                            ps_t[:, tsl, 0:512].rearrange("p (a b) -> p a b", b=128)),
                            inc="dve")

                for qtr in range(4):
                    for ngl in range(4):
                        ng = qtr * 4 + ngl
                        hsl = ng % 2
                        p.waits("pool", [gmm_tok.get((ng - 1, 7))])
                        wl_tok[(ng, 0)] = p.step("pool", lambda e, ng=ng: e.dma_start(
                            out=wgu[:, 0, :, :],
                            in_=wg_f[:, :, ng * 512:(ng + 1) * 512]), inc="wg0", n=16)
                        p.waits("pool", [umm_tok.get((ng - 1, 7))])
                        wl_tok[(ng, 1)] = p.step("pool", lambda e, ng=ng: e.dma_start(
                            out=wgu[:, 1, :, :],
                            in_=wu_f[:, :, ng * 512:(ng + 1) * 512]), inc="wg1", n=16)
                        for tt in range(8):
                            gsl = tt % 2
                            p.waits("pe", [wl_tok[(ng, 0)], t_ln,
                                           sa_tok.get((ng, tt - 2) if tt >= 2 else (ng - 1, tt + 6)),
                                           dn_evac.get(("last", qtr - 1)) if (ngl == 0 and tt == 0) else None])
                            for kc in range(KC):
                                tok = p.step("pe", lambda e, ng=ng, tt=tt, kc=kc, gsl=gsl: e.matmul(
                                    ps_g[:, gsl, :], ln[:, kc, tt * 128:(tt + 1) * 128],
                                    wgu[:, 0, kc, :],
                                    start=(kc == 0), stop=(kc == KC - 1)),
                                    inc="pe" if kc == KC - 1 else None)
                            gmm_tok[(ng, tt)] = tok
                            p.waits("act", [tok, hm_tok.get((ng - 1, tt))])
                            sa_tok[(ng, tt)] = p.step("act", lambda e, tt=tt, gsl=gsl: e.activation(
                                sgr[:, tt, :], ps_g[:, gsl, :], AF.Silu,
                                bias=0.0, scale=1.0), inc="act")
                        for tt in range(8):
                            gsl = tt % 2
                            p.waits("pe", [wl_tok[(ng, 1)],
                                           hm_tok.get((ng, tt - 2) if tt >= 2 else (ng - 1, tt + 6))])
                            for kc in range(KC):
                                tok = p.step("pe", lambda e, ng=ng, tt=tt, kc=kc, gsl=gsl: e.matmul(
                                    ps_u[:, gsl, :], ln[:, kc, tt * 128:(tt + 1) * 128],
                                    wgu[:, 1, kc, :],
                                    start=(kc == 0), stop=(kc == KC - 1)),
                                    inc="pe" if kc == KC - 1 else None)
                            umm_tok[(ng, tt)] = tok
                            p.waits("dve", [tok, sa_tok[(ng, tt)]])
                            hm_tok[(ng, tt)] = p.step("dve", lambda e, tt=tt, hsl=hsl, gsl=gsl: e.tensor_tensor(
                                hng[:, hsl, tt, :], sgr[:, tt, :], ps_u[:, gsl, :],
                                ALU.mult), inc="dve")
                        if ngl > 0:
                            emit_transposes(ng - 1)
                        heng = "dve"
                        esl = 0 if heng == "dve" else 1
                        hfl = hng[:, hsl, :, :]
                        afl = ang[:, hsl, :, :]
                        a4 = afl.rearrange("p t (g j) -> p t g j", j=4)
                        tq = tau[:, esl, :, :]
                        s1 = tsc[:, esl, 0, :, :]
                        s2 = tsc[:, esl, 1, :, :]
                        p.waits("act", [hm_tok[(ng, 7)], mask_tok.get(ng - 2)])
                        aa = p.step("act", lambda e, hfl=hfl, afl=afl: e.activation(
                            afl, hfl, AF.Abs, bias=0.0, scale=1.0), inc="act")
                        E = heng
                        p.waits(E, [aa])
                        p.step(E, lambda e, a4=a4, s1=s1: e.tensor_tensor(s1, a4[:, :, :, 0], a4[:, :, :, 1], ALU.max), inc=E)
                        p.step(E, lambda e, a4=a4, s2=s2: e.tensor_tensor(s2, a4[:, :, :, 2], a4[:, :, :, 3], ALU.max), inc=E)
                        p.step(E, lambda e, s1=s1, s2=s2, tq=tq: e.tensor_tensor(tq, s1, s2, ALU.min), inc=E)
                        p.step(E, lambda e, a4=a4, s1=s1: e.tensor_tensor(s1, a4[:, :, :, 0], a4[:, :, :, 1], ALU.min), inc=E)
                        p.step(E, lambda e, a4=a4, s2=s2: e.tensor_tensor(s2, a4[:, :, :, 2], a4[:, :, :, 3], ALU.min), inc=E)
                        p.step(E, lambda e, s1=s1, s2=s2: e.tensor_tensor(s1, s1, s2, ALU.max), inc=E)
                        p.step(E, lambda e, s1=s1, tq=tq: e.tensor_tensor(tq, tq, s1, ALU.max), inc=E)
                        taub = tq.unsqueeze(3).broadcast_to([128, 8, 128, 4])
                        p.step(E, lambda e, a4=a4, taub=taub: e.tensor_tensor(a4, a4, taub, ALU.is_ge), inc=E)
                        mask_tok[ng] = p.step(E, lambda e, hfl=hfl, afl=afl: e.tensor_tensor(
                            hfl, hfl, afl, ALU.mult), inc=E)
                    emit_transposes(qtr * 4 + 3)
                    wd_tok = {}
                    for mt in range(16):
                        wsl = mt % 2
                        wprev = (mt - 2, 1) if mt >= 2 else ((mt + 14, 1) if qtr else None)
                        p.waits("pool", [dn_evac.get(wprev) if wprev else None])
                        wd_tok[mt] = p.step("pool", lambda e, qtr=qtr, mt=mt, wsl=wsl: e.dma_start(
                            out=wdr[:, wsl, :, :], in_=wd_f[:, mt, qtr * 16:(qtr + 1) * 16, :]),
                            inc=f"wd{wsl}", n=16)
                        for slab in range(2):
                            psl = (mt * 2 + slab) % 2
                            dprev = (mt - 1, slab) if mt else ((15, slab) if qtr else None)
                            p.waits("pe", [wd_tok[mt], tr_evac[(qtr * 4 + 3, 7)],
                                           dn_evac.get(dprev) if dprev else None])
                            for kq in range(16):
                                tok = p.step("pe", lambda e, mt=mt, slab=slab, kq=kq, wsl=wsl, psl=psl: e.matmul(
                                    ps_d[:, psl, :], wdr[:, wsl, kq, :],
                                    hfm[:, kq, slab * 512:(slab + 1) * 512],
                                    start=(kq == 0), stop=(kq == 15)),
                                    inc="pe" if kq == 15 else None)
                            p.waits("dve", [tok])
                            if qtr == 0:
                                dn_evac[(mt, slab)] = p.step("dve", lambda e, mt=mt, slab=slab, psl=psl: e.tensor_copy(
                                    acc[:, mt, slab * 512:(slab + 1) * 512], ps_d[:, psl, :]),
                                    inc="dve")
                            else:
                                dn_evac[(mt, slab)] = p.step("dve", lambda e, mt=mt, slab=slab, psl=psl: e.tensor_tensor(
                                    acc[:, mt, slab * 512:(slab + 1) * 512],
                                    acc[:, mt, slab * 512:(slab + 1) * 512],
                                    ps_d[:, psl, :], ALU.add), inc="dve")
                    dn_evac[("last", qtr)] = dn_evac[(15, 1)]
                p.fence()

            SF.close()
            rtok = {}
            xstr = p.sb(S, f"fxstr{idx}", [128, 2, 2, TOWN], F32)
            for g in range(8):
                sl = g % 2
                p.waits("sync", [rtok.get(g - 2)])
                l1 = p.step("sync", lambda e, g=g, sl=sl: e.dma_start(
                    out=xstr[:, sl, :, :], in_=x_in.group2(g)), inc=f"xld{sl}", n=16)
                p.waits("dve", [l1])
                dtok = p.step("dve", lambda e, g=g, sl=sl: e.tensor_tensor(
                    xstr[:, sl, :, :], xstr[:, sl, :, :], acc[:, 2 * g:2 * g + 2, :],
                    ALU.add), inc="dve")
                p.wait("sync", dtok)
                rtok[g] = p.step("sync", lambda e, g=g, sl=sl: e.dma_start(
                    out=x_out.group2(g), in_=xstr[:, sl, :, :]), inc=f"xso{sl}", n=16)
            p.fence()

    subs = [("attn", "a"), ("ffn", None), ("attn", "b"), ("ffn", None)]
    for i in range(n_sub):
        x_in, x_out = chain[i]
        kind, var = subs[i]
        if kind == "attn":
            emit_attn(i, var, _ChunkWriter(x_in), _ChunkWriter(x_out))
        else:
            emit_ffn(i, _ChunkWriter(x_in), _ChunkWriter(x_out))

    p.fence()

    for _s in p.sems.values():
        nc.gpsimd.sem_clear(_s)
    nc.all_engine_barrier()

    with nc.Block() as block:
        @block.tensor
        def _(t):
            for fn in p.ev["pe"]:
                fn(t)

        @block.scalar
        def _(s):
            for fn in p.ev["act"]:
                fn(s)

        @block.vector
        def _(v):
            for fn in p.ev["dve"]:
                fn(v)

        @block.gpsimd
        def _(g):
            for fn in p.ev["pool"]:
                fn(g)

        @block.sync
        def _(sy):
            for fn in p.ev["sync"]:
                fn(sy)

    p.st.close()
    return nc


class _ChunkWriter:
    def __init__(self, x):
        self.x = x

    def __call__(self, kc):
        return self.x[kc, :, :]

    def __getitem__(self, kc):
        return self.x[kc, :, :]

    def group4(self, g):
        return self.x[4 * g:4 * g + 4, :, :].rearrange("k p t -> p k t")

    def group2(self, g):
        return self.x[2 * g:2 * g + 2, :, :].rearrange("k p t -> p k t")



_cached = {}


def _get_program(n_sub=N_SUB):
    if n_sub not in _cached:
        _cached[n_sub] = build_program(n_sub)
    return _cached[n_sub]


def make_in_maps(inputs):
    w_qkv = inputs["L0_attn_a_w_qkv"]
    w_o = inputs["L0_attn_a_w_o"]
    q_w = inputs["L0_attn_b_q_w"]
    k_w = inputs["L0_attn_b_k_w"]
    v_w = inputs["L0_attn_b_v_w"]
    o_w = inputs["L0_attn_b_o_w"]
    gate_w = inputs["L0_attn_b_gate_w"]
    gate_b = inputs["L0_attn_b_gate_b"]
    w_gate = inputs["L0_ffn_w_gate"]
    w_up = inputs["L0_ffn_w_up"]
    w_down = inputs["L0_ffn_w_down"]
    x = inputs["L0_attn_a_x"]

    cosf, sinS = _rope_tables()
    ident = np.eye(128, dtype=np.float32)
    shared = {
        "wgt": np.ascontiguousarray(gate_w.T),
        "gbi": np.ascontiguousarray(gate_b.reshape(128, 1)),
        "wg_f": _b_layout(w_gate),
        "wu_f": _b_layout(w_up),
        "wd_f": _a_layout(w_down),
        "cosf": cosf,
        "sinSf": sinS,
        "identf": ident,
    }
    qT, kT, vT, oT = q_w.T, k_w.T, v_w.T, o_w.T
    maps = []
    for c in range(8):
        b, h = c // 2, c % 2
        cs = slice(h * 1024, (h + 1) * 1024)
        m = dict(shared)
        m["wqk_a"] = _a_layout(np.concatenate(
            [w_qkv[:, h * 1024:(h + 1) * 1024],
             w_qkv[:, 2048 + h * 1024:2048 + (h + 1) * 1024]], axis=1))
        m["wv_a"] = _b_layout(w_qkv[:, 4096 + h * 1024:4096 + (h + 1) * 1024])
        m["wo_a"] = _a_layout(w_o[h * 1024:(h + 1) * 1024, :])
        m["wqk_b"] = _a_layout(np.concatenate([qT[:, cs], kT[:, cs]], axis=1))
        m["wv_b"] = _b_layout(vT[:, cs])
        m["wo_b"] = _a_layout(oT[h * 1024:(h + 1) * 1024, :])
        m["x0"] = _fm(x[b, h * 1024:(h + 1) * 1024, :])
        maps.append(m)
    return maps


def kernel(**inputs):
    nc = _get_program(N_SUB)
    maps = make_in_maps(inputs)
    res = run_bass_kernel_spmd(nc, maps, core_ids=list(range(8)))
    out = np.empty((B, T, D), dtype=np.float32)
    for c in range(8):
        b, h = c // 2, c % 2
        yc = np.asarray(res.results[c]["y"])
        out[b, h * 1024:(h + 1) * 1024, :] = yc.transpose(2, 0, 1).reshape(TOWN, D)
    return out


if __name__ == "__main__":
    nc = build_program(1)
    print("build ok")
